# version 12
# baseline (speedup 1.0000x reference)
import numpy as np
from contextlib import ExitStack

import concourse.bass as bass
import concourse.mybir as mybir
from concourse.bass_utils import run_bass_kernel_spmd
from concourse.alu_op_type import AluOpType as OP

F32 = mybir.dt.float32
BF16 = mybir.dt.bfloat16
I32 = mybir.dt.int32
AF = mybir.ActivationFunctionType
AX = mybir.AxisListType

B, S, D = 4, 4096, 1024
NH, HS = 16, 64
H = 4096
QO = 2048
NKT = S // 128
NTT = QO // 128
EPS_LN = 1e-5
EPS_Q = 1e-5
VSTRIDE = HS + 1
VTILE = NH * VSTRIDE

NEGM = 0
BQ = 16
BK = 24
S1C = 32
WS = 48
ST = 56
MQ2 = 64
C_T1, C_NT1, C_T2, C_NT2 = WS + 2, WS + 3, WS + 4, WS + 5
C_WQ1, C_WQ2 = WS + 6, WS + 7


QCH_OPS = 12


def emit_quant_chunk(vector, cDV, wf, wri, shm_ap, src_ap, dst_ap):
    from concourse.alu_op_type import AluOpType as OP
    qa = wf[:, 4096:4608]
    qb = wf[:, 4608:5120]
    qc = wf[:, 5120:5632]
    n = 0
    inst = vector.tensor_scalar(qa, src_ap, shm_ap, 256.0, OP.mult, OP.add)
    n += 1; cDV.inc(inst)
    inst = vector.tensor_copy(wri[:, :], qa)
    n += 1; cDV.inc(inst)
    inst = vector.tensor_copy(qb, wri[:, :])
    n += 1; cDV.inc(inst)
    inst = vector.tensor_sub(qa, qa, qb)
    n += 1; cDV.inc(inst)
    inst = vector.tensor_scalar(qc, qa, -0.5, None, OP.is_lt)
    n += 1; cDV.inc(inst)
    inst = vector.scalar_tensor_tensor(qc, qa, -1.5, qc, OP.is_lt, OP.add)
    n += 1; cDV.inc(inst)
    inst = vector.scalar_tensor_tensor(qc, qa, -2.5, qc, OP.is_lt, OP.add)
    n += 1; cDV.inc(inst)
    inst = vector.scalar_tensor_tensor(qb, qc, -1.0, qb, OP.mult, OP.add)
    n += 1; cDV.inc(inst)
    inst = vector.tensor_scalar(qc, qa, 0.5, None, OP.is_ge)
    n += 1; cDV.inc(inst)
    inst = vector.scalar_tensor_tensor(qc, qa, 1.5, qc, OP.is_ge, OP.add)
    n += 1; cDV.inc(inst)
    inst = vector.scalar_tensor_tensor(qc, qa, 2.5, qc, OP.is_ge, OP.add)
    n += 1; cDV.inc(inst)
    inst = vector.scalar_tensor_tensor(dst_ap, qb, -256.0, qc, OP.add, OP.add)
    n += 1; cDV.inc(inst)
    assert n == QCH_OPS
    return n


class Ctr:
    def __init__(self, sem, step=1):
        self.sem, self.step, self.n = sem, step, 0

    def inc(self, inst):
        inst.then_inc(self.sem, self.step)
        self.n += self.step
        return self.n


def build_nc(debug=False):
    nc = bass.Bass("TRN2", target_bir_lowering=False, debug=False,
                   detect_race_conditions=False)

    xT = nc.dram_tensor("xT", [D, S], F32, kind="ExternalInput").ap()
    xTq = nc.dram_tensor("xTq", [D, QO], F32, kind="ExternalInput").ap()
    x_own = nc.dram_tensor("x_own", [QO, D], F32, kind="ExternalInput").ap()
    wqkv = nc.dram_tensor("wqkv", [D, 3 * D], F32, kind="ExternalInput").ap()
    bqkv = nc.dram_tensor("bqkv", [3 * D], F32, kind="ExternalInput").ap()
    wo = nc.dram_tensor("wo", [D, D], F32, kind="ExternalInput").ap()
    bo = nc.dram_tensor("bo", [D], F32, kind="ExternalInput").ap()
    w1t = nc.dram_tensor("w1t", [D, H], F32, kind="ExternalInput").ap()
    b1 = nc.dram_tensor("b1", [H], F32, kind="ExternalInput").ap()
    w2t = nc.dram_tensor("w2t", [H, D], F32, kind="ExternalInput").ap()
    b2 = nc.dram_tensor("b2", [D], F32, kind="ExternalInput").ap()
    ident_in = nc.dram_tensor("ident_in", [128, 128], BF16, kind="ExternalInput").ap()
    y = nc.dram_tensor("y", [QO, D], F32, kind="ExternalOutput").ap()
    if debug:
        dbg_wsm = nc.dram_tensor("dbg_wsm", [128, 192], F32, kind="ExternalOutput").ap()
        dbg_x1 = nc.dram_tensor("dbg_x1", [QO, D], F32, kind="ExternalOutput").ap()
        dbg_wf = nc.dram_tensor("dbg_wf", [128, 5632], F32, kind="ExternalOutput").ap()
        dbg_wo = nc.dram_tensor("dbg_wo", [8, 128, D], BF16, kind="ExternalOutput").ap()
        dbg_q = nc.dram_tensor("dbg_q", [128, 8 * QO], BF16, kind="ExternalOutput").ap()
        dbg_k = nc.dram_tensor("dbg_k", [128, 8 * 4096], BF16, kind="ExternalOutput").ap()
        dbg_v = nc.dram_tensor("dbg_v", [128, NKT * VTILE], BF16, kind="ExternalOutput").ap()
        dbg_av = nc.dram_tensor("dbg_av", [128, 8 * QO], BF16, kind="ExternalOutput").ap()

    dscr = nc.dram_tensor("dscr", [1, 1024], F32, kind="Internal").ap()
    mscr = nc.dram_tensor("mscr", [16], F32, kind="Internal").ap()
    sscr = nc.dram_tensor("sscr", [8], F32, kind="Internal").ap()
    rscr = nc.dram_tensor("rscr", [128], F32, kind="Internal").ap()
    wo_bf = nc.dram_tensor("wo_bf", [8, 128, D], BF16, kind="Internal").ap()
    x1scr = nc.dram_tensor("x1scr", [QO, D], F32, kind="Internal").ap()

    with ExitStack() as ctx:
        e = ctx.enter_context
        karena = e(nc.sbuf_tensor("karena", [128, 8 * 4096], BF16))
        varena = e(nc.sbuf_tensor("varena", [128, NKT * VTILE], BF16))
        qarena = e(nc.sbuf_tensor("qarena", [128, 8 * QO], BF16))
        wkA = e(nc.sbuf_tensor("wkA", [128, 4096], BF16))
        wkB = e(nc.sbuf_tensor("wkB", [128, 5632], BF16))
        wf = e(nc.sbuf_tensor("wf", [128, 5632], F32))
        wsm = e(nc.sbuf_tensor("wsm", [128, 192], F32))
        wri = e(nc.sbuf_tensor("wri", [128, 512], I32))
        ident = e(nc.sbuf_tensor("ident", [128, 128], BF16))
        wsb = e(nc.sbuf_tensor("wsb", [128, 32], BF16))
        b2row = e(nc.sbuf_tensor("b2row", [1, 1024], BF16))
        s2invrow = e(nc.sbuf_tensor("s2invrow", [1, 128], BF16))

        psBIG = e(nc.psum_tensor("psBIG", [128, 2048], F32))
        psC = e(nc.psum_tensor("psC", [128, 1024], F32))
        psTa = e(nc.psum_tensor("psTa", [128, 512], BF16))
        psTb = e(nc.psum_tensor("psTb", [128, 512], BF16))

        qLD = Ctr(e(nc.semaphore("qLD")), 16)
        qLG = Ctr(e(nc.semaphore("qLG")), 16)
        qOUT = Ctr(e(nc.semaphore("qOUT")), 16)
        cPE = Ctr(e(nc.semaphore("cPE")))
        cDV = Ctr(e(nc.semaphore("cDV")))
        cAC = Ctr(e(nc.semaphore("cAC")))

        block = e(nc.Block())

        @block.sync
        def _(sync):
            qLD.inc(sync.dma_start(out=ident[:], in_=ident_in[:]))
            for fc in range(8):
                qLD.inc(sync.dma_start(
                    out=wsm[:, BQ + fc:BQ + fc + 1],
                    in_=bqkv[fc * 128:(fc + 1) * 128].rearrange("(p a) -> p a", a=1)))
                qLD.inc(sync.dma_start(
                    out=wsm[:, BK + fc:BK + fc + 1],
                    in_=bqkv[1024 + fc * 128:1024 + (fc + 1) * 128].rearrange("(p a) -> p a", a=1)))
            inst = sync.dma_start(out=wf[:, 2048:3072],
                                  in_=bqkv[2048:3072].rearrange("(a n) -> a n", a=1)
                                  .partition_broadcast(128))
            qLD.inc(inst)
        LD_CONST = qLD.n

        @block.vector
        def _(vector):
            for inst in [vector.memset(wsm[:, NEGM:NEGM + 1], -16.0),
                         vector.memset(wsb[:, :], 0.0),
                         vector.memset(wsb[0:64, 0:1], 1.0),
                         vector.memset(wsb[64:128, 1:2], 1.0),
                         vector.memset(wsb[:, 2:3], 1.0),
                         vector.memset(wsm[:, 96:97], 1.0),
                         vector.memset(varena[:, :], 1.0)]:
                cDV.inc(inst)
                vector.wait_ge(cDV.sem, cDV.n)
        DV_CONST = cDV.n

        w_pieces = [(w1t, c, p, 0) for c in range(8) for p in range(4)] + \
                   [(w2t, c, 0, 1) for c in range(32)]
        NW = len(w_pieces)
        lg0 = qLG.n
        dv0 = cDV.n
        pe0 = cPE.n

        @block.gpsimd
        def _(g):
            for i in range(NW):
                src, c, p, _w = w_pieces[i]
                if i >= 1:
                    g.wait_ge(cDV.sem, dv0 + i * 2)
                inst = g.dma_start(
                    out=wf[:, 3072 + (i % 2) * 1024:3072 + (i % 2) * 1024 + 1024],
                    in_=src[c * 128:(c + 1) * 128, p * 1024:(p + 1) * 1024])
                qLG.inc(inst)

        @block.vector
        def _(vector):
            for i in range(NW):
                src, c, p, w = w_pieces[i]
                vector.wait_ge(qLG.sem, lg0 + (i + 1) * 16)
                inst = vector.reduce_sum(
                    wsm[:, ST + 4:ST + 5],
                    wf[:, 3072 + (i % 2) * 1024:3072 + (i % 2) * 1024 + 1024],
                    AX.X, apply_absolute_value=True)
                cDV.inc(inst)
                vector.wait_ge(cDV.sem, cDV.n)
                acc = wsm[:, ST + 5:ST + 6] if w == 0 else wsm[:, ST + 6:ST + 7]
                if i == 0 or i == 32:
                    inst = vector.tensor_copy(acc, wsm[:, ST + 4:ST + 5])
                else:
                    inst = vector.tensor_add(acc, acc, wsm[:, ST + 4:ST + 5])
                cDV.inc(inst)
                vector.wait_ge(cDV.sem, cDV.n)
        assert cDV.n == dv0 + NW * 2
        DV_W0 = cDV.n

        @block.tensor
        def _(t):
            t.wait_ge(cDV.sem, DV_W0)
            t.matmul(psC[0:1, 0:1], wsm[:, 96:97], wsm[:, ST + 5:ST + 6],
                     start=True, stop=True)
            inst = t.matmul(psC[0:1, 512:513], wsm[:, 96:97], wsm[:, ST + 6:ST + 7],
                            start=True, stop=True)
            cPE.inc(inst)
        PE_W0 = cPE.n
        assert PE_W0 == pe0 + 1

        @block.vector
        def _(vector):
            vector.wait_ge(cPE.sem, PE_W0)
            for inst in [
                vector.tensor_copy(wsm[0:1, ST:ST + 1], psC[0:1, 0:1]),
                vector.tensor_copy(wsm[0:1, ST + 1:ST + 2], psC[0:1, 512:513]),
                vector.tensor_scalar(wsm[0:1, ST:ST + 1], wsm[0:1, ST:ST + 1],
                                     1.0 / (D * H), EPS_Q, OP.mult, OP.max),
                vector.tensor_scalar(wsm[0:1, ST + 1:ST + 2], wsm[0:1, ST + 1:ST + 2],
                                     1.0 / (D * H), EPS_Q, OP.mult, OP.max),
                vector.tensor_scalar_mul(wsm[0:1, ST + 2:ST + 3], wsm[0:1, ST:ST + 1], 0.5),
                vector.tensor_scalar_mul(wsm[0:1, ST + 3:ST + 4], wsm[0:1, ST:ST + 1], -0.5),
                vector.tensor_scalar_mul(wsm[0:1, ST + 4:ST + 5], wsm[0:1, ST + 1:ST + 2], 0.5),
                vector.tensor_scalar_mul(wsm[0:1, ST + 5:ST + 6], wsm[0:1, ST + 1:ST + 2], -0.5),
                vector.tensor_scalar_mul(wsm[0:1, ST + 6:ST + 7], wsm[0:1, ST:ST + 1], 1.0 / 127.0),
                vector.tensor_scalar_mul(wsm[0:1, ST + 7:ST + 8], wsm[0:1, ST + 1:ST + 2],
                                         1.0 / 127.0),
            ]:
                cDV.inc(inst)
                vector.wait_ge(cDV.sem, cDV.n)
        DV_WS = cDV.n

        @block.gpsimd
        def _(g):
            g.wait_ge(cDV.sem, DV_WS)
            inst = g.dma_start(out=sscr.rearrange("(a n) -> a n", a=1), in_=wsm[0:1, ST:ST + 8])
            qLG.inc(inst)
            g.wait_ge(qLG.sem, qLG.n)
            inst = g.dma_start(out=wsm[:, WS:WS + 8],
                               in_=sscr.rearrange("(a n) -> a n", a=1).partition_broadcast(128))
            qLG.inc(inst)
        LG_WS = qLG.n

        def emit_proj(sub, xsrc, ntb, wcol0):
            ld0 = qLD.n
            dv0 = cDV.n
            pe0 = cPE.n

            def x_dma(tb, dc):
                return ld0 + (tb * 24 + dc + 1) * 16

            def w_dma(tb, fh, dc):
                return ld0 + (tb * 24 + 8 + fh * 8 + dc + 1) * 16

            def x_cast(tb, dc):
                return dv0 + tb * 32 + dc + 1

            def w_cast(tb, fh, dc):
                return dv0 + tb * 32 + 8 + fh * 12 + dc + 1

            def evac(tb, fh, g):
                return dv0 + tb * 32 + 8 + fh * 12 + 8 + g + 1

            def pe_grp(tb, fh, g):
                return pe0 + (tb * 2 + fh) * 4 + g + 1

            @block.sync
            def _(sync):
                for tb in range(ntb):
                    for dc in range(8):
                        i = tb * 8 + dc
                        if i >= 1:
                            pb_, pdc = divmod(i - 1, 8)
                            sync.wait_ge(cDV.sem, x_cast(pb_, pdc))
                        inst = sync.dma_start(
                            out=wf[:, (i % 2) * 512:(i % 2) * 512 + 512],
                            in_=xsrc[dc * 128:(dc + 1) * 128, tb * 512:tb * 512 + 512])
                        qLD.inc(inst)
                    for fh in range(2):
                        for dc in range(8):
                            j = (tb * 2 + fh) * 8 + dc
                            if j >= 1:
                                pj = j - 1
                                ptb, pr = divmod(pj, 16)
                                pfh, pdc = divmod(pr, 8)
                                sync.wait_ge(cDV.sem, w_cast(ptb, pfh, pdc))
                            inst = sync.dma_start(
                                out=wf[:, 1024 + (j % 2) * 512:1024 + (j % 2) * 512 + 512],
                                in_=wqkv[dc * 128:(dc + 1) * 128,
                                         wcol0 + fh * 512:wcol0 + fh * 512 + 512])
                            qLD.inc(inst)

            @block.vector
            def _(vector):
                for tb in range(ntb):
                    for dc in range(8):
                        vector.wait_ge(qLD.sem, x_dma(tb, dc))
                        if tb == 0 and dc == 0:
                            vector.wait_ge(cPE.sem, pe0)
                        if tb >= 1 and dc == 0:
                            vector.wait_ge(cPE.sem, pe_grp(tb - 1, 1, 3))
                        inst = vector.tensor_copy(
                            wkB[:, dc * 512:dc * 512 + 512],
                            wf[:, ((tb * 8 + dc) % 2) * 512:((tb * 8 + dc) % 2) * 512 + 512])
                        cDV.inc(inst)
                        assert cDV.n == x_cast(tb, dc)
                    for fh in range(2):
                        for dc in range(8):
                            vector.wait_ge(qLD.sem, w_dma(tb, fh, dc))
                            if dc == 0:
                                if tb * 2 + fh >= 1:
                                    pfh = tb * 2 + fh - 1
                                    vector.wait_ge(cPE.sem, pe_grp(pfh // 2, pfh % 2, 3))
                                else:
                                    vector.wait_ge(cPE.sem, pe0)
                            j = (tb * 2 + fh) * 8 + dc
                            inst = vector.tensor_copy(
                                wkA[:, dc * 512:dc * 512 + 512],
                                wf[:, 1024 + (j % 2) * 512:1024 + (j % 2) * 512 + 512])
                            cDV.inc(inst)
                            assert cDV.n == w_cast(tb, fh, dc)
                        for g in range(4):
                            vector.wait_ge(cPE.sem, pe_grp(tb, fh, g))
                            q = (tb * 2 + fh) * 4 + g
                            ps = psBIG[:, (q % 4) * 512:(q % 4) * 512 + 512]
                            if sub == "K":
                                fc = fh * 4 + g
                                inst = vector.tensor_scalar(
                                    karena[:, fc * 4096 + tb * 512:fc * 4096 + tb * 512 + 512],
                                    ps, wsm[:, BK + fc:BK + fc + 1], None, OP.add)
                            elif sub == "Q":
                                fc = fh * 4 + g
                                inst = vector.tensor_scalar(
                                    qarena[:, fc * 2048 + tb * 512:fc * 2048 + tb * 512 + 512],
                                    ps, wsm[:, BQ + fc:BQ + fc + 1], None, OP.add)
                            else:
                                kt = tb * 4 + g
                                h0 = fh * 8
                                dst = varena[:, kt * VTILE:(kt + 1) * VTILE] \
                                    .rearrange("p (h v) -> p h v", v=VSTRIDE)[:, h0:h0 + 8, 0:64]
                                inst = vector.tensor_tensor(
                                    dst,
                                    ps.rearrange("p (h v) -> p h v", v=64),
                                    wf[:, 2048 + fh * 512:2048 + fh * 512 + 512]
                                    .rearrange("p (h v) -> p h v", v=64),
                                    OP.add)
                            cDV.inc(inst)
                            assert cDV.n == evac(tb, fh, g)

            @block.tensor
            def _(t):
                for tb in range(ntb):
                    for fh in range(2):
                        for g in range(4):
                            q = (tb * 2 + fh) * 4 + g
                            ps = psBIG[:, (q % 4) * 512:(q % 4) * 512 + 512]
                            if q >= 4:
                                pq = q - 4
                                t.wait_ge(cDV.sem, evac(pq // 8, (pq // 4) % 2, pq % 4))
                            for dc in range(8):
                                t.wait_ge(cDV.sem, max(x_cast(tb, dc), w_cast(tb, fh, dc)))
                                if sub in ("K", "Q"):
                                    inst = t.matmul(
                                        ps,
                                        wkA[:, dc * 512 + g * 128:dc * 512 + g * 128 + 128],
                                        wkB[:, dc * 512:dc * 512 + 512],
                                        start=(dc == 0), stop=(dc == 7))
                                else:
                                    inst = t.matmul(
                                        ps,
                                        wkB[:, dc * 512 + g * 128:dc * 512 + g * 128 + 128],
                                        wkA[:, dc * 512:dc * 512 + 512],
                                        start=(dc == 0), stop=(dc == 7))
                                if dc == 7:
                                    cPE.inc(inst)
                            assert cPE.n == pe_grp(tb, fh, g)

            qLD.n = ld0 + ntb * 24 * 16
            assert cDV.n == dv0 + ntb * 32
            assert cPE.n == pe0 + ntb * 8

        emit_proj("K", xT, 8, 1024)
        emit_proj("V", xT, 8, 2048)
        emit_proj("Q", xTq, 4, 0)
        PE_PROJ = cPE.n
        DV_PROJ = cDV.n

        pe0 = cPE.n
        ac0 = cAC.n
        dv0 = cDV.n
        lg0 = qLG.n
        steps = [(h, qc) for h in range(NH) for qc in range(2)]
        NSTEP = len(steps)

        def st_inc(gkt):
            return pe0 + gkt * 2 + 1

        def pv_inc(gkt):
            return pe0 + gkt * 2 + 2

        def exp_inc(gkt):
            return ac0 + gkt + 1

        def r_inc(m):
            return dv0 + m * 2 + 1

        def tt_inc(m):
            return dv0 + m * 2 + 2

        def lg_back(m):
            return lg0 + (m * 3 + 2) * 16

        def lg_move(m):
            return lg0 + (m * 3 + 3) * 16

        @block.tensor
        def _(t):
            t.wait_ge(cDV.sem, DV_WS)
            t.wait_ge(cDV.sem, DV_PROJ)
            for m, (h, qc) in enumerate(steps):
                ch, rh = h // 2, (h % 2) * 64
                for kt in range(NKT):
                    gkt = m * NKT + kt
                    stb = (gkt % 2) * 1024
                    if gkt >= 2:
                        t.wait_ge(cAC.sem, exp_inc(gkt - 2))
                    lhsK = karena[rh:rh + 64, ch * 4096 + kt * 128:ch * 4096 + kt * 128 + 128]
                    for half in range(2):
                        inst = t.matmul(psBIG[:, stb + half * 512:stb + half * 512 + 512],
                                        lhsK,
                                        qarena[rh:rh + 64,
                                               ch * 2048 + qc * 1024 + half * 512:
                                               ch * 2048 + qc * 1024 + half * 512 + 512],
                                        start=True, stop=True)
                    cPE.inc(inst)
                    assert cPE.n == st_inc(gkt)
                    t.wait_ge(cAC.sem, exp_inc(gkt))
                    if kt == 0 and m >= 1:
                        t.wait_ge(cDV.sem, tt_inc(m - 1))
                    lhsV = varena[:, kt * VTILE + h * VSTRIDE:kt * VTILE + h * VSTRIDE + VSTRIDE]
                    pt = wkB[:, (gkt % 2) * 1024:(gkt % 2) * 1024 + 1024]
                    for half in range(2):
                        inst = t.matmul(psC[0:65, half * 512:half * 512 + 512],
                                        lhsV, pt[:, half * 512:half * 512 + 512],
                                        start=(kt == 0), stop=(kt == NKT - 1))
                    cPE.inc(inst)
                    assert cPE.n == pv_inc(gkt)

        @block.scalar
        def _(scalar):
            for m, (h, qc) in enumerate(steps):
                for kt in range(NKT):
                    gkt = m * NKT + kt
                    stb = (gkt % 2) * 1024
                    scalar.wait_ge(cPE.sem, st_inc(gkt))
                    if gkt >= 2:
                        scalar.wait_ge(cPE.sem, pv_inc(gkt - 2))
                    inst = scalar.activation(wkB[:, (gkt % 2) * 1024:(gkt % 2) * 1024 + 1024],
                                             psBIG[:, stb:stb + 1024], AF.Exp,
                                             bias=wsm[:, NEGM:NEGM + 1], scale=0.125)
                    cAC.inc(inst)
                    assert cAC.n == exp_inc(gkt)

        @block.vector
        def _(vector):
            for m, (h, qc) in enumerate(steps):
                vector.wait_ge(cPE.sem, pv_inc(m * NKT + NKT - 1))
                inst = vector.reciprocal(wf[64:65, 3072:4096], psC[64:65, 0:1024])
                cDV.inc(inst)
                assert cDV.n == r_inc(m)
                vector.wait_ge(qLG.sem, lg_back(m))
                if m >= 1:
                    vector.wait_ge(qLG.sem, lg_move(m - 1))
                inst = vector.tensor_tensor(wkB[0:64, 2048:3072], psC[0:64, 0:1024],
                                            wf[0:64, 3072:4096], OP.mult)
                cDV.inc(inst)
                assert cDV.n == tt_inc(m)

        @block.gpsimd
        def _(g):
            for m, (h, qc) in enumerate(steps):
                ch, rh = h // 2, (h % 2) * 64
                g.wait_ge(cDV.sem, r_inc(m))
                inst = g.dma_start(out=dscr[:], in_=wf[64:65, 3072:4096])
                qLG.inc(inst)
                g.wait_ge(qLG.sem, qLG.n)
                if m >= 1:
                    g.wait_ge(cDV.sem, tt_inc(m - 1))
                inst = g.dma_start(out=wf[0:64, 3072:4096], in_=dscr.partition_broadcast(64))
                qLG.inc(inst)
                g.wait_ge(cDV.sem, tt_inc(m))
                inst = g.dma_start(
                    out=qarena[rh:rh + 64, ch * 2048 + qc * 1024:ch * 2048 + qc * 1024 + 1024],
                    in_=wkB[0:64, 2048:3072])
                qLG.inc(inst)

        assert cPE.n == pe0 + NSTEP * NKT * 2
        assert cAC.n == ac0 + NSTEP * NKT
        assert cDV.n == dv0 + NSTEP * 2
        assert qLG.n == lg0 + NSTEP * 3 * 16
        PE_ATT = cPE.n
        LG_ATT = qLG.n

        if debug:
            @block.sync
            def _(sync):
                sync.wait_ge(cPE.sem, PE_ATT)
                sync.wait_ge(qLG.sem, LG_ATT)
                qLD.inc(sync.dma_start(out=dbg_k[:], in_=karena[:, :]))
                qLD.inc(sync.dma_start(out=dbg_v[:], in_=varena[:, :]))
                qLD.inc(sync.dma_start(out=dbg_av[:], in_=qarena[:, :]))
        DBG_DUMP = qLD.n
        lg0 = qLG.n
        dv0 = cDV.n
        tern_pieces = [(w1t, c, p, 0) for c in range(8) for p in range(4)] + \
                      [(w2t, c, 0, 1) for c in range(32)]
        NT = len(tern_pieces)

        @block.gpsimd
        def _(g):
            g.wait_ge(cPE.sem, PE_ATT)
            if debug:
                g.wait_ge(qLD.sem, DBG_DUMP)
            for i in range(NT):
                src, c, p, w = tern_pieces[i]
                if i >= 1:
                    g.wait_ge(cDV.sem, dv0 + i * 2)
                inst = g.dma_start(
                    out=wf[:, (i % 2) * 1024:(i % 2) * 1024 + 1024],
                    in_=src[c * 128:(c + 1) * 128, p * 1024:(p + 1) * 1024])
                qLG.inc(inst)

        @block.vector
        def _(vector):
            for i in range(NT):
                src, c, p, w = tern_pieces[i]
                vector.wait_ge(qLG.sem, lg0 + (i + 1) * 16)
                stage = wf[:, (i % 2) * 1024:(i % 2) * 1024 + 1024]
                thr = wsm[:, (C_T1 if w == 0 else C_T2):(C_T1 if w == 0 else C_T2) + 1]
                nthr = wsm[:, (C_NT1 if w == 0 else C_NT2):(C_NT1 if w == 0 else C_NT2) + 1]
                inst = vector.tensor_scalar(wf[:, 2048:3072], stage, nthr, None, OP.is_lt)
                cDV.inc(inst)
                if w == 0:
                    dst = karena[:, c * 4096 + p * 1024:c * 4096 + p * 1024 + 1024]
                else:
                    dst = varena[:, c * 1024:(c + 1) * 1024]
                inst = vector.scalar_tensor_tensor(dst, stage, thr, wf[:, 2048:3072],
                                                   OP.is_gt, OP.subtract)
                cDV.inc(inst)
        assert cDV.n == dv0 + NT * 2
        qLG.n = lg0 + NT * 16
        DV_TERN = cDV.n

        lg0 = qLG.n
        dv0 = cDV.n

        @block.gpsimd
        def _(g):
            for c in range(8):
                if c >= 1:
                    g.wait_ge(cDV.sem, dv0 + c)
                inst = g.dma_start(out=wf[:, 0:1024], in_=wo[c * 128:(c + 1) * 128, :])
                qLG.inc(inst)
                g.wait_ge(cDV.sem, dv0 + c + 1)
                inst = g.dma_start(out=wo_bf[c], in_=wkB[:, 0:1024])
                qLG.inc(inst)
            for j in range(8):
                g.wait_ge(cDV.sem, dv0 + 8 + j)
                inst = g.dma_start(out=wf[:, 0:512],
                                   in_=b1[j * 512:(j + 1) * 512]
                                   .rearrange("(a n) -> a n", a=1).partition_broadcast(128))
                qLG.inc(inst)
            g.wait_ge(cDV.sem, dv0 + 16)
            inst = g.dma_start(out=wf[0:1, 0:1024],
                               in_=b2.rearrange("(a n) -> a n", a=1))
            qLG.inc(inst)
            g.wait_ge(cDV.sem, dv0 + 17)
            inst = g.dma_start(out=wf[:, 1024:2048],
                               in_=bo.rearrange("(a n) -> a n", a=1).partition_broadcast(128))
            qLG.inc(inst)

        @block.vector
        def _(vector):
            for c in range(8):
                vector.wait_ge(qLG.sem, lg0 + (2 * c + 1) * 16)
                inst = vector.tensor_copy(wkB[:, 0:1024], wf[:, 0:1024])
                cDV.inc(inst)
            for j in range(8):
                vector.wait_ge(qLG.sem, lg0 + (16 + j + 1) * 16)
                inst = vector.tensor_copy(wkA[:, j * 512:(j + 1) * 512], wf[:, 0:512])
                cDV.inc(inst)
            vector.wait_ge(qLG.sem, lg0 + 25 * 16)
            inst = vector.tensor_copy(b2row[0:1, :], wf[0:1, 0:1024])
            cDV.inc(inst)
        assert cDV.n == dv0 + 17
        qLG.n = lg0 + 26 * 16
        DV_W = cDV.n
        LG_W = qLG.n

        ld0 = qLD.n
        lg0 = qLG.n
        pe0 = cPE.n
        dv0 = cDV.n
        ac0 = cAC.n

        def pA_xdma(t):
            return ld0 + (t * 3 + 1) * 16

        def pA_wo(t, dh):
            return ld0 + (t * 3 + 2 + dh) * 16

        def pA_pe(t, k):
            return pe0 + t * 3 + k

        def pA_dv(t, k):
            return dv0 + t * 40 + k

        def pA_ac(t, k):
            return ac0 + t * 2 + k

        def pA_x1dma(t):
            return lg0 + (t + 1) * 16

        @block.sync
        def _(sync):
            sync.wait_ge(qLG.sem, LG_W)
            for t in range(NTT):
                if t >= 1:
                    sync.wait_ge(cDV.sem, pA_dv(t - 1, 1))
                inst = sync.dma_start(out=wf[:, 0:1024],
                                      in_=x_own[t * 128:(t + 1) * 128, :])
                qLD.inc(inst)
                for dh in range(2):
                    if t * 2 + dh >= 1:
                        pt_, pdh = divmod(t * 2 + dh - 1, 2)
                        sync.wait_ge(cPE.sem, pA_pe(pt_, pdh + 1))
                    inst = sync.dma_start(
                        out=wkB[:, 0:4096].rearrange("p (c x) -> p c x", x=512),
                        in_=wo_bf[:, :, dh * 512:dh * 512 + 512].rearrange("c p x -> p c x"))
                    qLD.inc(inst)

        @block.tensor
        def _(t_):
            t_.wait_ge(qLG.sem, LG_ATT)
            if debug:
                t_.wait_ge(qLD.sem, DBG_DUMP)
            for t in range(NTT):
                ps = psBIG[:, (t % 2) * 1024:(t % 2) * 1024 + 1024]
                if t >= 2:
                    t_.wait_ge(cDV.sem, pA_dv(t - 2, 2))
                for dh in range(2):
                    t_.wait_ge(qLD.sem, pA_wo(t, dh))
                    for c in range(8):
                        inst = t_.matmul(ps[:, dh * 512:dh * 512 + 512],
                                         qarena[:, c * 2048 + t * 128:c * 2048 + t * 128 + 128],
                                         wkB[:, c * 512:c * 512 + 512],
                                         start=(c == 0), stop=(c == 7))
                    cPE.inc(inst)
                    assert cPE.n == pA_pe(t, dh + 1)
                t_.wait_ge(cDV.sem, pA_dv(t, 38))
                if t >= 1:
                    t_.wait_ge(cDV.sem, pA_dv(t - 1, 40))
                for dc in range(8):
                    dst = psTa if dc < 4 else psTb
                    inst = t_.transpose(dst[:, (dc % 4) * 128:(dc % 4) * 128 + 128],
                                        wkB[:, 4096 + dc * 128:4096 + dc * 128 + 128],
                                        ident[:])
                cPE.inc(inst)
                assert cPE.n == pA_pe(t, 3)

        @block.vector
        def _(vector):
            for t in range(NTT):
                k = 0
                vector.wait_ge(qLD.sem, pA_xdma(t))
                inst = vector.tensor_add(wf[:, 2048:3072], wf[:, 0:1024], wf[:, 1024:2048])
                k += 1; cDV.inc(inst)
                vector.wait_ge(cPE.sem, pA_pe(t, 2))
                ps = psBIG[:, (t % 2) * 1024:(t % 2) * 1024 + 1024]
                inst = vector.tensor_add(wf[:, 2048:3072], ps, wf[:, 2048:3072])
                k += 1; cDV.inc(inst)
                inst = vector.reduce_sum(wsm[:, ST:ST + 1], wf[:, 2048:3072], AX.X)
                k += 1; cDV.inc(inst); vector.wait_ge(cDV.sem, cDV.n)
                inst = vector.tensor_scalar_mul(wsm[:, ST + 1:ST + 2], wsm[:, ST:ST + 1], -1.0 / D)
                k += 1; cDV.inc(inst); vector.wait_ge(cDV.sem, cDV.n)
                if t >= 1:
                    vector.wait_ge(qLG.sem, pA_x1dma(t - 1))
                inst = vector.tensor_scalar(wf[:, 3072:4096], wf[:, 2048:3072],
                                            wsm[:, ST + 1:ST + 2], None, OP.add)
                k += 1; cDV.inc(inst)
                vector.wait_ge(cAC.sem, pA_ac(t, 1))
                inst = vector.reduce_sum(wsm[:, ST + 2:ST + 3], wf[:, 4096:5120], AX.X)
                k += 1; cDV.inc(inst); vector.wait_ge(cDV.sem, cDV.n)
                inst = vector.tensor_scalar(wsm[:, ST + 3:ST + 4], wsm[:, ST + 2:ST + 3],
                                            1.0 / D, EPS_LN, OP.mult, OP.add)
                k += 1; cDV.inc(inst)
                vector.wait_ge(cAC.sem, pA_ac(t, 2))
                inst = vector.reciprocal(wsm[:, ST + 5:ST + 6], wsm[:, ST + 4:ST + 5])
                k += 1; cDV.inc(inst); vector.wait_ge(cDV.sem, cDV.n)
                inst = vector.tensor_scalar(wf[:, 3072:4096], wf[:, 3072:4096],
                                            wsm[:, ST + 5:ST + 6], None, OP.mult)
                k += 1; cDV.inc(inst)
                inst = vector.reduce_max(wsm[:, ST + 6:ST + 7], wf[:, 3072:4096], AX.X,
                                         apply_absolute_value=True)
                k += 1; cDV.inc(inst); vector.wait_ge(cDV.sem, cDV.n)
                inst = vector.tensor_scalar_max(wsm[:, ST + 6:ST + 7], wsm[:, ST + 6:ST + 7], EPS_Q)
                k += 1; cDV.inc(inst); vector.wait_ge(cDV.sem, cDV.n)
                inst = vector.reciprocal(wsm[:, ST + 7:ST + 8], wsm[:, ST + 6:ST + 7])
                k += 1; cDV.inc(inst); vector.wait_ge(cDV.sem, cDV.n)
                inst = vector.tensor_scalar_mul(wsm[:, ST + 7:ST + 8], wsm[:, ST + 7:ST + 8], 127.0)
                k += 1; cDV.inc(inst); vector.wait_ge(cDV.sem, cDV.n)
                inst = vector.tensor_tensor(wsm[:, S1C + t:S1C + t + 1],
                                            wsm[:, ST + 6:ST + 7],
                                            wsm[:, C_WQ1:C_WQ1 + 1], OP.mult)
                k += 1; cDV.inc(inst)
                for qch in range(2):
                    k += emit_quant_chunk(vector, cDV, wf, wri,
                                          wsm[:, ST + 7:ST + 8],
                                          wf[:, 3072 + qch * 512:3072 + qch * 512 + 512],
                                          wkB[:, 4096 + qch * 512:4096 + qch * 512 + 512])
                assert k == 38
                vector.wait_ge(cPE.sem, pA_pe(t, 3))
                inst = vector.tensor_copy(
                    qarena.ap().rearrange("p (c n) -> p c n", n=2048)[:, 0:4, t * 128:t * 128 + 128],
                    psTa.ap().rearrange("p (c n) -> p c n", n=128))
                k += 1; cDV.inc(inst)
                inst = vector.tensor_copy(
                    qarena.ap().rearrange("p (c n) -> p c n", n=2048)[:, 4:8, t * 128:t * 128 + 128],
                    psTb.ap().rearrange("p (c n) -> p c n", n=128))
                k += 1; cDV.inc(inst)
                assert cDV.n == pA_dv(t, 40)

        @block.scalar
        def _(scalar):
            for t in range(NTT):
                scalar.wait_ge(cDV.sem, pA_dv(t, 5))
                inst = scalar.activation(wf[:, 4096:5120], wf[:, 3072:4096], AF.Square)
                cAC.inc(inst)
                scalar.wait_ge(cDV.sem, pA_dv(t, 7))
                inst = scalar.activation(wsm[:, ST + 4:ST + 5], wsm[:, ST + 3:ST + 4], AF.Sqrt)
                cAC.inc(inst)

        @block.gpsimd
        def _(g):
            for t in range(NTT):
                g.wait_ge(cDV.sem, pA_dv(t, 38))
                inst = g.dma_start(out=x1scr[t * 128:(t + 1) * 128, :], in_=wf[:, 3072:4096])
                qLG.inc(inst)

        qLD.n = ld0 + NTT * 3 * 16
        assert cPE.n == pe0 + NTT * 3
        assert cDV.n == dv0 + NTT * 40
        assert cAC.n == ac0 + NTT * 2
        X1DMA = [pA_x1dma(t) for t in range(NTT)]
        qLG.n = lg0 + NTT * 16
        DV_A = cDV.n

        ld0 = qLD.n
        pe0 = cPE.n
        dv0 = cDV.n
        ac0 = cAC.n

        lgB0 = qLG.n

        def pB_pe(t, k):
            return pe0 + t * 25 + k

        def pB_lg(t, k):
            return lgB0 + (t * 2 + k) * 16

        def pB_dv(t, k):
            return dv0 + t * 127 + k

        def pB_ac(t, k):
            return ac0 + t * 10 + k

        def pB_x1(t):
            return ld0 + (t + 1) * 16

        def pB_y(t):
            return (t + 1) * 16

        @block.sync
        def _(sync):
            for t in range(NTT):
                sync.wait_ge(cDV.sem, pB_dv(t, 110))
                sync.wait_ge(qLG.sem, X1DMA[t])
                inst = sync.dma_start(out=wf[:, 4096:5120], in_=x1scr[t * 128:(t + 1) * 128, :])
                qLD.inc(inst)

        @block.tensor
        def _(t_):
            t_.wait_ge(cDV.sem, DV_A)
            for t in range(NTT):
                for jh in range(2):
                    if jh == 1:
                        t_.wait_ge(cDV.sem, pB_dv(t, 4))
                    elif t >= 1:
                        t_.wait_ge(cDV.sem, pB_dv(t - 1, 8))
                    for dc in range(8):
                        lhs = qarena[:, dc * 2048 + t * 128:dc * 2048 + t * 128 + 128]
                        for sl in range(4):
                            inst = t_.matmul(
                                psBIG[:, sl * 512:sl * 512 + 512], lhs,
                                karena[:, dc * 4096 + jh * 2048 + sl * 512:
                                       dc * 4096 + jh * 2048 + sl * 512 + 512],
                                start=(dc == 0), stop=(dc == 7))
                            if dc == 7:
                                cPE.inc(inst)
                    assert cPE.n == pB_pe(t, jh * 4 + 4)
                t_.wait_ge(cDV.sem, pB_dv(t, 110))
                if t >= 1:
                    t_.wait_ge(cDV.sem, pB_dv(t - 1, 120))

                def _tr(grp):
                    if grp >= 2:
                        t_.wait_ge(cDV.sem, pB_dv(t, 112 + grp - 2))
                    elif t >= 1:
                        t_.wait_ge(cDV.sem, pB_dv(t - 1, 118 + grp))
                    dstp = psTa if grp % 2 == 0 else psTb
                    for i in range(4):
                        hc = grp * 4 + i
                        inst2 = t_.transpose(
                            dstp[:, i * 128:i * 128 + 128],
                            wkB[:, hc * 128:hc * 128 + 128], ident[:])
                    cPE.inc(inst2)

                def _mm(grp):
                    t_.wait_ge(cDV.sem, pB_dv(t, 112 + grp))
                    for i in range(4):
                        hc = grp * 4 + i
                        lhs = wkB[:, 4096 + (grp % 2) * 512 + i * 128:
                                  4096 + (grp % 2) * 512 + i * 128 + 128]
                        for half in range(2):
                            inst2 = t_.matmul(psC[:, half * 512:half * 512 + 512], lhs,
                                              varena[:, hc * 1024 + half * 512:
                                                     hc * 1024 + half * 512 + 512],
                                              start=(hc == 0), stop=False)
                    cPE.inc(inst2)

                _tr(0)
                _tr(1)
                for grp in range(2, 8):
                    _mm(grp - 2)
                    _tr(grp)
                _mm(6)
                _mm(7)
                t_.wait_ge(cDV.sem, pB_dv(t, 111))
                for half in range(2):
                    inst = t_.matmul(psC[:, half * 512:half * 512 + 512],
                                     s2invrow[0:1, :],
                                     b2row[0:1, half * 512:half * 512 + 512],
                                     start=False, stop=True)
                cPE.inc(inst)
                assert cPE.n == pB_pe(t, 25)

        @block.vector
        def _(vector):
            for t in range(NTT):
                k = 0
                for jh in range(2):
                    for sl in range(4):
                        gidx = jh * 4 + sl
                        vector.wait_ge(cPE.sem, pB_pe(t, gidx + 1))
                        if gidx >= 2:
                            vector.wait_ge(cAC.sem, pB_ac(t, gidx - 1))
                        elif t >= 1:
                            vector.wait_ge(cAC.sem, pB_ac(t - 1, 7 + gidx))
                        inst = vector.scalar_tensor_tensor(
                            wf[:, 4096 + (gidx % 2) * 512:4096 + (gidx % 2) * 512 + 512],
                            psBIG[:, sl * 512:sl * 512 + 512],
                            wsm[:, S1C + t:S1C + t + 1],
                            wkA[:, jh * 2048 + sl * 512:jh * 2048 + sl * 512 + 512],
                            OP.mult, OP.add)
                        k += 1; cDV.inc(inst)
                vector.wait_ge(cAC.sem, pB_ac(t, 8))
                inst = vector.reduce_max(wsm[:, ST:ST + 1], wf[:, 0:4096], AX.X,
                                         apply_absolute_value=True)
                k += 1; cDV.inc(inst); vector.wait_ge(cDV.sem, cDV.n)
                inst = vector.tensor_scalar_max(wsm[:, ST:ST + 1], wsm[:, ST:ST + 1], EPS_Q)
                k += 1; cDV.inc(inst); vector.wait_ge(cDV.sem, cDV.n)
                inst = vector.reciprocal(wsm[:, ST + 1:ST + 2], wsm[:, ST:ST + 1])
                k += 1; cDV.inc(inst); vector.wait_ge(cDV.sem, cDV.n)
                inst = vector.tensor_scalar_mul(wsm[:, ST + 1:ST + 2], wsm[:, ST + 1:ST + 2],
                                                127.0)
                k += 1; cDV.inc(inst); vector.wait_ge(cDV.sem, cDV.n)
                inst = vector.tensor_tensor(wsm[:, ST + 2:ST + 3], wsm[:, ST:ST + 1],
                                            wsm[:, C_WQ2:C_WQ2 + 1], OP.mult)
                k += 1; cDV.inc(inst); vector.wait_ge(cDV.sem, cDV.n)
                inst = vector.reciprocal(wsm[:, ST + 3:ST + 4], wsm[:, ST + 2:ST + 3])
                k += 1; cDV.inc(inst)
                assert k == 14
                for qch in range(8):
                    k += emit_quant_chunk(vector, cDV, wf, wri,
                                          wsm[:, ST + 1:ST + 2],
                                          wf[:, qch * 512:qch * 512 + 512],
                                          wkB[:, qch * 512:qch * 512 + 512])
                assert k == 110
                vector.wait_ge(qLG.sem, pB_lg(t, 2))
                inst = vector.tensor_copy(s2invrow[0:1, :], wsm[0:1, 64:192])
                k += 1; cDV.inc(inst)
                for grp in range(8):
                    tri = 9 + grp if grp < 2 else 8 + 2 * grp
                    vector.wait_ge(cPE.sem, pB_pe(t, tri))
                    if grp >= 2:
                        mmi = 11 + 2 * (grp - 2) if grp - 2 <= 5 else 17 + (grp - 2)
                        vector.wait_ge(cPE.sem, pB_pe(t, mmi))
                    elif t >= 1:
                        pmi = 17 + (grp + 6)
                        vector.wait_ge(cPE.sem, pB_pe(t - 1, pmi))
                    srcp = psTa if grp % 2 == 0 else psTb
                    inst = vector.tensor_copy(
                        wkB[:, 4096 + (grp % 2) * 512:4096 + (grp % 2) * 512 + 512],
                        srcp[:, 0:512])
                    k += 1; cDV.inc(inst)
                vector.wait_ge(qLD.sem, pB_x1(t))
                vector.wait_ge(cPE.sem, pB_pe(t, 25))
                if t >= 1:
                    vector.wait_ge(qOUT.sem, pB_y(t - 1))
                inst = vector.scalar_tensor_tensor(wf[:, 0:1024], psC[:, 0:1024],
                                                   wsm[:, ST + 2:ST + 3],
                                                   wf[:, 4096:5120], OP.mult, OP.add)
                k += 1; cDV.inc(inst)
                inst = vector.reduce_sum(wsm[:, ST + 4:ST + 5], wf[:, 0:1024], AX.X)
                k += 1; cDV.inc(inst); vector.wait_ge(cDV.sem, cDV.n)
                inst = vector.tensor_scalar_mul(wsm[:, ST + 5:ST + 6], wsm[:, ST + 4:ST + 5],
                                                -1.0 / D)
                k += 1; cDV.inc(inst); vector.wait_ge(cDV.sem, cDV.n)
                inst = vector.tensor_scalar(wf[:, 0:1024], wf[:, 0:1024],
                                            wsm[:, ST + 5:ST + 6], None, OP.add)
                k += 1; cDV.inc(inst)
                vector.wait_ge(cAC.sem, pB_ac(t, 9))
                inst = vector.reduce_sum(wsm[:, ST + 7:ST + 8], wf[:, 1024:2048], AX.X)
                k += 1; cDV.inc(inst); vector.wait_ge(cDV.sem, cDV.n)
                inst = vector.tensor_scalar(wsm[:, ST + 6:ST + 7], wsm[:, ST + 7:ST + 8],
                                            1.0 / D, EPS_LN, OP.mult, OP.add)
                k += 1; cDV.inc(inst)
                vector.wait_ge(cAC.sem, pB_ac(t, 10))
                inst = vector.reciprocal(wsm[:, ST + 5:ST + 6], wsm[:, ST + 7:ST + 8])
                k += 1; cDV.inc(inst); vector.wait_ge(cDV.sem, cDV.n)
                inst = vector.tensor_scalar(wf[:, 0:1024], wf[:, 0:1024],
                                            wsm[:, ST + 5:ST + 6], None, OP.mult)
                k += 1; cDV.inc(inst)
                assert k == 127
                assert cDV.n == pB_dv(t, 127)

        @block.scalar
        def _(scalar):
            for t in range(NTT):
                for jh in range(2):
                    for sl in range(4):
                        gidx = jh * 4 + sl
                        scalar.wait_ge(cDV.sem, pB_dv(t, gidx + 1))
                        if gidx < 2 and t >= 1:
                            scalar.wait_ge(qOUT.sem, pB_y(t - 1))
                        inst = scalar.activation(
                            wf[:, gidx * 512:gidx * 512 + 512],
                            wf[:, 4096 + (gidx % 2) * 512:4096 + (gidx % 2) * 512 + 512],
                            AF.Gelu)
                        cAC.inc(inst)
                scalar.wait_ge(cDV.sem, pB_dv(t, 123))
                inst = scalar.activation(wf[:, 1024:2048], wf[:, 0:1024], AF.Square)
                cAC.inc(inst)
                scalar.wait_ge(cDV.sem, pB_dv(t, 125))
                inst = scalar.activation(wsm[:, ST + 7:ST + 8], wsm[:, ST + 6:ST + 7], AF.Sqrt)
                cAC.inc(inst)
                assert cAC.n == pB_ac(t, 10)

        if debug:
            @block.vector
            def _(vector):
                vector.wait_ge(qOUT.sem, 0)
        @block.gpsimd
        def _(g):
            for t in range(NTT):
                g.wait_ge(cDV.sem, pB_dv(t, 14))
                if t >= 1:
                    g.wait_ge(cDV.sem, pB_dv(t - 1, 111))
                inst = g.dma_start(out=rscr.rearrange("(p a) -> p a", a=1),
                                   in_=wsm[:, ST + 3:ST + 4])
                qLG.inc(inst)
                g.wait_ge(qLG.sem, pB_lg(t, 1))
                inst = g.dma_start(out=wsm[0:1, 64:192],
                                   in_=rscr.rearrange("(a n) -> a n", a=1))
                qLG.inc(inst)
                g.wait_ge(cDV.sem, pB_dv(t, 127))
                inst = g.dma_start(out=y[t * 128:(t + 1) * 128, :], in_=wf[:, 0:1024])
                qOUT.inc(inst)
            if debug:
                qOUT.inc(g.dma_start(out=dbg_wsm[:], in_=wsm[:, :]))
                qOUT.inc(g.dma_start(out=dbg_x1[:], in_=x1scr[:]))
                qOUT.inc(g.dma_start(out=dbg_wf[:], in_=wf[:, :]))
                qOUT.inc(g.dma_start(out=dbg_wo[:], in_=wo_bf[:]))
                qOUT.inc(g.dma_start(out=dbg_q[:], in_=qarena[:, :]))
            g.wait_ge(qOUT.sem, qOUT.n)

        qLD.n = ld0 + NTT * 16
        assert cPE.n == pe0 + NTT * 25
        assert cDV.n == dv0 + NTT * 127
        assert cAC.n == ac0 + NTT * 10
        assert qLG.n == lgB0 + NTT * 2 * 16

    return nc


_CACHE = {}


def kernel(L0_attn_x, L0_attn_W_qkv, L0_attn_b_qkv, L0_attn_W_o, L0_attn_b_o,
           L0_ffn_W1, L0_ffn_b1, L0_ffn_W2, L0_ffn_b2):
    import ml_dtypes
    x = np.ascontiguousarray(np.asarray(L0_attn_x, dtype=np.float32))
    wqkv_ = np.ascontiguousarray(np.asarray(L0_attn_W_qkv, dtype=np.float32))
    bqkv_ = np.ascontiguousarray(np.asarray(L0_attn_b_qkv, dtype=np.float32))
    wo_ = np.ascontiguousarray(np.asarray(L0_attn_W_o, dtype=np.float32))
    bo_ = np.ascontiguousarray(np.asarray(L0_attn_b_o, dtype=np.float32))
    w1t_ = np.ascontiguousarray(np.asarray(L0_ffn_W1, dtype=np.float32).T)
    b1_ = np.ascontiguousarray(np.asarray(L0_ffn_b1, dtype=np.float32))
    w2t_ = np.ascontiguousarray(np.asarray(L0_ffn_W2, dtype=np.float32).T)
    b2_ = np.ascontiguousarray(np.asarray(L0_ffn_b2, dtype=np.float32))
    ident_ = np.eye(128, dtype=ml_dtypes.bfloat16)

    if "nc" not in _CACHE:
        _CACHE["nc"] = build_nc()
    nc = _CACHE["nc"]

    in_maps = []
    for c in range(8):
        b, half = c // 2, c % 2
        xT_ = np.ascontiguousarray(x[b].T)
        in_maps.append({
            "xT": xT_,
            "xTq": np.ascontiguousarray(xT_[:, half * QO:(half + 1) * QO]),
            "x_own": np.ascontiguousarray(x[b, half * QO:(half + 1) * QO]),
            "wqkv": wqkv_, "bqkv": bqkv_, "wo": wo_, "bo": bo_,
            "w1t": w1t_, "b1": b1_, "w2t": w2t_, "b2": b2_,
            "ident_in": ident_,
        })
    res = run_bass_kernel_spmd(nc, in_maps, core_ids=list(range(8)))
    out = np.empty((B, S, D), dtype=np.float32)
    for c in range(8):
        b, half = c // 2, c % 2
        out[b, half * QO:(half + 1) * QO] = res.results[c]["y"]
    return out


# version 13
# speedup vs baseline: 1.3694x; 1.3694x over previous
import numpy as np
from contextlib import ExitStack

import concourse.bass as bass
import concourse.mybir as mybir
from concourse.bass_utils import run_bass_kernel_spmd
from concourse.alu_op_type import AluOpType as OP

F32 = mybir.dt.float32
BF16 = mybir.dt.bfloat16
I32 = mybir.dt.int32
AF = mybir.ActivationFunctionType
AX = mybir.AxisListType

B, S, D = 4, 4096, 1024
NH, HS = 16, 64
H = 4096
QO = 2048
NKT = S // 128
NTT = QO // 128
EPS_LN = 1e-5
EPS_Q = 1e-5
VSTRIDE = HS + 1
VTILE = NH * VSTRIDE

NEGM = 0
BQ = 16
BK = 24
S1C = 32
WS = 48
ST = 56
MQ2 = 64
C_T1, C_NT1, C_T2, C_NT2 = WS + 2, WS + 3, WS + 4, WS + 5
C_WQ1, C_WQ2 = WS + 6, WS + 7


QCH_OPS = 12


def emit_quant_chunk(vector, cDV, wf, wri, shm_ap, src_ap, dst_ap):
    from concourse.alu_op_type import AluOpType as OP
    qa = wf[:, 4096:4608]
    qb = wf[:, 4608:5120]
    qc = wf[:, 5120:5632]
    n = 0
    inst = vector.tensor_scalar(qa, src_ap, shm_ap, 256.0, OP.mult, OP.add)
    n += 1; cDV.inc(inst)
    inst = vector.tensor_copy(wri[:, :], qa)
    n += 1; cDV.inc(inst)
    inst = vector.tensor_copy(qb, wri[:, :])
    n += 1; cDV.inc(inst)
    inst = vector.tensor_sub(qa, qa, qb)
    n += 1; cDV.inc(inst)
    inst = vector.tensor_scalar(qc, qa, -0.5, None, OP.is_lt)
    n += 1; cDV.inc(inst)
    inst = vector.scalar_tensor_tensor(qc, qa, -1.5, qc, OP.is_lt, OP.add)
    n += 1; cDV.inc(inst)
    inst = vector.scalar_tensor_tensor(qc, qa, -2.5, qc, OP.is_lt, OP.add)
    n += 1; cDV.inc(inst)
    inst = vector.scalar_tensor_tensor(qb, qc, -1.0, qb, OP.mult, OP.add)
    n += 1; cDV.inc(inst)
    inst = vector.tensor_scalar(qc, qa, 0.5, None, OP.is_ge)
    n += 1; cDV.inc(inst)
    inst = vector.scalar_tensor_tensor(qc, qa, 1.5, qc, OP.is_ge, OP.add)
    n += 1; cDV.inc(inst)
    inst = vector.scalar_tensor_tensor(qc, qa, 2.5, qc, OP.is_ge, OP.add)
    n += 1; cDV.inc(inst)
    inst = vector.scalar_tensor_tensor(dst_ap, qb, -256.0, qc, OP.add, OP.add)
    n += 1; cDV.inc(inst)
    assert n == QCH_OPS
    return n


class Ctr:
    def __init__(self, sem, step=1):
        self.sem, self.step, self.n = sem, step, 0

    def inc(self, inst):
        inst.then_inc(self.sem, self.step)
        self.n += self.step
        return self.n


def build_nc(debug=False):
    nc = bass.Bass("TRN2", target_bir_lowering=False, debug=False,
                   detect_race_conditions=False)

    xT = nc.dram_tensor("xT", [D, S], F32, kind="ExternalInput").ap()
    xTq = nc.dram_tensor("xTq", [D, QO], F32, kind="ExternalInput").ap()
    x_own = nc.dram_tensor("x_own", [QO, D], F32, kind="ExternalInput").ap()
    wqkv = nc.dram_tensor("wqkv", [D, 3 * D], F32, kind="ExternalInput").ap()
    bqkv = nc.dram_tensor("bqkv", [3 * D], F32, kind="ExternalInput").ap()
    wo = nc.dram_tensor("wo", [D, D], F32, kind="ExternalInput").ap()
    bo = nc.dram_tensor("bo", [D], F32, kind="ExternalInput").ap()
    w1t = nc.dram_tensor("w1t", [D, H], F32, kind="ExternalInput").ap()
    b1 = nc.dram_tensor("b1", [H], F32, kind="ExternalInput").ap()
    w2t = nc.dram_tensor("w2t", [H, D], F32, kind="ExternalInput").ap()
    b2 = nc.dram_tensor("b2", [D], F32, kind="ExternalInput").ap()
    ident_in = nc.dram_tensor("ident_in", [128, 128], BF16, kind="ExternalInput").ap()
    y = nc.dram_tensor("y", [QO, D], F32, kind="ExternalOutput").ap()
    if debug:
        dbg_wsm = nc.dram_tensor("dbg_wsm", [128, 192], F32, kind="ExternalOutput").ap()
        dbg_x1 = nc.dram_tensor("dbg_x1", [QO, D], F32, kind="ExternalOutput").ap()
        dbg_wf = nc.dram_tensor("dbg_wf", [128, 5632], F32, kind="ExternalOutput").ap()
        dbg_wo = nc.dram_tensor("dbg_wo", [8, 128, D], BF16, kind="ExternalOutput").ap()
        dbg_q = nc.dram_tensor("dbg_q", [128, 8 * QO], BF16, kind="ExternalOutput").ap()
        dbg_k = nc.dram_tensor("dbg_k", [128, 8 * 4096], BF16, kind="ExternalOutput").ap()
        dbg_v = nc.dram_tensor("dbg_v", [128, NKT * VTILE], BF16, kind="ExternalOutput").ap()
        dbg_av = nc.dram_tensor("dbg_av", [128, 8 * QO], BF16, kind="ExternalOutput").ap()

    dscr = nc.dram_tensor("dscr", [1, 1024], F32, kind="Internal").ap()
    mscr = nc.dram_tensor("mscr", [16], F32, kind="Internal").ap()
    sscr = nc.dram_tensor("sscr", [8], F32, kind="Internal").ap()
    rscr = nc.dram_tensor("rscr", [128], F32, kind="Internal").ap()
    wo_bf = nc.dram_tensor("wo_bf", [8, 128, D], BF16, kind="Internal").ap()
    x1scr = nc.dram_tensor("x1scr", [QO, D], F32, kind="Internal").ap()

    with ExitStack() as ctx:
        e = ctx.enter_context
        karena = e(nc.sbuf_tensor("karena", [128, 8 * 4096], BF16))
        varena = e(nc.sbuf_tensor("varena", [128, NKT * VTILE], BF16))
        qarena = e(nc.sbuf_tensor("qarena", [128, 8 * QO], BF16))
        wkA = e(nc.sbuf_tensor("wkA", [128, 4096], BF16))
        wkB = e(nc.sbuf_tensor("wkB", [128, 5632], BF16))
        wf = e(nc.sbuf_tensor("wf", [128, 5632], F32))
        wsm = e(nc.sbuf_tensor("wsm", [128, 192], F32))
        wri = e(nc.sbuf_tensor("wri", [128, 512], I32))
        ident = e(nc.sbuf_tensor("ident", [128, 128], BF16))
        wsb = e(nc.sbuf_tensor("wsb", [128, 32], BF16))
        b2row = e(nc.sbuf_tensor("b2row", [1, 1024], BF16))
        s2invrow = e(nc.sbuf_tensor("s2invrow", [1, 128], BF16))

        psBIG = e(nc.psum_tensor("psBIG", [128, 2048], F32))
        psC = e(nc.psum_tensor("psC", [128, 1024], F32))
        psTa = e(nc.psum_tensor("psTa", [128, 512], BF16))
        psTb = e(nc.psum_tensor("psTb", [128, 512], BF16))

        qLD = Ctr(e(nc.semaphore("qLD")), 16)
        qLG = Ctr(e(nc.semaphore("qLG")), 16)
        qOUT = Ctr(e(nc.semaphore("qOUT")), 16)
        cPE = Ctr(e(nc.semaphore("cPE")))
        cP0 = Ctr(e(nc.semaphore("cP0")))
        cDV = Ctr(e(nc.semaphore("cDV")))
        cAC = Ctr(e(nc.semaphore("cAC")))

        block = e(nc.Block())

        @block.sync
        def _(sync):
            qLD.inc(sync.dma_start(out=ident[:], in_=ident_in[:]))
            for fc in range(8):
                qLD.inc(sync.dma_start(
                    out=wsm[:, BQ + fc:BQ + fc + 1],
                    in_=bqkv[fc * 128:(fc + 1) * 128].rearrange("(p a) -> p a", a=1)))
                qLD.inc(sync.dma_start(
                    out=wsm[:, BK + fc:BK + fc + 1],
                    in_=bqkv[1024 + fc * 128:1024 + (fc + 1) * 128].rearrange("(p a) -> p a", a=1)))
            inst = sync.dma_start(out=wf[:, 4096:5120],
                                  in_=bqkv[2048:3072].rearrange("(a n) -> a n", a=1)
                                  .partition_broadcast(128))
            qLD.inc(inst)
        LD_CONST = qLD.n

        @block.vector
        def _(vector):
            for inst in [vector.memset(wsm[:, NEGM:NEGM + 1], -16.0),
                         vector.memset(wsb[:, :], 0.0),
                         vector.memset(wsb[0:64, 0:1], 1.0),
                         vector.memset(wsb[64:128, 1:2], 1.0),
                         vector.memset(wsb[:, 2:3], 1.0),
                         vector.memset(wsm[:, 96:97], 1.0),
                         vector.memset(varena[:, :], 1.0)]:
                cDV.inc(inst)
                vector.wait_ge(cDV.sem, cDV.n)
        DV_CONST = cDV.n

        NW = 128
        lgP0 = qLG.n

        @block.gpsimd
        def _(g):
            for i in range(NW):
                if i >= 1:
                    g.wait_ge(cP0.sem, i * 2)
                if i < 64:
                    c, p = divmod(i, 8)
                    srcap = w1t[c * 128:(c + 1) * 128, p * 512:(p + 1) * 512]
                else:
                    c, p = divmod(i - 64, 2)
                    srcap = w2t[c * 128:(c + 1) * 128, p * 512:(p + 1) * 512]
                inst = g.dma_start(out=wf[:, 5120:5632], in_=srcap)
                qLG.inc(inst)

        p0b_state = {"i": 0}

        def p0b_emit_some(vector, n):
            for _ in range(n):
                i = p0b_state["i"]
                if i >= NW:
                    return
                p0b_state["i"] = i + 1
                vector.wait_ge(qLG.sem, lgP0 + (i + 1) * 16)
                inst = vector.reduce_sum(wsm[:, ST + 4:ST + 5], wf[:, 5120:5632],
                                         AX.X, apply_absolute_value=True)
                cP0.inc(inst)
                vector.wait_ge(cP0.sem, cP0.n)
                acc = wsm[:, ST + 5:ST + 6] if i < 64 else wsm[:, ST + 6:ST + 7]
                if i == 0 or i == 64:
                    inst = vector.tensor_copy(acc, wsm[:, ST + 4:ST + 5])
                else:
                    inst = vector.tensor_add(acc, acc, wsm[:, ST + 4:ST + 5])
                cP0.inc(inst)
                vector.wait_ge(cP0.sem, cP0.n)

        def emit_proj(sub, xsrc, ntb, wcol0):
            ld0 = qLD.n
            dv0 = cDV.n
            pe0 = cPE.n

            xr = xsrc.rearrange("(c p) n -> p c n", p=128)
            wr = wqkv.rearrange("(c p) n -> p c n", p=128)

            def dma_cnt(tb, j):
                return ld0 + (tb * 6 + j + 1) * 16

            def cast_cnt(tb, j):
                return dv0 + tb * 14 + j + 1

            def evac(tb, fh, g):
                return dv0 + tb * 14 + 6 + fh * 4 + g + 1

            def pe_grp(tb, fh, g):
                return pe0 + (tb * 2 + fh) * 4 + g + 1

            @block.sync
            def _(sync):
                for tb in range(ntb):
                    for j in range(6):
                        if tb == 0 and j in (0, 2):
                            sync.wait_ge(cDV.sem, dv0)
                        elif j in (0, 1):
                            px = (tb, 0) if j == 1 else (tb - 1, 1)
                            sync.wait_ge(cDV.sem, cast_cnt(px[0], px[1]))
                        else:
                            pw = (tb, j - 1) if j > 2 else (tb - 1, 5)
                            sync.wait_ge(cDV.sem, cast_cnt(pw[0], pw[1]))
                        if j < 2:
                            inst = sync.dma_start(
                                out=wf[:, 0:2048].rearrange("p (c n) -> p c n", n=512),
                                in_=xr[:, j * 4:(j + 1) * 4, tb * 512:tb * 512 + 512])
                        else:
                            fh, piece = divmod(j - 2, 2)
                            inst = sync.dma_start(
                                out=wf[:, 2048:4096].rearrange("p (c n) -> p c n", n=512),
                                in_=wr[:, piece * 4:(piece + 1) * 4,
                                       wcol0 + fh * 512:wcol0 + fh * 512 + 512])
                        qLD.inc(inst)

            @block.vector
            def _(vector):
                for tb in range(ntb):
                    for j in range(6):
                        vector.wait_ge(qLD.sem, dma_cnt(tb, j))
                        if j < 2:
                            if j == 0:
                                if tb >= 1:
                                    vector.wait_ge(cPE.sem, pe_grp(tb - 1, 1, 3))
                                else:
                                    vector.wait_ge(cPE.sem, pe0)
                            inst = vector.tensor_copy(
                                wkB[:, j * 2048:(j + 1) * 2048], wf[:, 0:2048])
                        else:
                            fh, piece = divmod(j - 2, 2)
                            if piece == 0:
                                if tb * 2 + fh >= 1:
                                    pfh = tb * 2 + fh - 1
                                    vector.wait_ge(cPE.sem, pe_grp(pfh // 2, pfh % 2, 3))
                                else:
                                    vector.wait_ge(cPE.sem, pe0)
                            inst = vector.tensor_copy(
                                wkA[:, piece * 2048:(piece + 1) * 2048], wf[:, 2048:4096])
                        cDV.inc(inst)
                        assert cDV.n == cast_cnt(tb, j)
                    for fh in range(2):
                        for g in range(4):
                            vector.wait_ge(cPE.sem, pe_grp(tb, fh, g))
                            q = (tb * 2 + fh) * 4 + g
                            ps = psBIG[:, (q % 4) * 512:(q % 4) * 512 + 512]
                            if sub == "K":
                                fc = fh * 4 + g
                                inst = vector.tensor_scalar(
                                    karena[:, fc * 4096 + tb * 512:fc * 4096 + tb * 512 + 512],
                                    ps, wsm[:, BK + fc:BK + fc + 1], None, OP.add)
                            elif sub == "Q":
                                fc = fh * 4 + g
                                inst = vector.tensor_scalar(
                                    qarena[:, fc * 2048 + tb * 512:fc * 2048 + tb * 512 + 512],
                                    ps, wsm[:, BQ + fc:BQ + fc + 1], None, OP.add)
                            else:
                                kt = tb * 4 + g
                                h0 = fh * 8
                                dst = varena[:, kt * VTILE:(kt + 1) * VTILE] \
                                    .rearrange("p (h v) -> p h v", v=VSTRIDE)[:, h0:h0 + 8, 0:64]
                                inst = vector.tensor_tensor(
                                    dst,
                                    ps.rearrange("p (h v) -> p h v", v=64),
                                    wf[:, 4096 + fh * 512:4096 + fh * 512 + 512]
                                    .rearrange("p (h v) -> p h v", v=64),
                                    OP.add)
                            cDV.inc(inst)
                            assert cDV.n == evac(tb, fh, g)
                    p0b_emit_some(vector, 7)

            @block.tensor
            def _(t):
                for tb in range(ntb):
                    for fh in range(2):
                        for g in range(4):
                            q = (tb * 2 + fh) * 4 + g
                            ps = psBIG[:, (q % 4) * 512:(q % 4) * 512 + 512]
                            if q >= 4:
                                pq = q - 4
                                t.wait_ge(cDV.sem, evac(pq // 8, (pq // 4) % 2, pq % 4))
                            for dc in range(8):
                                t.wait_ge(cDV.sem, max(cast_cnt(tb, dc // 4),
                                                       cast_cnt(tb, 2 + fh * 2 + dc // 4)))
                                if sub in ("K", "Q"):
                                    inst = t.matmul(
                                        ps,
                                        wkA[:, dc * 512 + g * 128:dc * 512 + g * 128 + 128],
                                        wkB[:, dc * 512:dc * 512 + 512],
                                        start=(dc == 0), stop=(dc == 7))
                                else:
                                    inst = t.matmul(
                                        ps,
                                        wkB[:, dc * 512 + g * 128:dc * 512 + g * 128 + 128],
                                        wkA[:, dc * 512:dc * 512 + 512],
                                        start=(dc == 0), stop=(dc == 7))
                                if dc == 7:
                                    cPE.inc(inst)
                            assert cPE.n == pe_grp(tb, fh, g)

            qLD.n = ld0 + ntb * 6 * 16
            assert cDV.n == dv0 + ntb * 14
            assert cPE.n == pe0 + ntb * 8

        emit_proj("K", xT, 8, 1024)
        emit_proj("V", xT, 8, 2048)
        emit_proj("Q", xTq, 4, 0)

        @block.vector
        def _(vector):
            p0b_emit_some(vector, NW)
        assert p0b_state["i"] == NW
        assert cP0.n == NW * 2
        PE_PROJ = cPE.n
        DV_PROJ = cDV.n

        @block.tensor
        def _(t):
            t.wait_ge(cP0.sem, NW * 2)
            t.matmul(psC[0:1, 0:1], wsm[:, 96:97], wsm[:, ST + 5:ST + 6],
                     start=True, stop=True)
            inst = t.matmul(psC[0:1, 512:513], wsm[:, 96:97], wsm[:, ST + 6:ST + 7],
                            start=True, stop=True)
            cPE.inc(inst)
        PE_W0 = cPE.n

        @block.vector
        def _(vector):
            vector.wait_ge(cPE.sem, PE_W0)
            for inst in [
                vector.tensor_copy(wsm[0:1, ST:ST + 1], psC[0:1, 0:1]),
                vector.tensor_copy(wsm[0:1, ST + 1:ST + 2], psC[0:1, 512:513]),
                vector.tensor_scalar(wsm[0:1, ST:ST + 1], wsm[0:1, ST:ST + 1],
                                     1.0 / (D * H), EPS_Q, OP.mult, OP.max),
                vector.tensor_scalar(wsm[0:1, ST + 1:ST + 2], wsm[0:1, ST + 1:ST + 2],
                                     1.0 / (D * H), EPS_Q, OP.mult, OP.max),
                vector.tensor_scalar_mul(wsm[0:1, ST + 2:ST + 3], wsm[0:1, ST:ST + 1], 0.5),
                vector.tensor_scalar_mul(wsm[0:1, ST + 3:ST + 4], wsm[0:1, ST:ST + 1], -0.5),
                vector.tensor_scalar_mul(wsm[0:1, ST + 4:ST + 5], wsm[0:1, ST + 1:ST + 2], 0.5),
                vector.tensor_scalar_mul(wsm[0:1, ST + 5:ST + 6], wsm[0:1, ST + 1:ST + 2], -0.5),
                vector.tensor_scalar_mul(wsm[0:1, ST + 6:ST + 7], wsm[0:1, ST:ST + 1], 1.0 / 127.0),
                vector.tensor_scalar_mul(wsm[0:1, ST + 7:ST + 8], wsm[0:1, ST + 1:ST + 2],
                                         1.0 / 127.0),
            ]:
                cDV.inc(inst)
                vector.wait_ge(cDV.sem, cDV.n)
        DV_WS = cDV.n

        @block.gpsimd
        def _(g):
            g.wait_ge(cDV.sem, DV_WS)
            inst = g.dma_start(out=sscr.rearrange("(a n) -> a n", a=1), in_=wsm[0:1, ST:ST + 8])
            qLG.inc(inst)
            g.wait_ge(qLG.sem, qLG.n)
            inst = g.dma_start(out=wsm[:, WS:WS + 8],
                               in_=sscr.rearrange("(a n) -> a n", a=1).partition_broadcast(128))
            qLG.inc(inst)
        LG_WS = qLG.n

        pe0 = cPE.n
        ac0 = cAC.n
        dv0 = cDV.n
        lg0 = qLG.n
        steps = [(h, qc) for h in range(NH) for qc in range(2)]
        NSTEP = len(steps)

        def st_inc(gkt):
            return pe0 + (gkt + 1 if gkt < 2 else 2 * gkt)

        def pv_inc(gkt):
            return pe0 + 2 * gkt + 3 if gkt < NSTEP * NKT - 1 else pe0 + 2 * gkt + 2

        def exp_inc(gkt):
            return ac0 + gkt + 1

        def r_inc(m):
            return dv0 + m * 2 + 1

        def tt_inc(m):
            return dv0 + m * 2 + 2

        def lg_back(m):
            return lg0 + (m * 3 + 2) * 16

        def lg_move(m):
            return lg0 + (m * 3 + 3) * 16

        GKT = NSTEP * NKT

        def emit_S(t, gkt):
            m, kt = divmod(gkt, NKT)
            h, qc = steps[m]
            ch, rh = h // 2, (h % 2) * 64
            stb = (gkt % 2) * 1024
            lhsK = karena[rh:rh + 64, ch * 4096 + kt * 128:ch * 4096 + kt * 128 + 128]
            for half in range(2):
                inst = t.matmul(psBIG[:, stb + half * 512:stb + half * 512 + 512],
                                lhsK,
                                qarena[rh:rh + 64,
                                       ch * 2048 + qc * 1024 + half * 512:
                                       ch * 2048 + qc * 1024 + half * 512 + 512],
                                start=True, stop=True)
            cPE.inc(inst)
            assert cPE.n == st_inc(gkt)

        def emit_PV(t, gkt):
            m, kt = divmod(gkt, NKT)
            h, qc = steps[m]
            t.wait_ge(cAC.sem, exp_inc(gkt))
            if kt == 0 and m >= 1:
                t.wait_ge(cDV.sem, tt_inc(m - 1))
            lhsV = varena[:, kt * VTILE + h * VSTRIDE:kt * VTILE + h * VSTRIDE + VSTRIDE]
            pt = wkB[:, (gkt % 2) * 1024:(gkt % 2) * 1024 + 1024]
            for half in range(2):
                inst = t.matmul(psC[0:65, half * 512:half * 512 + 512],
                                lhsV, pt[:, half * 512:half * 512 + 512],
                                start=(kt == 0), stop=(kt == NKT - 1))
            cPE.inc(inst)
            assert cPE.n == pv_inc(gkt)

        @block.tensor
        def _(t):
            t.wait_ge(cDV.sem, DV_WS)
            t.wait_ge(cDV.sem, DV_PROJ)
            emit_S(t, 0)
            emit_S(t, 1)
            for gkt in range(GKT - 2):
                emit_PV(t, gkt)
                emit_S(t, gkt + 2)
            emit_PV(t, GKT - 2)
            emit_PV(t, GKT - 1)

        @block.scalar
        def _(scalar):
            for m, (h, qc) in enumerate(steps):
                for kt in range(NKT):
                    gkt = m * NKT + kt
                    stb = (gkt % 2) * 1024
                    scalar.wait_ge(cPE.sem, st_inc(gkt))
                    if gkt >= 2:
                        scalar.wait_ge(cPE.sem, pv_inc(gkt - 2))
                    inst = scalar.activation(wkB[:, (gkt % 2) * 1024:(gkt % 2) * 1024 + 1024],
                                             psBIG[:, stb:stb + 1024], AF.Exp,
                                             bias=wsm[:, NEGM:NEGM + 1], scale=0.125)
                    cAC.inc(inst)
                    assert cAC.n == exp_inc(gkt)

        @block.vector
        def _(vector):
            for m, (h, qc) in enumerate(steps):
                vector.wait_ge(cPE.sem, pv_inc(m * NKT + NKT - 1))
                inst = vector.reciprocal(wf[64:65, 3072:4096], psC[64:65, 0:1024])
                cDV.inc(inst)
                assert cDV.n == r_inc(m)
                vector.wait_ge(qLG.sem, lg_back(m))
                if m >= 1:
                    vector.wait_ge(qLG.sem, lg_move(m - 1))
                inst = vector.tensor_tensor(wkB[0:64, 2048:3072], psC[0:64, 0:1024],
                                            wf[0:64, 3072:4096], OP.mult)
                cDV.inc(inst)
                assert cDV.n == tt_inc(m)

        @block.gpsimd
        def _(g):
            for m, (h, qc) in enumerate(steps):
                ch, rh = h // 2, (h % 2) * 64
                g.wait_ge(cDV.sem, r_inc(m))
                inst = g.dma_start(out=dscr[:], in_=wf[64:65, 3072:4096])
                qLG.inc(inst)
                g.wait_ge(qLG.sem, qLG.n)
                if m >= 1:
                    g.wait_ge(cDV.sem, tt_inc(m - 1))
                inst = g.dma_start(out=wf[0:64, 3072:4096], in_=dscr.partition_broadcast(64))
                qLG.inc(inst)
                g.wait_ge(cDV.sem, tt_inc(m))
                inst = g.dma_start(
                    out=qarena[rh:rh + 64, ch * 2048 + qc * 1024:ch * 2048 + qc * 1024 + 1024],
                    in_=wkB[0:64, 2048:3072])
                qLG.inc(inst)

        assert cPE.n == pe0 + NSTEP * NKT * 2
        assert cAC.n == ac0 + NSTEP * NKT
        assert cDV.n == dv0 + NSTEP * 2
        assert qLG.n == lg0 + NSTEP * 3 * 16
        PE_ATT = cPE.n
        LG_ATT = qLG.n

        if debug:
            @block.sync
            def _(sync):
                sync.wait_ge(cPE.sem, PE_ATT)
                sync.wait_ge(qLG.sem, LG_ATT)
                qLD.inc(sync.dma_start(out=dbg_k[:], in_=karena[:, :]))
                qLD.inc(sync.dma_start(out=dbg_v[:], in_=varena[:, :]))
                qLD.inc(sync.dma_start(out=dbg_av[:], in_=qarena[:, :]))
        DBG_DUMP = qLD.n
        lg0 = qLG.n
        dv0 = cDV.n
        tern_pieces = [(w1t, c, p, 0) for c in range(8) for p in range(4)] + \
                      [(w2t, c, 0, 1) for c in range(32)]
        NT = len(tern_pieces)

        @block.gpsimd
        def _(g):
            g.wait_ge(cPE.sem, PE_ATT)
            if debug:
                g.wait_ge(qLD.sem, DBG_DUMP)
            for i in range(NT):
                src, c, p, w = tern_pieces[i]
                if i >= 1:
                    g.wait_ge(cDV.sem, dv0 + i * 2)
                inst = g.dma_start(
                    out=wf[:, (i % 2) * 1024:(i % 2) * 1024 + 1024],
                    in_=src[c * 128:(c + 1) * 128, p * 1024:(p + 1) * 1024])
                qLG.inc(inst)

        @block.vector
        def _(vector):
            for i in range(NT):
                src, c, p, w = tern_pieces[i]
                vector.wait_ge(qLG.sem, lg0 + (i + 1) * 16)
                stage = wf[:, (i % 2) * 1024:(i % 2) * 1024 + 1024]
                thr = wsm[:, (C_T1 if w == 0 else C_T2):(C_T1 if w == 0 else C_T2) + 1]
                nthr = wsm[:, (C_NT1 if w == 0 else C_NT2):(C_NT1 if w == 0 else C_NT2) + 1]
                inst = vector.tensor_scalar(wf[:, 2048:3072], stage, nthr, None, OP.is_lt)
                cDV.inc(inst)
                if w == 0:
                    dst = karena[:, c * 4096 + p * 1024:c * 4096 + p * 1024 + 1024]
                else:
                    dst = varena[:, c * 1024:(c + 1) * 1024]
                inst = vector.scalar_tensor_tensor(dst, stage, thr, wf[:, 2048:3072],
                                                   OP.is_gt, OP.subtract)
                cDV.inc(inst)
        assert cDV.n == dv0 + NT * 2
        qLG.n = lg0 + NT * 16
        DV_TERN = cDV.n

        lg0 = qLG.n
        dv0 = cDV.n

        @block.gpsimd
        def _(g):
            for c in range(8):
                if c >= 1:
                    g.wait_ge(cDV.sem, dv0 + c)
                inst = g.dma_start(out=wf[:, 0:1024], in_=wo[c * 128:(c + 1) * 128, :])
                qLG.inc(inst)
                g.wait_ge(cDV.sem, dv0 + c + 1)
                inst = g.dma_start(out=wo_bf[c], in_=wkB[:, 0:1024])
                qLG.inc(inst)
            for j in range(8):
                g.wait_ge(cDV.sem, dv0 + 8 + j)
                inst = g.dma_start(out=wf[:, 0:512],
                                   in_=b1[j * 512:(j + 1) * 512]
                                   .rearrange("(a n) -> a n", a=1).partition_broadcast(128))
                qLG.inc(inst)
            g.wait_ge(cDV.sem, dv0 + 16)
            inst = g.dma_start(out=wf[0:1, 0:1024],
                               in_=b2.rearrange("(a n) -> a n", a=1))
            qLG.inc(inst)
            g.wait_ge(cDV.sem, dv0 + 17)
            inst = g.dma_start(out=wf[:, 1024:2048],
                               in_=bo.rearrange("(a n) -> a n", a=1).partition_broadcast(128))
            qLG.inc(inst)

        @block.vector
        def _(vector):
            for c in range(8):
                vector.wait_ge(qLG.sem, lg0 + (2 * c + 1) * 16)
                inst = vector.tensor_copy(wkB[:, 0:1024], wf[:, 0:1024])
                cDV.inc(inst)
            for j in range(8):
                vector.wait_ge(qLG.sem, lg0 + (16 + j + 1) * 16)
                inst = vector.tensor_copy(wkA[:, j * 512:(j + 1) * 512], wf[:, 0:512])
                cDV.inc(inst)
            vector.wait_ge(qLG.sem, lg0 + 25 * 16)
            inst = vector.tensor_copy(b2row[0:1, :], wf[0:1, 0:1024])
            cDV.inc(inst)
        assert cDV.n == dv0 + 17
        qLG.n = lg0 + 26 * 16
        DV_W = cDV.n
        LG_W = qLG.n

        ld0 = qLD.n
        lg0 = qLG.n
        pe0 = cPE.n
        dv0 = cDV.n
        ac0 = cAC.n

        def pA_xdma(t):
            return ld0 + (t * 3 + 1) * 16

        def pA_wo(t, dh):
            return ld0 + (t * 3 + 2 + dh) * 16

        def pA_pe(t, k):
            return pe0 + t * 3 + k

        def pA_dv(t, k):
            return dv0 + t * 40 + k

        def pA_ac(t, k):
            return ac0 + t * 2 + k

        def pA_x1dma(t):
            return lg0 + (t + 1) * 16

        @block.sync
        def _(sync):
            sync.wait_ge(qLG.sem, LG_W)
            for t in range(NTT):
                if t >= 1:
                    sync.wait_ge(cDV.sem, pA_dv(t - 1, 1))
                inst = sync.dma_start(out=wf[:, 0:1024],
                                      in_=x_own[t * 128:(t + 1) * 128, :])
                qLD.inc(inst)
                for dh in range(2):
                    if t * 2 + dh >= 1:
                        pt_, pdh = divmod(t * 2 + dh - 1, 2)
                        sync.wait_ge(cPE.sem, pA_pe(pt_, pdh + 1))
                    inst = sync.dma_start(
                        out=wkB[:, 0:4096].rearrange("p (c x) -> p c x", x=512),
                        in_=wo_bf[:, :, dh * 512:dh * 512 + 512].rearrange("c p x -> p c x"))
                    qLD.inc(inst)

        @block.tensor
        def _(t_):
            t_.wait_ge(qLG.sem, LG_ATT)
            if debug:
                t_.wait_ge(qLD.sem, DBG_DUMP)
            for t in range(NTT):
                ps = psBIG[:, (t % 2) * 1024:(t % 2) * 1024 + 1024]
                if t >= 2:
                    t_.wait_ge(cDV.sem, pA_dv(t - 2, 2))
                for dh in range(2):
                    t_.wait_ge(qLD.sem, pA_wo(t, dh))
                    for c in range(8):
                        inst = t_.matmul(ps[:, dh * 512:dh * 512 + 512],
                                         qarena[:, c * 2048 + t * 128:c * 2048 + t * 128 + 128],
                                         wkB[:, c * 512:c * 512 + 512],
                                         start=(c == 0), stop=(c == 7))
                    cPE.inc(inst)
                    assert cPE.n == pA_pe(t, dh + 1)
                t_.wait_ge(cDV.sem, pA_dv(t, 38))
                if t >= 1:
                    t_.wait_ge(cDV.sem, pA_dv(t - 1, 40))
                for dc in range(8):
                    dst = psTa if dc < 4 else psTb
                    inst = t_.transpose(dst[:, (dc % 4) * 128:(dc % 4) * 128 + 128],
                                        wkB[:, 4096 + dc * 128:4096 + dc * 128 + 128],
                                        ident[:])
                cPE.inc(inst)
                assert cPE.n == pA_pe(t, 3)

        @block.vector
        def _(vector):
            for t in range(NTT):
                k = 0
                vector.wait_ge(qLD.sem, pA_xdma(t))
                inst = vector.tensor_add(wf[:, 2048:3072], wf[:, 0:1024], wf[:, 1024:2048])
                k += 1; cDV.inc(inst)
                vector.wait_ge(cPE.sem, pA_pe(t, 2))
                ps = psBIG[:, (t % 2) * 1024:(t % 2) * 1024 + 1024]
                inst = vector.tensor_add(wf[:, 2048:3072], ps, wf[:, 2048:3072])
                k += 1; cDV.inc(inst)
                inst = vector.reduce_sum(wsm[:, ST:ST + 1], wf[:, 2048:3072], AX.X)
                k += 1; cDV.inc(inst); vector.wait_ge(cDV.sem, cDV.n)
                inst = vector.tensor_scalar_mul(wsm[:, ST + 1:ST + 2], wsm[:, ST:ST + 1], -1.0 / D)
                k += 1; cDV.inc(inst); vector.wait_ge(cDV.sem, cDV.n)
                if t >= 1:
                    vector.wait_ge(qLG.sem, pA_x1dma(t - 1))
                inst = vector.tensor_scalar(wf[:, 3072:4096], wf[:, 2048:3072],
                                            wsm[:, ST + 1:ST + 2], None, OP.add)
                k += 1; cDV.inc(inst)
                vector.wait_ge(cAC.sem, pA_ac(t, 1))
                inst = vector.reduce_sum(wsm[:, ST + 2:ST + 3], wf[:, 4096:5120], AX.X)
                k += 1; cDV.inc(inst); vector.wait_ge(cDV.sem, cDV.n)
                inst = vector.tensor_scalar(wsm[:, ST + 3:ST + 4], wsm[:, ST + 2:ST + 3],
                                            1.0 / D, EPS_LN, OP.mult, OP.add)
                k += 1; cDV.inc(inst)
                vector.wait_ge(cAC.sem, pA_ac(t, 2))
                inst = vector.reciprocal(wsm[:, ST + 5:ST + 6], wsm[:, ST + 4:ST + 5])
                k += 1; cDV.inc(inst); vector.wait_ge(cDV.sem, cDV.n)
                inst = vector.tensor_scalar(wf[:, 3072:4096], wf[:, 3072:4096],
                                            wsm[:, ST + 5:ST + 6], None, OP.mult)
                k += 1; cDV.inc(inst)
                inst = vector.reduce_max(wsm[:, ST + 6:ST + 7], wf[:, 3072:4096], AX.X,
                                         apply_absolute_value=True)
                k += 1; cDV.inc(inst); vector.wait_ge(cDV.sem, cDV.n)
                inst = vector.tensor_scalar_max(wsm[:, ST + 6:ST + 7], wsm[:, ST + 6:ST + 7], EPS_Q)
                k += 1; cDV.inc(inst); vector.wait_ge(cDV.sem, cDV.n)
                inst = vector.reciprocal(wsm[:, ST + 7:ST + 8], wsm[:, ST + 6:ST + 7])
                k += 1; cDV.inc(inst); vector.wait_ge(cDV.sem, cDV.n)
                inst = vector.tensor_scalar_mul(wsm[:, ST + 7:ST + 8], wsm[:, ST + 7:ST + 8], 127.0)
                k += 1; cDV.inc(inst); vector.wait_ge(cDV.sem, cDV.n)
                inst = vector.tensor_tensor(wsm[:, S1C + t:S1C + t + 1],
                                            wsm[:, ST + 6:ST + 7],
                                            wsm[:, C_WQ1:C_WQ1 + 1], OP.mult)
                k += 1; cDV.inc(inst)
                for qch in range(2):
                    k += emit_quant_chunk(vector, cDV, wf, wri,
                                          wsm[:, ST + 7:ST + 8],
                                          wf[:, 3072 + qch * 512:3072 + qch * 512 + 512],
                                          wkB[:, 4096 + qch * 512:4096 + qch * 512 + 512])
                assert k == 38
                vector.wait_ge(cPE.sem, pA_pe(t, 3))
                inst = vector.tensor_copy(
                    qarena.ap().rearrange("p (c n) -> p c n", n=2048)[:, 0:4, t * 128:t * 128 + 128],
                    psTa.ap().rearrange("p (c n) -> p c n", n=128))
                k += 1; cDV.inc(inst)
                inst = vector.tensor_copy(
                    qarena.ap().rearrange("p (c n) -> p c n", n=2048)[:, 4:8, t * 128:t * 128 + 128],
                    psTb.ap().rearrange("p (c n) -> p c n", n=128))
                k += 1; cDV.inc(inst)
                assert cDV.n == pA_dv(t, 40)

        @block.scalar
        def _(scalar):
            for t in range(NTT):
                scalar.wait_ge(cDV.sem, pA_dv(t, 5))
                inst = scalar.activation(wf[:, 4096:5120], wf[:, 3072:4096], AF.Square)
                cAC.inc(inst)
                scalar.wait_ge(cDV.sem, pA_dv(t, 7))
                inst = scalar.activation(wsm[:, ST + 4:ST + 5], wsm[:, ST + 3:ST + 4], AF.Sqrt)
                cAC.inc(inst)

        @block.gpsimd
        def _(g):
            for t in range(NTT):
                g.wait_ge(cDV.sem, pA_dv(t, 38))
                inst = g.dma_start(out=x1scr[t * 128:(t + 1) * 128, :], in_=wf[:, 3072:4096])
                qLG.inc(inst)

        qLD.n = ld0 + NTT * 3 * 16
        assert cPE.n == pe0 + NTT * 3
        assert cDV.n == dv0 + NTT * 40
        assert cAC.n == ac0 + NTT * 2
        X1DMA = [pA_x1dma(t) for t in range(NTT)]
        qLG.n = lg0 + NTT * 16
        DV_A = cDV.n

        ld0 = qLD.n
        pe0 = cPE.n
        dv0 = cDV.n
        ac0 = cAC.n

        lgB0 = qLG.n

        def pB_pe(t, k):
            return pe0 + t * 25 + k

        def pB_lg(t, k):
            return lgB0 + (t * 2 + k) * 16

        def pB_dv(t, k):
            return dv0 + t * 127 + k

        def pB_ac(t, k):
            return ac0 + t * 10 + k

        def pB_x1(t):
            return ld0 + (t + 1) * 16

        def pB_y(t):
            return (t + 1) * 16

        @block.sync
        def _(sync):
            for t in range(NTT):
                sync.wait_ge(cDV.sem, pB_dv(t, 110))
                sync.wait_ge(qLG.sem, X1DMA[t])
                inst = sync.dma_start(out=wf[:, 4096:5120], in_=x1scr[t * 128:(t + 1) * 128, :])
                qLD.inc(inst)

        @block.tensor
        def _(t_):
            t_.wait_ge(cDV.sem, DV_A)
            for t in range(NTT):
                for jh in range(2):
                    if jh == 1:
                        t_.wait_ge(cDV.sem, pB_dv(t, 4))
                    elif t >= 1:
                        t_.wait_ge(cDV.sem, pB_dv(t - 1, 8))
                    for dc in range(8):
                        lhs = qarena[:, dc * 2048 + t * 128:dc * 2048 + t * 128 + 128]
                        for sl in range(4):
                            inst = t_.matmul(
                                psBIG[:, sl * 512:sl * 512 + 512], lhs,
                                karena[:, dc * 4096 + jh * 2048 + sl * 512:
                                       dc * 4096 + jh * 2048 + sl * 512 + 512],
                                start=(dc == 0), stop=(dc == 7))
                            if dc == 7:
                                cPE.inc(inst)
                    assert cPE.n == pB_pe(t, jh * 4 + 4)
                t_.wait_ge(cDV.sem, pB_dv(t, 110))
                if t >= 1:
                    t_.wait_ge(cDV.sem, pB_dv(t - 1, 120))

                def _tr(grp):
                    if grp >= 2:
                        t_.wait_ge(cDV.sem, pB_dv(t, 112 + grp - 2))
                    elif t >= 1:
                        t_.wait_ge(cDV.sem, pB_dv(t - 1, 118 + grp))
                    dstp = psTa if grp % 2 == 0 else psTb
                    for i in range(4):
                        hc = grp * 4 + i
                        inst2 = t_.transpose(
                            dstp[:, i * 128:i * 128 + 128],
                            wkB[:, hc * 128:hc * 128 + 128], ident[:])
                    cPE.inc(inst2)

                def _mm(grp):
                    t_.wait_ge(cDV.sem, pB_dv(t, 112 + grp))
                    for i in range(4):
                        hc = grp * 4 + i
                        lhs = wkB[:, 4096 + (grp % 2) * 512 + i * 128:
                                  4096 + (grp % 2) * 512 + i * 128 + 128]
                        for half in range(2):
                            inst2 = t_.matmul(psC[:, half * 512:half * 512 + 512], lhs,
                                              varena[:, hc * 1024 + half * 512:
                                                     hc * 1024 + half * 512 + 512],
                                              start=(hc == 0), stop=False)
                    cPE.inc(inst2)

                _tr(0)
                _tr(1)
                for grp in range(2, 8):
                    _mm(grp - 2)
                    _tr(grp)
                _mm(6)
                _mm(7)
                t_.wait_ge(cDV.sem, pB_dv(t, 111))
                for half in range(2):
                    inst = t_.matmul(psC[:, half * 512:half * 512 + 512],
                                     s2invrow[0:1, :],
                                     b2row[0:1, half * 512:half * 512 + 512],
                                     start=False, stop=True)
                cPE.inc(inst)
                assert cPE.n == pB_pe(t, 25)

        @block.vector
        def _(vector):
            for t in range(NTT):
                k = 0
                for jh in range(2):
                    for sl in range(4):
                        gidx = jh * 4 + sl
                        vector.wait_ge(cPE.sem, pB_pe(t, gidx + 1))
                        if gidx >= 2:
                            vector.wait_ge(cAC.sem, pB_ac(t, gidx - 1))
                        elif t >= 1:
                            vector.wait_ge(cAC.sem, pB_ac(t - 1, 7 + gidx))
                        inst = vector.scalar_tensor_tensor(
                            wf[:, 4096 + (gidx % 2) * 512:4096 + (gidx % 2) * 512 + 512],
                            psBIG[:, sl * 512:sl * 512 + 512],
                            wsm[:, S1C + t:S1C + t + 1],
                            wkA[:, jh * 2048 + sl * 512:jh * 2048 + sl * 512 + 512],
                            OP.mult, OP.add)
                        k += 1; cDV.inc(inst)
                vector.wait_ge(cAC.sem, pB_ac(t, 8))
                inst = vector.reduce_max(wsm[:, ST:ST + 1], wf[:, 0:4096], AX.X,
                                         apply_absolute_value=True)
                k += 1; cDV.inc(inst); vector.wait_ge(cDV.sem, cDV.n)
                inst = vector.tensor_scalar_max(wsm[:, ST:ST + 1], wsm[:, ST:ST + 1], EPS_Q)
                k += 1; cDV.inc(inst); vector.wait_ge(cDV.sem, cDV.n)
                inst = vector.reciprocal(wsm[:, ST + 1:ST + 2], wsm[:, ST:ST + 1])
                k += 1; cDV.inc(inst); vector.wait_ge(cDV.sem, cDV.n)
                inst = vector.tensor_scalar_mul(wsm[:, ST + 1:ST + 2], wsm[:, ST + 1:ST + 2],
                                                127.0)
                k += 1; cDV.inc(inst); vector.wait_ge(cDV.sem, cDV.n)
                inst = vector.tensor_tensor(wsm[:, ST + 2:ST + 3], wsm[:, ST:ST + 1],
                                            wsm[:, C_WQ2:C_WQ2 + 1], OP.mult)
                k += 1; cDV.inc(inst); vector.wait_ge(cDV.sem, cDV.n)
                inst = vector.reciprocal(wsm[:, ST + 3:ST + 4], wsm[:, ST + 2:ST + 3])
                k += 1; cDV.inc(inst)
                assert k == 14
                for qch in range(8):
                    k += emit_quant_chunk(vector, cDV, wf, wri,
                                          wsm[:, ST + 1:ST + 2],
                                          wf[:, qch * 512:qch * 512 + 512],
                                          wkB[:, qch * 512:qch * 512 + 512])
                assert k == 110
                vector.wait_ge(qLG.sem, pB_lg(t, 2))
                inst = vector.tensor_copy(s2invrow[0:1, :], wsm[0:1, 64:192])
                k += 1; cDV.inc(inst)
                for grp in range(8):
                    tri = 9 + grp if grp < 2 else 8 + 2 * grp
                    vector.wait_ge(cPE.sem, pB_pe(t, tri))
                    if grp >= 2:
                        mmi = 11 + 2 * (grp - 2) if grp - 2 <= 5 else 17 + (grp - 2)
                        vector.wait_ge(cPE.sem, pB_pe(t, mmi))
                    elif t >= 1:
                        pmi = 17 + (grp + 6)
                        vector.wait_ge(cPE.sem, pB_pe(t - 1, pmi))
                    srcp = psTa if grp % 2 == 0 else psTb
                    inst = vector.tensor_copy(
                        wkB[:, 4096 + (grp % 2) * 512:4096 + (grp % 2) * 512 + 512],
                        srcp[:, 0:512])
                    k += 1; cDV.inc(inst)
                vector.wait_ge(qLD.sem, pB_x1(t))
                vector.wait_ge(cPE.sem, pB_pe(t, 25))
                if t >= 1:
                    vector.wait_ge(qOUT.sem, pB_y(t - 1))
                inst = vector.scalar_tensor_tensor(wf[:, 0:1024], psC[:, 0:1024],
                                                   wsm[:, ST + 2:ST + 3],
                                                   wf[:, 4096:5120], OP.mult, OP.add)
                k += 1; cDV.inc(inst)
                inst = vector.reduce_sum(wsm[:, ST + 4:ST + 5], wf[:, 0:1024], AX.X)
                k += 1; cDV.inc(inst); vector.wait_ge(cDV.sem, cDV.n)
                inst = vector.tensor_scalar_mul(wsm[:, ST + 5:ST + 6], wsm[:, ST + 4:ST + 5],
                                                -1.0 / D)
                k += 1; cDV.inc(inst); vector.wait_ge(cDV.sem, cDV.n)
                inst = vector.tensor_scalar(wf[:, 0:1024], wf[:, 0:1024],
                                            wsm[:, ST + 5:ST + 6], None, OP.add)
                k += 1; cDV.inc(inst)
                vector.wait_ge(cAC.sem, pB_ac(t, 9))
                inst = vector.reduce_sum(wsm[:, ST + 7:ST + 8], wf[:, 1024:2048], AX.X)
                k += 1; cDV.inc(inst); vector.wait_ge(cDV.sem, cDV.n)
                inst = vector.tensor_scalar(wsm[:, ST + 6:ST + 7], wsm[:, ST + 7:ST + 8],
                                            1.0 / D, EPS_LN, OP.mult, OP.add)
                k += 1; cDV.inc(inst)
                vector.wait_ge(cAC.sem, pB_ac(t, 10))
                inst = vector.reciprocal(wsm[:, ST + 5:ST + 6], wsm[:, ST + 7:ST + 8])
                k += 1; cDV.inc(inst); vector.wait_ge(cDV.sem, cDV.n)
                inst = vector.tensor_scalar(wf[:, 0:1024], wf[:, 0:1024],
                                            wsm[:, ST + 5:ST + 6], None, OP.mult)
                k += 1; cDV.inc(inst)
                assert k == 127
                assert cDV.n == pB_dv(t, 127)

        @block.scalar
        def _(scalar):
            for t in range(NTT):
                for jh in range(2):
                    for sl in range(4):
                        gidx = jh * 4 + sl
                        scalar.wait_ge(cDV.sem, pB_dv(t, gidx + 1))
                        if gidx < 2 and t >= 1:
                            scalar.wait_ge(qOUT.sem, pB_y(t - 1))
                        inst = scalar.activation(
                            wf[:, gidx * 512:gidx * 512 + 512],
                            wf[:, 4096 + (gidx % 2) * 512:4096 + (gidx % 2) * 512 + 512],
                            AF.Gelu)
                        cAC.inc(inst)
                scalar.wait_ge(cDV.sem, pB_dv(t, 123))
                inst = scalar.activation(wf[:, 1024:2048], wf[:, 0:1024], AF.Square)
                cAC.inc(inst)
                scalar.wait_ge(cDV.sem, pB_dv(t, 125))
                inst = scalar.activation(wsm[:, ST + 7:ST + 8], wsm[:, ST + 6:ST + 7], AF.Sqrt)
                cAC.inc(inst)
                assert cAC.n == pB_ac(t, 10)

        if debug:
            @block.vector
            def _(vector):
                vector.wait_ge(qOUT.sem, 0)
        @block.gpsimd
        def _(g):
            for t in range(NTT):
                g.wait_ge(cDV.sem, pB_dv(t, 14))
                if t >= 1:
                    g.wait_ge(cDV.sem, pB_dv(t - 1, 111))
                inst = g.dma_start(out=rscr.rearrange("(p a) -> p a", a=1),
                                   in_=wsm[:, ST + 3:ST + 4])
                qLG.inc(inst)
                g.wait_ge(qLG.sem, pB_lg(t, 1))
                inst = g.dma_start(out=wsm[0:1, 64:192],
                                   in_=rscr.rearrange("(a n) -> a n", a=1))
                qLG.inc(inst)
                g.wait_ge(cDV.sem, pB_dv(t, 127))
                inst = g.dma_start(out=y[t * 128:(t + 1) * 128, :], in_=wf[:, 0:1024])
                qOUT.inc(inst)
            if debug:
                qOUT.inc(g.dma_start(out=dbg_wsm[:], in_=wsm[:, :]))
                qOUT.inc(g.dma_start(out=dbg_x1[:], in_=x1scr[:]))
                qOUT.inc(g.dma_start(out=dbg_wf[:], in_=wf[:, :]))
                qOUT.inc(g.dma_start(out=dbg_wo[:], in_=wo_bf[:]))
                qOUT.inc(g.dma_start(out=dbg_q[:], in_=qarena[:, :]))
            g.wait_ge(qOUT.sem, qOUT.n)

        qLD.n = ld0 + NTT * 16
        assert cPE.n == pe0 + NTT * 25
        assert cDV.n == dv0 + NTT * 127
        assert cAC.n == ac0 + NTT * 10
        assert qLG.n == lgB0 + NTT * 2 * 16

    return nc


_CACHE = {}


def kernel(L0_attn_x, L0_attn_W_qkv, L0_attn_b_qkv, L0_attn_W_o, L0_attn_b_o,
           L0_ffn_W1, L0_ffn_b1, L0_ffn_W2, L0_ffn_b2):
    import ml_dtypes
    x = np.ascontiguousarray(np.asarray(L0_attn_x, dtype=np.float32))
    wqkv_ = np.ascontiguousarray(np.asarray(L0_attn_W_qkv, dtype=np.float32))
    bqkv_ = np.ascontiguousarray(np.asarray(L0_attn_b_qkv, dtype=np.float32))
    wo_ = np.ascontiguousarray(np.asarray(L0_attn_W_o, dtype=np.float32))
    bo_ = np.ascontiguousarray(np.asarray(L0_attn_b_o, dtype=np.float32))
    w1t_ = np.ascontiguousarray(np.asarray(L0_ffn_W1, dtype=np.float32).T)
    b1_ = np.ascontiguousarray(np.asarray(L0_ffn_b1, dtype=np.float32))
    w2t_ = np.ascontiguousarray(np.asarray(L0_ffn_W2, dtype=np.float32).T)
    b2_ = np.ascontiguousarray(np.asarray(L0_ffn_b2, dtype=np.float32))
    ident_ = np.eye(128, dtype=ml_dtypes.bfloat16)

    if "nc" not in _CACHE:
        _CACHE["nc"] = build_nc()
    nc = _CACHE["nc"]

    in_maps = []
    for c in range(8):
        b, half = c // 2, c % 2
        xT_ = np.ascontiguousarray(x[b].T)
        in_maps.append({
            "xT": xT_,
            "xTq": np.ascontiguousarray(xT_[:, half * QO:(half + 1) * QO]),
            "x_own": np.ascontiguousarray(x[b, half * QO:(half + 1) * QO]),
            "wqkv": wqkv_, "bqkv": bqkv_, "wo": wo_, "bo": bo_,
            "w1t": w1t_, "b1": b1_, "w2t": w2t_, "b2": b2_,
            "ident_in": ident_,
        })
    res = run_bass_kernel_spmd(nc, in_maps, core_ids=list(range(8)))
    out = np.empty((B, S, D), dtype=np.float32)
    for c in range(8):
        b, half = c // 2, c % 2
        out[b, half * QO:(half + 1) * QO] = res.results[c]["y"]
    return out


# version 15
# speedup vs baseline: 1.4934x; 1.0906x over previous
import numpy as np
from contextlib import ExitStack

import concourse.bass as bass
import concourse.mybir as mybir
from concourse.bass_utils import run_bass_kernel_spmd
from concourse.alu_op_type import AluOpType as OP

F32 = mybir.dt.float32
BF16 = mybir.dt.bfloat16
I32 = mybir.dt.int32
AF = mybir.ActivationFunctionType
AX = mybir.AxisListType

B, S, D = 4, 4096, 1024
NH, HS = 16, 64
H = 4096
QO = 2048
NKT = S // 128
NTT = QO // 128
EPS_LN = 1e-5
EPS_Q = 1e-5
VSTRIDE = HS + 1
VTILE = NH * VSTRIDE

NEGM = 0
BQ = 16
BK = 24
S1C = 32
WS = 48
ST = 56
MQ2 = 64
C_T1, C_NT1, C_T2, C_NT2 = WS + 2, WS + 3, WS + 4, WS + 5
C_WQ1, C_WQ2 = WS + 6, WS + 7


QCH_OPS = 12


def emit_quant_chunk(vector, cDV, wf, wri, shm_ap, src_ap, dst_ap):
    from concourse.alu_op_type import AluOpType as OP
    qa = wf[:, 4096:4608]
    qb = wf[:, 4608:5120]
    qc = wf[:, 5120:5632]
    n = 0
    inst = vector.tensor_scalar(qa, src_ap, shm_ap, 256.0, OP.mult, OP.add)
    n += 1; cDV.inc(inst)
    inst = vector.tensor_copy(wri[:, :], qa)
    n += 1; cDV.inc(inst)
    inst = vector.tensor_copy(qb, wri[:, :])
    n += 1; cDV.inc(inst)
    inst = vector.tensor_sub(qa, qa, qb)
    n += 1; cDV.inc(inst)
    inst = vector.tensor_scalar(qc, qa, -0.5, None, OP.is_lt)
    n += 1; cDV.inc(inst)
    inst = vector.scalar_tensor_tensor(qc, qa, -1.5, qc, OP.is_lt, OP.add)
    n += 1; cDV.inc(inst)
    inst = vector.scalar_tensor_tensor(qc, qa, -2.5, qc, OP.is_lt, OP.add)
    n += 1; cDV.inc(inst)
    inst = vector.scalar_tensor_tensor(qb, qc, -1.0, qb, OP.mult, OP.add)
    n += 1; cDV.inc(inst)
    inst = vector.tensor_scalar(qc, qa, 0.5, None, OP.is_ge)
    n += 1; cDV.inc(inst)
    inst = vector.scalar_tensor_tensor(qc, qa, 1.5, qc, OP.is_ge, OP.add)
    n += 1; cDV.inc(inst)
    inst = vector.scalar_tensor_tensor(qc, qa, 2.5, qc, OP.is_ge, OP.add)
    n += 1; cDV.inc(inst)
    inst = vector.scalar_tensor_tensor(dst_ap, qb, -256.0, qc, OP.add, OP.add)
    n += 1; cDV.inc(inst)
    assert n == QCH_OPS
    return n


class Ctr:
    def __init__(self, sem, step=1):
        self.sem, self.step, self.n = sem, step, 0

    def inc(self, inst):
        inst.then_inc(self.sem, self.step)
        self.n += self.step
        return self.n


def build_nc(debug=False):
    nc = bass.Bass("TRN2", target_bir_lowering=False, debug=False,
                   detect_race_conditions=False)

    xT = nc.dram_tensor("xT", [D, S], F32, kind="ExternalInput").ap()
    xTq = nc.dram_tensor("xTq", [D, QO], F32, kind="ExternalInput").ap()
    x_own = nc.dram_tensor("x_own", [QO, D], F32, kind="ExternalInput").ap()
    wqkv = nc.dram_tensor("wqkv", [D, 3 * D], F32, kind="ExternalInput").ap()
    bqkv = nc.dram_tensor("bqkv", [3 * D], F32, kind="ExternalInput").ap()
    wo = nc.dram_tensor("wo", [D, D], F32, kind="ExternalInput").ap()
    bo = nc.dram_tensor("bo", [D], F32, kind="ExternalInput").ap()
    w1t = nc.dram_tensor("w1t", [D, H], F32, kind="ExternalInput").ap()
    b1 = nc.dram_tensor("b1", [H], F32, kind="ExternalInput").ap()
    w2t = nc.dram_tensor("w2t", [H, D], F32, kind="ExternalInput").ap()
    b2 = nc.dram_tensor("b2", [D], F32, kind="ExternalInput").ap()
    ident_in = nc.dram_tensor("ident_in", [128, 128], BF16, kind="ExternalInput").ap()
    y = nc.dram_tensor("y", [QO, D], F32, kind="ExternalOutput").ap()
    if debug:
        dbg_wsm = nc.dram_tensor("dbg_wsm", [128, 192], F32, kind="ExternalOutput").ap()
        dbg_x1 = nc.dram_tensor("dbg_x1", [QO, D], F32, kind="ExternalOutput").ap()
        dbg_wf = nc.dram_tensor("dbg_wf", [128, 5632], F32, kind="ExternalOutput").ap()
        dbg_wo = nc.dram_tensor("dbg_wo", [8, 128, D], BF16, kind="ExternalOutput").ap()
        dbg_q = nc.dram_tensor("dbg_q", [128, 8 * QO], BF16, kind="ExternalOutput").ap()
        dbg_k = nc.dram_tensor("dbg_k", [128, 8 * 4096], BF16, kind="ExternalOutput").ap()
        dbg_v = nc.dram_tensor("dbg_v", [128, NKT * VTILE], BF16, kind="ExternalOutput").ap()
        dbg_av = nc.dram_tensor("dbg_av", [128, 8 * QO], BF16, kind="ExternalOutput").ap()

    dscr = nc.dram_tensor("dscr", [1, 1024], F32, kind="Internal").ap()
    mscr = nc.dram_tensor("mscr", [16], F32, kind="Internal").ap()
    sscr = nc.dram_tensor("sscr", [8], F32, kind="Internal").ap()
    rscr = nc.dram_tensor("rscr", [128], F32, kind="Internal").ap()
    wo_bf = nc.dram_tensor("wo_bf", [8, 128, D], BF16, kind="Internal").ap()
    x1scr = nc.dram_tensor("x1scr", [QO, D], F32, kind="Internal").ap()

    with ExitStack() as ctx:
        e = ctx.enter_context
        karena = e(nc.sbuf_tensor("karena", [128, 8 * 4096], BF16))
        varena = e(nc.sbuf_tensor("varena", [128, NKT * VTILE], BF16))
        qarena = e(nc.sbuf_tensor("qarena", [128, 8 * QO], BF16))
        wkA = e(nc.sbuf_tensor("wkA", [128, 4096], BF16))
        wkB = e(nc.sbuf_tensor("wkB", [128, 5632], BF16))
        wf = e(nc.sbuf_tensor("wf", [128, 5632], F32))
        wsm = e(nc.sbuf_tensor("wsm", [128, 192], F32))
        wri = e(nc.sbuf_tensor("wri", [128, 512], I32))
        ident = e(nc.sbuf_tensor("ident", [128, 128], BF16))
        wsb = e(nc.sbuf_tensor("wsb", [128, 32], BF16))
        b2row = e(nc.sbuf_tensor("b2row", [1, 1024], BF16))
        s2invrow = e(nc.sbuf_tensor("s2invrow", [1, 128], BF16))

        psBIG = e(nc.psum_tensor("psBIG", [128, 2048], F32))
        psC = e(nc.psum_tensor("psC", [128, 1024], F32))
        psTa = e(nc.psum_tensor("psTa", [128, 512], BF16))
        psTb = e(nc.psum_tensor("psTb", [128, 512], BF16))

        qLD = Ctr(e(nc.semaphore("qLD")), 16)
        qLG = Ctr(e(nc.semaphore("qLG")), 16)
        qOUT = Ctr(e(nc.semaphore("qOUT")), 16)
        cPE = Ctr(e(nc.semaphore("cPE")))
        cP0 = Ctr(e(nc.semaphore("cP0")))
        cTN = Ctr(e(nc.semaphore("cTN")))
        cDV = Ctr(e(nc.semaphore("cDV")))
        cAC = Ctr(e(nc.semaphore("cAC")))

        block = e(nc.Block())

        @block.sync
        def _(sync):
            qLD.inc(sync.dma_start(out=ident[:], in_=ident_in[:]))
            for fc in range(8):
                qLD.inc(sync.dma_start(
                    out=wsm[:, BQ + fc:BQ + fc + 1],
                    in_=bqkv[fc * 128:(fc + 1) * 128].rearrange("(p a) -> p a", a=1)))
                qLD.inc(sync.dma_start(
                    out=wsm[:, BK + fc:BK + fc + 1],
                    in_=bqkv[1024 + fc * 128:1024 + (fc + 1) * 128].rearrange("(p a) -> p a", a=1)))
            inst = sync.dma_start(out=wf[:, 4096:5120],
                                  in_=bqkv[2048:3072].rearrange("(a n) -> a n", a=1)
                                  .partition_broadcast(128))
            qLD.inc(inst)
        LD_CONST = qLD.n

        @block.vector
        def _(vector):
            for inst in [vector.memset(wsm[:, NEGM:NEGM + 1], -16.0),
                         vector.memset(wsb[:, :], 0.0),
                         vector.memset(wsb[0:64, 0:1], 1.0),
                         vector.memset(wsb[64:128, 1:2], 1.0),
                         vector.memset(wsb[:, 2:3], 1.0),
                         vector.memset(wsm[:, 96:97], 1.0),
                         vector.memset(varena[:, :], 1.0)]:
                cDV.inc(inst)
                vector.wait_ge(cDV.sem, cDV.n)
        DV_CONST = cDV.n

        NW = 128
        lgP0 = qLG.n

        @block.gpsimd
        def _(g):
            for i in range(NW):
                if i >= 1:
                    g.wait_ge(cP0.sem, i * 2)
                if i < 64:
                    c, p = divmod(i, 8)
                    srcap = w1t[c * 128:(c + 1) * 128, p * 512:(p + 1) * 512]
                else:
                    c, p = divmod(i - 64, 2)
                    srcap = w2t[c * 128:(c + 1) * 128, p * 512:(p + 1) * 512]
                inst = g.dma_start(out=wf[:, 5120:5632], in_=srcap)
                qLG.inc(inst)

        p0b_state = {"i": 0}

        def p0b_emit_some(vector, n):
            for _ in range(n):
                i = p0b_state["i"]
                if i >= NW:
                    return
                p0b_state["i"] = i + 1
                vector.wait_ge(qLG.sem, lgP0 + (i + 1) * 16)
                inst = vector.reduce_sum(wsm[:, ST + 4:ST + 5], wf[:, 5120:5632],
                                         AX.X, apply_absolute_value=True)
                cP0.inc(inst)
                vector.wait_ge(cP0.sem, cP0.n)
                acc = wsm[:, ST + 5:ST + 6] if i < 64 else wsm[:, ST + 6:ST + 7]
                if i == 0 or i == 64:
                    inst = vector.tensor_copy(acc, wsm[:, ST + 4:ST + 5])
                else:
                    inst = vector.tensor_add(acc, acc, wsm[:, ST + 4:ST + 5])
                cP0.inc(inst)
                vector.wait_ge(cP0.sem, cP0.n)

        def emit_proj(sub, xsrc, ntb, wcol0):
            ld0 = qLD.n
            dv0 = cDV.n
            pe0 = cPE.n

            xr = xsrc.rearrange("(c p) n -> p c n", p=128)
            wr = wqkv.rearrange("(c p) n -> p c n", p=128)

            def dma_cnt(tb, j):
                return ld0 + (tb * 6 + j + 1) * 16

            def cast_cnt(tb, j):
                return dv0 + tb * 14 + j + 1

            def evac(tb, fh, g):
                return dv0 + tb * 14 + 6 + fh * 4 + g + 1

            def pe_grp(tb, fh, g):
                return pe0 + (tb * 2 + fh) * 4 + g + 1

            @block.sync
            def _(sync):
                for tb in range(ntb):
                    for j in range(6):
                        if tb == 0 and j in (0, 2):
                            sync.wait_ge(cDV.sem, dv0)
                        elif j in (0, 1):
                            px = (tb, 0) if j == 1 else (tb - 1, 1)
                            sync.wait_ge(cDV.sem, cast_cnt(px[0], px[1]))
                        else:
                            pw = (tb, j - 1) if j > 2 else (tb - 1, 5)
                            sync.wait_ge(cDV.sem, cast_cnt(pw[0], pw[1]))
                        if j < 2:
                            inst = sync.dma_start(
                                out=wf[:, 0:2048].rearrange("p (c n) -> p c n", n=512),
                                in_=xr[:, j * 4:(j + 1) * 4, tb * 512:tb * 512 + 512])
                        else:
                            fh, piece = divmod(j - 2, 2)
                            inst = sync.dma_start(
                                out=wf[:, 2048:4096].rearrange("p (c n) -> p c n", n=512),
                                in_=wr[:, piece * 4:(piece + 1) * 4,
                                       wcol0 + fh * 512:wcol0 + fh * 512 + 512])
                        qLD.inc(inst)

            @block.vector
            def _(vector):
                for tb in range(ntb):
                    for j in range(6):
                        vector.wait_ge(qLD.sem, dma_cnt(tb, j))
                        if j < 2:
                            if j == 0:
                                if tb >= 1:
                                    vector.wait_ge(cPE.sem, pe_grp(tb - 1, 1, 3))
                                else:
                                    vector.wait_ge(cPE.sem, pe0)
                            inst = vector.tensor_copy(
                                wkB[:, j * 2048:(j + 1) * 2048], wf[:, 0:2048])
                        else:
                            fh, piece = divmod(j - 2, 2)
                            if piece == 0:
                                if tb * 2 + fh >= 1:
                                    pfh = tb * 2 + fh - 1
                                    vector.wait_ge(cPE.sem, pe_grp(pfh // 2, pfh % 2, 3))
                                else:
                                    vector.wait_ge(cPE.sem, pe0)
                            inst = vector.tensor_copy(
                                wkA[:, piece * 2048:(piece + 1) * 2048], wf[:, 2048:4096])
                        cDV.inc(inst)
                        assert cDV.n == cast_cnt(tb, j)
                    for fh in range(2):
                        for g in range(4):
                            vector.wait_ge(cPE.sem, pe_grp(tb, fh, g))
                            q = (tb * 2 + fh) * 4 + g
                            ps = psBIG[:, (q % 4) * 512:(q % 4) * 512 + 512]
                            if sub == "K":
                                fc = fh * 4 + g
                                inst = vector.tensor_scalar(
                                    karena[:, fc * 4096 + tb * 512:fc * 4096 + tb * 512 + 512],
                                    ps, wsm[:, BK + fc:BK + fc + 1], None, OP.add)
                            elif sub == "Q":
                                fc = fh * 4 + g
                                inst = vector.tensor_scalar(
                                    qarena[:, fc * 2048 + tb * 512:fc * 2048 + tb * 512 + 512],
                                    ps, wsm[:, BQ + fc:BQ + fc + 1], None, OP.add)
                            else:
                                kt = tb * 4 + g
                                h0 = fh * 8
                                dst = varena[:, kt * VTILE:(kt + 1) * VTILE] \
                                    .rearrange("p (h v) -> p h v", v=VSTRIDE)[:, h0:h0 + 8, 0:64]
                                inst = vector.tensor_tensor(
                                    dst,
                                    ps.rearrange("p (h v) -> p h v", v=64),
                                    wf[:, 4096 + fh * 512:4096 + fh * 512 + 512]
                                    .rearrange("p (h v) -> p h v", v=64),
                                    OP.add)
                            cDV.inc(inst)
                            assert cDV.n == evac(tb, fh, g)
                    p0b_emit_some(vector, 7)

            @block.tensor
            def _(t):
                for tb in range(ntb):
                    for fh in range(2):
                        for g in range(4):
                            q = (tb * 2 + fh) * 4 + g
                            ps = psBIG[:, (q % 4) * 512:(q % 4) * 512 + 512]
                            if q >= 4:
                                pq = q - 4
                                t.wait_ge(cDV.sem, evac(pq // 8, (pq // 4) % 2, pq % 4))
                            for dc in range(8):
                                t.wait_ge(cDV.sem, max(cast_cnt(tb, dc // 4),
                                                       cast_cnt(tb, 2 + fh * 2 + dc // 4)))
                                if sub in ("K", "Q"):
                                    inst = t.matmul(
                                        ps,
                                        wkA[:, dc * 512 + g * 128:dc * 512 + g * 128 + 128],
                                        wkB[:, dc * 512:dc * 512 + 512],
                                        start=(dc == 0), stop=(dc == 7))
                                else:
                                    inst = t.matmul(
                                        ps,
                                        wkB[:, dc * 512 + g * 128:dc * 512 + g * 128 + 128],
                                        wkA[:, dc * 512:dc * 512 + 512],
                                        start=(dc == 0), stop=(dc == 7))
                                if dc == 7:
                                    cPE.inc(inst)
                            assert cPE.n == pe_grp(tb, fh, g)

            qLD.n = ld0 + ntb * 6 * 16
            assert cDV.n == dv0 + ntb * 14
            assert cPE.n == pe0 + ntb * 8

        emit_proj("K", xT, 8, 1024)
        emit_proj("V", xT, 8, 2048)
        emit_proj("Q", xTq, 4, 0)

        @block.vector
        def _(vector):
            p0b_emit_some(vector, NW)
        assert p0b_state["i"] == NW
        assert cP0.n == NW * 2
        PE_PROJ = cPE.n
        DV_PROJ = cDV.n

        @block.tensor
        def _(t):
            t.wait_ge(cP0.sem, NW * 2)
            t.matmul(psC[0:1, 0:1], wsm[:, 96:97], wsm[:, ST + 5:ST + 6],
                     start=True, stop=True)
            inst = t.matmul(psC[0:1, 512:513], wsm[:, 96:97], wsm[:, ST + 6:ST + 7],
                            start=True, stop=True)
            cPE.inc(inst)
        PE_W0 = cPE.n

        @block.vector
        def _(vector):
            vector.wait_ge(cPE.sem, PE_W0)
            for inst in [
                vector.tensor_copy(wsm[0:1, ST:ST + 1], psC[0:1, 0:1]),
                vector.tensor_copy(wsm[0:1, ST + 1:ST + 2], psC[0:1, 512:513]),
                vector.tensor_scalar(wsm[0:1, ST:ST + 1], wsm[0:1, ST:ST + 1],
                                     1.0 / (D * H), EPS_Q, OP.mult, OP.max),
                vector.tensor_scalar(wsm[0:1, ST + 1:ST + 2], wsm[0:1, ST + 1:ST + 2],
                                     1.0 / (D * H), EPS_Q, OP.mult, OP.max),
                vector.tensor_scalar_mul(wsm[0:1, ST + 2:ST + 3], wsm[0:1, ST:ST + 1], 0.5),
                vector.tensor_scalar_mul(wsm[0:1, ST + 3:ST + 4], wsm[0:1, ST:ST + 1], -0.5),
                vector.tensor_scalar_mul(wsm[0:1, ST + 4:ST + 5], wsm[0:1, ST + 1:ST + 2], 0.5),
                vector.tensor_scalar_mul(wsm[0:1, ST + 5:ST + 6], wsm[0:1, ST + 1:ST + 2], -0.5),
                vector.tensor_scalar_mul(wsm[0:1, ST + 6:ST + 7], wsm[0:1, ST:ST + 1], 1.0 / 127.0),
                vector.tensor_scalar_mul(wsm[0:1, ST + 7:ST + 8], wsm[0:1, ST + 1:ST + 2],
                                         1.0 / 127.0),
            ]:
                cDV.inc(inst)
                vector.wait_ge(cDV.sem, cDV.n)
        DV_WS = cDV.n

        @block.gpsimd
        def _(g):
            g.wait_ge(cDV.sem, DV_WS)
            inst = g.dma_start(out=sscr.rearrange("(a n) -> a n", a=1), in_=wsm[0:1, ST:ST + 8])
            qLG.inc(inst)
            g.wait_ge(qLG.sem, qLG.n)
            inst = g.dma_start(out=wsm[:, WS:WS + 8],
                               in_=sscr.rearrange("(a n) -> a n", a=1).partition_broadcast(128))
            qLG.inc(inst)
        LG_WS = qLG.n

        pe0 = cPE.n
        ac0 = cAC.n
        dv0 = cDV.n
        lg0 = qLG.n
        ldA0 = qLD.n
        steps = [(h, qc) for h in range(NH) for qc in range(2)]
        NSTEP = len(steps)
        GKT = NSTEP * NKT

        def st_inc(gkt):
            return pe0 + (gkt + 1 if gkt < 2 else 2 * gkt)

        def pv_inc(gkt):
            return pe0 + 2 * gkt + 3 if gkt < GKT - 1 else pe0 + 2 * gkt + 2

        def exp_inc(gkt):
            return ac0 + (gkt // NKT) * 33 + (gkt % NKT) + 1

        def ac_copy(m):
            return ac0 + (m + 1) * 33

        def r_inc(m):
            return dv0 + m * 2 + 1

        def tt_inc(m):
            return dv0 + m * 2 + 2

        def lg_back(m):
            return lg0 + (m * 3 + 2) * 16

        def lg_move(m):
            return lg0 + (m * 3 + 3) * 16

        side_dma = []
        side_dve = []

        def w1_dma(i):
            c, p = divmod(i, 4)
            def f(sync):
                sync.wait_ge(cPE.sem, st_inc((4 * c + 4) * NKT - 1))
                if i >= 1:
                    sync.wait_ge(cTN.sem, 2 * i)
                inst = sync.dma_start(out=wf[:, 4096:5120],
                                      in_=w1t[c * 128:(c + 1) * 128,
                                              p * 1024:(p + 1) * 1024])
                qLD.inc(inst)
            return f

        def w1_dve(i):
            c, p = divmod(i, 4)
            def f(vector):
                if i == 0:
                    vector.wait_ge(qLG.sem, LG_WS)
                vector.wait_ge(qLD.sem, ldA0 + (i + 1) * 16)
                inst = vector.tensor_scalar(wf[:, 1024:2048], wf[:, 4096:5120],
                                            wsm[:, C_NT1:C_NT1 + 1], None, OP.is_lt)
                cTN.inc(inst)
                inst = vector.scalar_tensor_tensor(
                    karena[:, c * 4096 + p * 1024:c * 4096 + p * 1024 + 1024],
                    wf[:, 4096:5120], wsm[:, C_T1:C_T1 + 1], wf[:, 1024:2048],
                    OP.is_gt, OP.subtract)
                cTN.inc(inst)
            return f

        for i in range(32):
            side_dma.append(w1_dma(i))
            side_dve.append(w1_dve(i))

        def wo_in(c):
            def f(sync):
                sync.wait_ge(cTN.sem, 64 + c)
                inst = sync.dma_start(out=wf[:, 4096:5120], in_=wo[c * 128:(c + 1) * 128, :])
                qLD.inc(inst)
            return f

        def wo_out(c):
            def f(sync):
                sync.wait_ge(cTN.sem, 64 + c + 1)
                inst = sync.dma_start(out=wo_bf[c], in_=wkB[:, 3072:4096])
                qLD.inc(inst)
            return f

        def wo_cast(c):
            def f(vector):
                vector.wait_ge(qLD.sem, ldA0 + (32 + 2 * c + 1) * 16)
                if c >= 1:
                    vector.wait_ge(qLD.sem, ldA0 + (32 + 2 * c) * 16)
                inst = vector.tensor_copy(wkB[:, 3072:4096], wf[:, 4096:5120])
                cTN.inc(inst)
            return f

        for c in range(8):
            side_dma.append(wo_in(c))
            side_dma.append(wo_out(c))
            side_dve.append(wo_cast(c))

        def b1_dma(j):
            def f(sync):
                sync.wait_ge(cTN.sem, 72 + j)
                inst = sync.dma_start(out=wf[:, 4096:4608],
                                      in_=b1[j * 512:(j + 1) * 512]
                                      .rearrange("(a n) -> a n", a=1).partition_broadcast(128))
                qLD.inc(inst)
            return f

        def b1_cast(j):
            def f(vector):
                vector.wait_ge(qLD.sem, ldA0 + (48 + j + 1) * 16)
                inst = vector.tensor_copy(wkA[:, j * 512:(j + 1) * 512], wf[:, 4096:4608])
                cTN.inc(inst)
            return f

        for j in range(8):
            side_dma.append(b1_dma(j))
            side_dve.append(b1_cast(j))

        def b2_dma(sync):
            sync.wait_ge(cTN.sem, 80)
            inst = sync.dma_start(out=wf[0:1, 4096:5120],
                                  in_=b2.rearrange("(a n) -> a n", a=1))
            qLD.inc(inst)

        def b2_cast(vector):
            vector.wait_ge(qLD.sem, ldA0 + 57 * 16)
            inst = vector.tensor_copy(b2row[0:1, :], wf[0:1, 4096:5120])
            cTN.inc(inst)

        def bo_dma(sync):
            sync.wait_ge(cTN.sem, 64)
            inst = sync.dma_start(out=wf[:, 1024:2048],
                                  in_=bo.rearrange("(a n) -> a n", a=1).partition_broadcast(128))
            qLD.inc(inst)

        side_dma.append(b2_dma)
        side_dve.append(b2_cast)
        side_dma.append(bo_dma)
        N_SIDE_DMA = len(side_dma)
        N_SIDE_DVE = len(side_dve)

        def emit_S(t, gkt):
            m, kt = divmod(gkt, NKT)
            h, qc = steps[m]
            ch, rh = h // 2, (h % 2) * 64
            stb = (gkt % 2) * 1024
            lhsK = karena[rh:rh + 64, ch * 4096 + kt * 128:ch * 4096 + kt * 128 + 128]
            for half in range(2):
                inst = t.matmul(psBIG[:, stb + half * 512:stb + half * 512 + 512],
                                lhsK,
                                qarena[rh:rh + 64,
                                       ch * 2048 + qc * 1024 + half * 512:
                                       ch * 2048 + qc * 1024 + half * 512 + 512],
                                start=True, stop=True)
            cPE.inc(inst)
            assert cPE.n == st_inc(gkt)

        def emit_PV(t, gkt):
            m, kt = divmod(gkt, NKT)
            h, qc = steps[m]
            t.wait_ge(cAC.sem, exp_inc(gkt))
            if kt == 0 and m >= 1:
                t.wait_ge(cAC.sem, ac_copy(m - 1))
            lhsV = varena[:, kt * VTILE + h * VSTRIDE:kt * VTILE + h * VSTRIDE + VSTRIDE]
            pt = wkB[:, (gkt % 2) * 1024:(gkt % 2) * 1024 + 1024]
            for half in range(2):
                inst = t.matmul(psC[0:65, half * 512:half * 512 + 512],
                                lhsV, pt[:, half * 512:half * 512 + 512],
                                start=(kt == 0), stop=(kt == NKT - 1))
            cPE.inc(inst)
            assert cPE.n == pv_inc(gkt)

        @block.tensor
        def _(t):
            t.wait_ge(cDV.sem, DV_WS)
            t.wait_ge(cDV.sem, DV_PROJ)
            emit_S(t, 0)
            emit_S(t, 1)
            for gkt in range(GKT - 2):
                emit_PV(t, gkt)
                emit_S(t, gkt + 2)
            emit_PV(t, GKT - 2)
            emit_PV(t, GKT - 1)

        @block.sync
        def _(sync):
            for f in side_dma:
                f(sync)
        LD_SIDE = qLD.n
        assert LD_SIDE == ldA0 + N_SIDE_DMA * 16

        @block.scalar
        def _(scalar):
            for m, (h, qc) in enumerate(steps):
                for kt in range(NKT):
                    gkt = m * NKT + kt
                    stb = (gkt % 2) * 1024
                    scalar.wait_ge(cPE.sem, st_inc(gkt))
                    if gkt >= 2:
                        scalar.wait_ge(cPE.sem, pv_inc(gkt - 2))
                    inst = scalar.activation(wkB[:, (gkt % 2) * 1024:(gkt % 2) * 1024 + 1024],
                                             psBIG[:, stb:stb + 1024], AF.Exp,
                                             bias=wsm[:, NEGM:NEGM + 1], scale=0.125)
                    cAC.inc(inst)
                    assert cAC.n == exp_inc(gkt)
                scalar.wait_ge(cPE.sem, pv_inc(m * NKT + NKT - 1))
                if m >= 1:
                    scalar.wait_ge(cDV.sem, tt_inc(m - 1))
                inst = scalar.activation(wf[:, 0:1024], psC[:, 0:1024], AF.Copy)
                cAC.inc(inst)
                assert cAC.n == ac_copy(m)

        @block.vector
        def _(vector):
            popped = 0
            for m, (h, qc) in enumerate(steps):
                vector.wait_ge(cAC.sem, ac_copy(m))
                inst = vector.reciprocal(wf[64:65, 3072:4096], wf[64:65, 0:1024])
                cDV.inc(inst)
                assert cDV.n == r_inc(m)
                vector.wait_ge(qLG.sem, lg_back(m))
                if m >= 1:
                    vector.wait_ge(qLG.sem, lg_move(m - 1))
                inst = vector.tensor_tensor(wkB[0:64, 2048:3072], wf[0:64, 0:1024],
                                            wf[0:64, 3072:4096], OP.mult)
                cDV.inc(inst)
                assert cDV.n == tt_inc(m)
                if m >= 4 and popped < 28:
                    side_dve[popped](vector)
                    popped += 1
            for f in side_dve[popped:]:
                f(vector)
        assert cTN.n == 81

        @block.gpsimd
        def _(g):
            for m, (h, qc) in enumerate(steps):
                ch, rh = h // 2, (h % 2) * 64
                g.wait_ge(cDV.sem, r_inc(m))
                inst = g.dma_start(out=dscr[:], in_=wf[64:65, 3072:4096])
                qLG.inc(inst)
                g.wait_ge(qLG.sem, qLG.n)
                if m >= 1:
                    g.wait_ge(cDV.sem, tt_inc(m - 1))
                inst = g.dma_start(out=wf[0:64, 3072:4096], in_=dscr.partition_broadcast(64))
                qLG.inc(inst)
                g.wait_ge(cDV.sem, tt_inc(m))
                inst = g.dma_start(
                    out=qarena[rh:rh + 64, ch * 2048 + qc * 1024:ch * 2048 + qc * 1024 + 1024],
                    in_=wkB[0:64, 2048:3072])
                qLG.inc(inst)

        assert cPE.n == pe0 + NSTEP * NKT * 2
        assert cAC.n == ac0 + NSTEP * 33
        assert cDV.n == dv0 + NSTEP * 2
        assert qLG.n == lg0 + NSTEP * 3 * 16
        PE_ATT = cPE.n
        LG_ATT = qLG.n

        if debug:
            @block.sync
            def _(sync):
                sync.wait_ge(cPE.sem, PE_ATT)
                sync.wait_ge(qLG.sem, LG_ATT)
                qLD.inc(sync.dma_start(out=dbg_k[:], in_=karena[:, :]))
                qLD.inc(sync.dma_start(out=dbg_v[:], in_=varena[:, :]))
                qLD.inc(sync.dma_start(out=dbg_av[:], in_=qarena[:, :]))
        DBG_DUMP = qLD.n

        lg0 = qLG.n
        dv0 = cDV.n

        @block.gpsimd
        def _(g):
            g.wait_ge(cPE.sem, PE_ATT)
            g.wait_ge(cTN.sem, 81)
            if debug:
                g.wait_ge(qLD.sem, DBG_DUMP)
            for i in range(32):
                if i >= 1:
                    g.wait_ge(cDV.sem, dv0 + i * 2)
                inst = g.dma_start(out=wf[:, 4096:5120],
                                   in_=w2t[i * 128:(i + 1) * 128, :])
                qLG.inc(inst)

        @block.vector
        def _(vector):
            for i in range(32):
                vector.wait_ge(qLG.sem, lg0 + (i + 1) * 16)
                inst = vector.tensor_scalar(wf[:, 1024:2048] if False else wf[:, 2048:3072],
                                            wf[:, 4096:5120],
                                            wsm[:, C_NT2:C_NT2 + 1], None, OP.is_lt)
                cDV.inc(inst)
                inst = vector.scalar_tensor_tensor(
                    varena[:, i * 1024:(i + 1) * 1024],
                    wf[:, 4096:5120], wsm[:, C_T2:C_T2 + 1], wf[:, 2048:3072],
                    OP.is_gt, OP.subtract)
                cDV.inc(inst)
        assert cDV.n == dv0 + 64
        qLG.n = lg0 + 32 * 16
        DV_W = cDV.n
        LG_W = qLG.n

        ld0 = qLD.n
        lg0 = qLG.n
        pe0 = cPE.n
        dv0 = cDV.n
        ac0 = cAC.n

        def pA_xdma(t):
            return ld0 + (t * 3 + 1) * 16

        def pA_wo(t, dh):
            return ld0 + (t * 3 + 2 + dh) * 16

        def pA_pe(t, k):
            return pe0 + t * 3 + k

        def pA_dv(t, k):
            return dv0 + t * 40 + k

        def pA_ac(t, k):
            return ac0 + t * 2 + k

        def pA_x1dma(t):
            return lg0 + (t + 1) * 16

        @block.sync
        def _(sync):
            sync.wait_ge(qLG.sem, LG_ATT)
            for t in range(NTT):
                if t >= 1:
                    sync.wait_ge(cDV.sem, pA_dv(t - 1, 1))
                inst = sync.dma_start(out=wf[:, 0:1024],
                                      in_=x_own[t * 128:(t + 1) * 128, :])
                qLD.inc(inst)
                for dh in range(2):
                    if t * 2 + dh >= 1:
                        pt_, pdh = divmod(t * 2 + dh - 1, 2)
                        sync.wait_ge(cPE.sem, pA_pe(pt_, pdh + 1))
                    inst = sync.dma_start(
                        out=wkB[:, 0:4096].rearrange("p (c x) -> p c x", x=512),
                        in_=wo_bf[:, :, dh * 512:dh * 512 + 512].rearrange("c p x -> p c x"))
                    qLD.inc(inst)

        @block.tensor
        def _(t_):
            t_.wait_ge(qLG.sem, LG_ATT)
            if debug:
                t_.wait_ge(qLD.sem, DBG_DUMP)
            for t in range(NTT):
                ps = psBIG[:, (t % 2) * 1024:(t % 2) * 1024 + 1024]
                if t >= 2:
                    t_.wait_ge(cDV.sem, pA_dv(t - 2, 2))
                for dh in range(2):
                    t_.wait_ge(qLD.sem, pA_wo(t, dh))
                    for c in range(8):
                        inst = t_.matmul(ps[:, dh * 512:dh * 512 + 512],
                                         qarena[:, c * 2048 + t * 128:c * 2048 + t * 128 + 128],
                                         wkB[:, c * 512:c * 512 + 512],
                                         start=(c == 0), stop=(c == 7))
                    cPE.inc(inst)
                    assert cPE.n == pA_pe(t, dh + 1)
                t_.wait_ge(cDV.sem, pA_dv(t, 38))
                if t >= 1:
                    t_.wait_ge(cDV.sem, pA_dv(t - 1, 40))
                for dc in range(8):
                    dst = psTa if dc < 4 else psTb
                    inst = t_.transpose(dst[:, (dc % 4) * 128:(dc % 4) * 128 + 128],
                                        wkB[:, 4096 + dc * 128:4096 + dc * 128 + 128],
                                        ident[:])
                cPE.inc(inst)
                assert cPE.n == pA_pe(t, 3)

        @block.vector
        def _(vector):
            vector.wait_ge(qLD.sem, LD_SIDE)
            for t in range(NTT):
                k = 0
                vector.wait_ge(qLD.sem, pA_xdma(t))
                inst = vector.tensor_add(wf[:, 2048:3072], wf[:, 0:1024], wf[:, 1024:2048])
                k += 1; cDV.inc(inst)
                vector.wait_ge(cPE.sem, pA_pe(t, 2))
                ps = psBIG[:, (t % 2) * 1024:(t % 2) * 1024 + 1024]
                inst = vector.tensor_add(wf[:, 2048:3072], ps, wf[:, 2048:3072])
                k += 1; cDV.inc(inst)
                inst = vector.reduce_sum(wsm[:, ST:ST + 1], wf[:, 2048:3072], AX.X)
                k += 1; cDV.inc(inst); vector.wait_ge(cDV.sem, cDV.n)
                inst = vector.tensor_scalar_mul(wsm[:, ST + 1:ST + 2], wsm[:, ST:ST + 1], -1.0 / D)
                k += 1; cDV.inc(inst); vector.wait_ge(cDV.sem, cDV.n)
                if t >= 1:
                    vector.wait_ge(qLG.sem, pA_x1dma(t - 1))
                inst = vector.tensor_scalar(wf[:, 3072:4096], wf[:, 2048:3072],
                                            wsm[:, ST + 1:ST + 2], None, OP.add)
                k += 1; cDV.inc(inst)
                vector.wait_ge(cAC.sem, pA_ac(t, 1))
                inst = vector.reduce_sum(wsm[:, ST + 2:ST + 3], wf[:, 4096:5120], AX.X)
                k += 1; cDV.inc(inst); vector.wait_ge(cDV.sem, cDV.n)
                inst = vector.tensor_scalar(wsm[:, ST + 3:ST + 4], wsm[:, ST + 2:ST + 3],
                                            1.0 / D, EPS_LN, OP.mult, OP.add)
                k += 1; cDV.inc(inst)
                vector.wait_ge(cAC.sem, pA_ac(t, 2))
                inst = vector.reciprocal(wsm[:, ST + 5:ST + 6], wsm[:, ST + 4:ST + 5])
                k += 1; cDV.inc(inst); vector.wait_ge(cDV.sem, cDV.n)
                inst = vector.tensor_scalar(wf[:, 3072:4096], wf[:, 3072:4096],
                                            wsm[:, ST + 5:ST + 6], None, OP.mult)
                k += 1; cDV.inc(inst)
                inst = vector.reduce_max(wsm[:, ST + 6:ST + 7], wf[:, 3072:4096], AX.X,
                                         apply_absolute_value=True)
                k += 1; cDV.inc(inst); vector.wait_ge(cDV.sem, cDV.n)
                inst = vector.tensor_scalar_max(wsm[:, ST + 6:ST + 7], wsm[:, ST + 6:ST + 7], EPS_Q)
                k += 1; cDV.inc(inst); vector.wait_ge(cDV.sem, cDV.n)
                inst = vector.reciprocal(wsm[:, ST + 7:ST + 8], wsm[:, ST + 6:ST + 7])
                k += 1; cDV.inc(inst); vector.wait_ge(cDV.sem, cDV.n)
                inst = vector.tensor_scalar_mul(wsm[:, ST + 7:ST + 8], wsm[:, ST + 7:ST + 8], 127.0)
                k += 1; cDV.inc(inst); vector.wait_ge(cDV.sem, cDV.n)
                inst = vector.tensor_tensor(wsm[:, S1C + t:S1C + t + 1],
                                            wsm[:, ST + 6:ST + 7],
                                            wsm[:, C_WQ1:C_WQ1 + 1], OP.mult)
                k += 1; cDV.inc(inst)
                for qch in range(2):
                    k += emit_quant_chunk(vector, cDV, wf, wri,
                                          wsm[:, ST + 7:ST + 8],
                                          wf[:, 3072 + qch * 512:3072 + qch * 512 + 512],
                                          wkB[:, 4096 + qch * 512:4096 + qch * 512 + 512])
                assert k == 38
                vector.wait_ge(cPE.sem, pA_pe(t, 3))
                inst = vector.tensor_copy(
                    qarena.ap().rearrange("p (c n) -> p c n", n=2048)[:, 0:4, t * 128:t * 128 + 128],
                    psTa.ap().rearrange("p (c n) -> p c n", n=128))
                k += 1; cDV.inc(inst)
                inst = vector.tensor_copy(
                    qarena.ap().rearrange("p (c n) -> p c n", n=2048)[:, 4:8, t * 128:t * 128 + 128],
                    psTb.ap().rearrange("p (c n) -> p c n", n=128))
                k += 1; cDV.inc(inst)
                assert cDV.n == pA_dv(t, 40)

        @block.scalar
        def _(scalar):
            for t in range(NTT):
                scalar.wait_ge(cDV.sem, pA_dv(t, 5))
                inst = scalar.activation(wf[:, 4096:5120], wf[:, 3072:4096], AF.Square)
                cAC.inc(inst)
                scalar.wait_ge(cDV.sem, pA_dv(t, 7))
                inst = scalar.activation(wsm[:, ST + 4:ST + 5], wsm[:, ST + 3:ST + 4], AF.Sqrt)
                cAC.inc(inst)

        @block.gpsimd
        def _(g):
            for t in range(NTT):
                g.wait_ge(cDV.sem, pA_dv(t, 38))
                inst = g.dma_start(out=x1scr[t * 128:(t + 1) * 128, :], in_=wf[:, 3072:4096])
                qLG.inc(inst)

        qLD.n = ld0 + NTT * 3 * 16
        assert cPE.n == pe0 + NTT * 3
        assert cDV.n == dv0 + NTT * 40
        assert cAC.n == ac0 + NTT * 2
        X1DMA = [pA_x1dma(t) for t in range(NTT)]
        qLG.n = lg0 + NTT * 16
        DV_A = cDV.n

        ld0 = qLD.n
        pe0 = cPE.n
        dv0 = cDV.n
        ac0 = cAC.n

        lgB0 = qLG.n

        def pB_pe(t, k):
            return pe0 + t * 25 + k

        def pB_lg(t, k):
            return lgB0 + (t * 2 + k) * 16

        def pB_dv(t, k):
            return dv0 + t * 127 + k

        def pB_ac(t, k):
            return ac0 + t * 10 + k

        def pB_x1(t):
            return ld0 + (t + 1) * 16

        def pB_y(t):
            return (t + 1) * 16

        @block.sync
        def _(sync):
            for t in range(NTT):
                sync.wait_ge(cDV.sem, pB_dv(t, 110))
                sync.wait_ge(qLG.sem, X1DMA[t])
                inst = sync.dma_start(out=wf[:, 4096:5120], in_=x1scr[t * 128:(t + 1) * 128, :])
                qLD.inc(inst)

        @block.tensor
        def _(t_):
            t_.wait_ge(cDV.sem, DV_A)
            for t in range(NTT):
                for jh in range(2):
                    if jh == 1:
                        t_.wait_ge(cDV.sem, pB_dv(t, 4))
                    elif t >= 1:
                        t_.wait_ge(cDV.sem, pB_dv(t - 1, 8))
                    for dc in range(8):
                        lhs = qarena[:, dc * 2048 + t * 128:dc * 2048 + t * 128 + 128]
                        for sl in range(4):
                            inst = t_.matmul(
                                psBIG[:, sl * 512:sl * 512 + 512], lhs,
                                karena[:, dc * 4096 + jh * 2048 + sl * 512:
                                       dc * 4096 + jh * 2048 + sl * 512 + 512],
                                start=(dc == 0), stop=(dc == 7))
                            if dc == 7:
                                cPE.inc(inst)
                    assert cPE.n == pB_pe(t, jh * 4 + 4)
                t_.wait_ge(cDV.sem, pB_dv(t, 110))
                if t >= 1:
                    t_.wait_ge(cDV.sem, pB_dv(t - 1, 120))

                def _tr(grp):
                    if grp >= 2:
                        t_.wait_ge(cDV.sem, pB_dv(t, 112 + grp - 2))
                    elif t >= 1:
                        t_.wait_ge(cDV.sem, pB_dv(t - 1, 118 + grp))
                    dstp = psTa if grp % 2 == 0 else psTb
                    for i in range(4):
                        hc = grp * 4 + i
                        inst2 = t_.transpose(
                            dstp[:, i * 128:i * 128 + 128],
                            wkB[:, hc * 128:hc * 128 + 128], ident[:])
                    cPE.inc(inst2)

                def _mm(grp):
                    t_.wait_ge(cDV.sem, pB_dv(t, 112 + grp))
                    for i in range(4):
                        hc = grp * 4 + i
                        lhs = wkB[:, 4096 + (grp % 2) * 512 + i * 128:
                                  4096 + (grp % 2) * 512 + i * 128 + 128]
                        for half in range(2):
                            inst2 = t_.matmul(psC[:, half * 512:half * 512 + 512], lhs,
                                              varena[:, hc * 1024 + half * 512:
                                                     hc * 1024 + half * 512 + 512],
                                              start=(hc == 0), stop=False)
                    cPE.inc(inst2)

                _tr(0)
                _tr(1)
                for grp in range(2, 8):
                    _mm(grp - 2)
                    _tr(grp)
                _mm(6)
                _mm(7)
                t_.wait_ge(cDV.sem, pB_dv(t, 111))
                for half in range(2):
                    inst = t_.matmul(psC[:, half * 512:half * 512 + 512],
                                     s2invrow[0:1, :],
                                     b2row[0:1, half * 512:half * 512 + 512],
                                     start=False, stop=True)
                cPE.inc(inst)
                assert cPE.n == pB_pe(t, 25)

        @block.vector
        def _(vector):
            for t in range(NTT):
                k = 0
                for jh in range(2):
                    for sl in range(4):
                        gidx = jh * 4 + sl
                        vector.wait_ge(cPE.sem, pB_pe(t, gidx + 1))
                        if gidx >= 2:
                            vector.wait_ge(cAC.sem, pB_ac(t, gidx - 1))
                        elif t >= 1:
                            vector.wait_ge(cAC.sem, pB_ac(t - 1, 7 + gidx))
                        inst = vector.scalar_tensor_tensor(
                            wf[:, 4096 + (gidx % 2) * 512:4096 + (gidx % 2) * 512 + 512],
                            psBIG[:, sl * 512:sl * 512 + 512],
                            wsm[:, S1C + t:S1C + t + 1],
                            wkA[:, jh * 2048 + sl * 512:jh * 2048 + sl * 512 + 512],
                            OP.mult, OP.add)
                        k += 1; cDV.inc(inst)
                vector.wait_ge(cAC.sem, pB_ac(t, 8))
                inst = vector.reduce_max(wsm[:, ST:ST + 1], wf[:, 0:4096], AX.X,
                                         apply_absolute_value=True)
                k += 1; cDV.inc(inst); vector.wait_ge(cDV.sem, cDV.n)
                inst = vector.tensor_scalar_max(wsm[:, ST:ST + 1], wsm[:, ST:ST + 1], EPS_Q)
                k += 1; cDV.inc(inst); vector.wait_ge(cDV.sem, cDV.n)
                inst = vector.reciprocal(wsm[:, ST + 1:ST + 2], wsm[:, ST:ST + 1])
                k += 1; cDV.inc(inst); vector.wait_ge(cDV.sem, cDV.n)
                inst = vector.tensor_scalar_mul(wsm[:, ST + 1:ST + 2], wsm[:, ST + 1:ST + 2],
                                                127.0)
                k += 1; cDV.inc(inst); vector.wait_ge(cDV.sem, cDV.n)
                inst = vector.tensor_tensor(wsm[:, ST + 2:ST + 3], wsm[:, ST:ST + 1],
                                            wsm[:, C_WQ2:C_WQ2 + 1], OP.mult)
                k += 1; cDV.inc(inst); vector.wait_ge(cDV.sem, cDV.n)
                inst = vector.reciprocal(wsm[:, ST + 3:ST + 4], wsm[:, ST + 2:ST + 3])
                k += 1; cDV.inc(inst)
                assert k == 14
                for qch in range(8):
                    k += emit_quant_chunk(vector, cDV, wf, wri,
                                          wsm[:, ST + 1:ST + 2],
                                          wf[:, qch * 512:qch * 512 + 512],
                                          wkB[:, qch * 512:qch * 512 + 512])
                assert k == 110
                vector.wait_ge(qLG.sem, pB_lg(t, 2))
                inst = vector.tensor_copy(s2invrow[0:1, :], wsm[0:1, 64:192])
                k += 1; cDV.inc(inst)
                for grp in range(8):
                    tri = 9 + grp if grp < 2 else 8 + 2 * grp
                    vector.wait_ge(cPE.sem, pB_pe(t, tri))
                    if grp >= 2:
                        mmi = 11 + 2 * (grp - 2) if grp - 2 <= 5 else 17 + (grp - 2)
                        vector.wait_ge(cPE.sem, pB_pe(t, mmi))
                    elif t >= 1:
                        pmi = 17 + (grp + 6)
                        vector.wait_ge(cPE.sem, pB_pe(t - 1, pmi))
                    srcp = psTa if grp % 2 == 0 else psTb
                    inst = vector.tensor_copy(
                        wkB[:, 4096 + (grp % 2) * 512:4096 + (grp % 2) * 512 + 512],
                        srcp[:, 0:512])
                    k += 1; cDV.inc(inst)
                vector.wait_ge(qLD.sem, pB_x1(t))
                vector.wait_ge(cPE.sem, pB_pe(t, 25))
                if t >= 1:
                    vector.wait_ge(qOUT.sem, pB_y(t - 1))
                inst = vector.scalar_tensor_tensor(wf[:, 0:1024], psC[:, 0:1024],
                                                   wsm[:, ST + 2:ST + 3],
                                                   wf[:, 4096:5120], OP.mult, OP.add)
                k += 1; cDV.inc(inst)
                inst = vector.reduce_sum(wsm[:, ST + 4:ST + 5], wf[:, 0:1024], AX.X)
                k += 1; cDV.inc(inst); vector.wait_ge(cDV.sem, cDV.n)
                inst = vector.tensor_scalar_mul(wsm[:, ST + 5:ST + 6], wsm[:, ST + 4:ST + 5],
                                                -1.0 / D)
                k += 1; cDV.inc(inst); vector.wait_ge(cDV.sem, cDV.n)
                inst = vector.tensor_scalar(wf[:, 0:1024], wf[:, 0:1024],
                                            wsm[:, ST + 5:ST + 6], None, OP.add)
                k += 1; cDV.inc(inst)
                vector.wait_ge(cAC.sem, pB_ac(t, 9))
                inst = vector.reduce_sum(wsm[:, ST + 7:ST + 8], wf[:, 1024:2048], AX.X)
                k += 1; cDV.inc(inst); vector.wait_ge(cDV.sem, cDV.n)
                inst = vector.tensor_scalar(wsm[:, ST + 6:ST + 7], wsm[:, ST + 7:ST + 8],
                                            1.0 / D, EPS_LN, OP.mult, OP.add)
                k += 1; cDV.inc(inst)
                vector.wait_ge(cAC.sem, pB_ac(t, 10))
                inst = vector.reciprocal(wsm[:, ST + 5:ST + 6], wsm[:, ST + 7:ST + 8])
                k += 1; cDV.inc(inst); vector.wait_ge(cDV.sem, cDV.n)
                inst = vector.tensor_scalar(wf[:, 0:1024], wf[:, 0:1024],
                                            wsm[:, ST + 5:ST + 6], None, OP.mult)
                k += 1; cDV.inc(inst)
                assert k == 127
                assert cDV.n == pB_dv(t, 127)

        @block.scalar
        def _(scalar):
            for t in range(NTT):
                for jh in range(2):
                    for sl in range(4):
                        gidx = jh * 4 + sl
                        scalar.wait_ge(cDV.sem, pB_dv(t, gidx + 1))
                        if gidx < 2 and t >= 1:
                            scalar.wait_ge(qOUT.sem, pB_y(t - 1))
                        inst = scalar.activation(
                            wf[:, gidx * 512:gidx * 512 + 512],
                            wf[:, 4096 + (gidx % 2) * 512:4096 + (gidx % 2) * 512 + 512],
                            AF.Gelu)
                        cAC.inc(inst)
                scalar.wait_ge(cDV.sem, pB_dv(t, 123))
                inst = scalar.activation(wf[:, 1024:2048], wf[:, 0:1024], AF.Square)
                cAC.inc(inst)
                scalar.wait_ge(cDV.sem, pB_dv(t, 125))
                inst = scalar.activation(wsm[:, ST + 7:ST + 8], wsm[:, ST + 6:ST + 7], AF.Sqrt)
                cAC.inc(inst)
                assert cAC.n == pB_ac(t, 10)

        if debug:
            @block.vector
            def _(vector):
                vector.wait_ge(qOUT.sem, 0)
        @block.gpsimd
        def _(g):
            for t in range(NTT):
                g.wait_ge(cDV.sem, pB_dv(t, 14))
                if t >= 1:
                    g.wait_ge(cDV.sem, pB_dv(t - 1, 111))
                inst = g.dma_start(out=rscr.rearrange("(p a) -> p a", a=1),
                                   in_=wsm[:, ST + 3:ST + 4])
                qLG.inc(inst)
                g.wait_ge(qLG.sem, pB_lg(t, 1))
                inst = g.dma_start(out=wsm[0:1, 64:192],
                                   in_=rscr.rearrange("(a n) -> a n", a=1))
                qLG.inc(inst)
                g.wait_ge(cDV.sem, pB_dv(t, 127))
                inst = g.dma_start(out=y[t * 128:(t + 1) * 128, :], in_=wf[:, 0:1024])
                qOUT.inc(inst)
            if debug:
                qOUT.inc(g.dma_start(out=dbg_wsm[:], in_=wsm[:, :]))
                qOUT.inc(g.dma_start(out=dbg_x1[:], in_=x1scr[:]))
                qOUT.inc(g.dma_start(out=dbg_wf[:], in_=wf[:, :]))
                qOUT.inc(g.dma_start(out=dbg_wo[:], in_=wo_bf[:]))
                qOUT.inc(g.dma_start(out=dbg_q[:], in_=qarena[:, :]))
            g.wait_ge(qOUT.sem, qOUT.n)

        qLD.n = ld0 + NTT * 16
        assert cPE.n == pe0 + NTT * 25
        assert cDV.n == dv0 + NTT * 127
        assert cAC.n == ac0 + NTT * 10
        assert qLG.n == lgB0 + NTT * 2 * 16

    return nc


_CACHE = {}


def kernel(L0_attn_x, L0_attn_W_qkv, L0_attn_b_qkv, L0_attn_W_o, L0_attn_b_o,
           L0_ffn_W1, L0_ffn_b1, L0_ffn_W2, L0_ffn_b2):
    import ml_dtypes
    x = np.ascontiguousarray(np.asarray(L0_attn_x, dtype=np.float32))
    wqkv_ = np.ascontiguousarray(np.asarray(L0_attn_W_qkv, dtype=np.float32))
    bqkv_ = np.ascontiguousarray(np.asarray(L0_attn_b_qkv, dtype=np.float32))
    wo_ = np.ascontiguousarray(np.asarray(L0_attn_W_o, dtype=np.float32))
    bo_ = np.ascontiguousarray(np.asarray(L0_attn_b_o, dtype=np.float32))
    w1t_ = np.ascontiguousarray(np.asarray(L0_ffn_W1, dtype=np.float32).T)
    b1_ = np.ascontiguousarray(np.asarray(L0_ffn_b1, dtype=np.float32))
    w2t_ = np.ascontiguousarray(np.asarray(L0_ffn_W2, dtype=np.float32).T)
    b2_ = np.ascontiguousarray(np.asarray(L0_ffn_b2, dtype=np.float32))
    ident_ = np.eye(128, dtype=ml_dtypes.bfloat16)

    if "nc" not in _CACHE:
        _CACHE["nc"] = build_nc()
    nc = _CACHE["nc"]

    in_maps = []
    for c in range(8):
        b, half = c // 2, c % 2
        xT_ = np.ascontiguousarray(x[b].T)
        in_maps.append({
            "xT": xT_,
            "xTq": np.ascontiguousarray(xT_[:, half * QO:(half + 1) * QO]),
            "x_own": np.ascontiguousarray(x[b, half * QO:(half + 1) * QO]),
            "wqkv": wqkv_, "bqkv": bqkv_, "wo": wo_, "bo": bo_,
            "w1t": w1t_, "b1": b1_, "w2t": w2t_, "b2": b2_,
            "ident_in": ident_,
        })
    res = run_bass_kernel_spmd(nc, in_maps, core_ids=list(range(8)))
    out = np.empty((B, S, D), dtype=np.float32)
    for c in range(8):
        b, half = c // 2, c % 2
        out[b, half * QO:(half + 1) * QO] = res.results[c]["y"]
    return out


# version 16
# speedup vs baseline: 1.6654x; 1.1152x over previous
import numpy as np
from contextlib import ExitStack

import concourse.bass as bass
import concourse.mybir as mybir
from concourse.bass_utils import run_bass_kernel_spmd
from concourse.alu_op_type import AluOpType as OP

F32 = mybir.dt.float32
BF16 = mybir.dt.bfloat16
I32 = mybir.dt.int32
AF = mybir.ActivationFunctionType
AX = mybir.AxisListType

B, S, D = 4, 4096, 1024
NH, HS = 16, 64
H = 4096
QO = 2048
NKT = S // 128
NTT = QO // 128
EPS_LN = 1e-5
EPS_Q = 1e-5
VSTRIDE = HS + 1
VTILE = NH * VSTRIDE

NEGM = 0
BQ = 16
BK = 24
S1C = 32
WS = 48
ST = 56
MQ2 = 64
C_T1, C_NT1, C_T2, C_NT2 = WS + 2, WS + 3, WS + 4, WS + 5
C_WQ1, C_WQ2 = WS + 6, WS + 7


QCH_OPS = 12


def emit_quant_chunk(vector, cDV, wf, wri, shm_ap, src_ap, dst_ap):
    from concourse.alu_op_type import AluOpType as OP
    qa = wf[:, 4096:4608]
    qb = wf[:, 4608:5120]
    qc = wf[:, 5120:5632]
    n = 0
    inst = vector.tensor_scalar(qa, src_ap, shm_ap, 256.0, OP.mult, OP.add)
    n += 1; cDV.inc(inst)
    inst = vector.tensor_copy(wri[:, :], qa)
    n += 1; cDV.inc(inst)
    inst = vector.tensor_copy(qb, wri[:, :])
    n += 1; cDV.inc(inst)
    inst = vector.tensor_sub(qa, qa, qb)
    n += 1; cDV.inc(inst)
    inst = vector.tensor_scalar(qc, qa, -0.5, None, OP.is_lt)
    n += 1; cDV.inc(inst)
    inst = vector.scalar_tensor_tensor(qc, qa, -1.5, qc, OP.is_lt, OP.add)
    n += 1; cDV.inc(inst)
    inst = vector.scalar_tensor_tensor(qc, qa, -2.5, qc, OP.is_lt, OP.add)
    n += 1; cDV.inc(inst)
    inst = vector.scalar_tensor_tensor(qb, qc, -1.0, qb, OP.mult, OP.add)
    n += 1; cDV.inc(inst)
    inst = vector.tensor_scalar(qc, qa, 0.5, None, OP.is_ge)
    n += 1; cDV.inc(inst)
    inst = vector.scalar_tensor_tensor(qc, qa, 1.5, qc, OP.is_ge, OP.add)
    n += 1; cDV.inc(inst)
    inst = vector.scalar_tensor_tensor(qc, qa, 2.5, qc, OP.is_ge, OP.add)
    n += 1; cDV.inc(inst)
    inst = vector.scalar_tensor_tensor(dst_ap, qb, -256.0, qc, OP.add, OP.add)
    n += 1; cDV.inc(inst)
    assert n == QCH_OPS
    return n


class Ctr:
    def __init__(self, sem, step=1):
        self.sem, self.step, self.n = sem, step, 0

    def inc(self, inst):
        inst.then_inc(self.sem, self.step)
        self.n += self.step
        return self.n


def build_nc(debug=False):
    nc = bass.Bass("TRN2", target_bir_lowering=False, debug=False,
                   detect_race_conditions=False)

    xT = nc.dram_tensor("xT", [D, S], F32, kind="ExternalInput").ap()
    xTq = nc.dram_tensor("xTq", [D, QO], F32, kind="ExternalInput").ap()
    x_own = nc.dram_tensor("x_own", [QO, D], F32, kind="ExternalInput").ap()
    wqkv = nc.dram_tensor("wqkv", [D, 3 * D], F32, kind="ExternalInput").ap()
    bqkv = nc.dram_tensor("bqkv", [3 * D], F32, kind="ExternalInput").ap()
    wo = nc.dram_tensor("wo", [D, D], F32, kind="ExternalInput").ap()
    bo = nc.dram_tensor("bo", [D], F32, kind="ExternalInput").ap()
    w1t = nc.dram_tensor("w1t", [D, H], F32, kind="ExternalInput").ap()
    b1 = nc.dram_tensor("b1", [H], F32, kind="ExternalInput").ap()
    w2t = nc.dram_tensor("w2t", [H, D], F32, kind="ExternalInput").ap()
    b2 = nc.dram_tensor("b2", [D], F32, kind="ExternalInput").ap()
    ident_in = nc.dram_tensor("ident_in", [128, 128], BF16, kind="ExternalInput").ap()
    y = nc.dram_tensor("y", [QO, D], F32, kind="ExternalOutput").ap()
    if debug:
        dbg_wsm = nc.dram_tensor("dbg_wsm", [128, 192], F32, kind="ExternalOutput").ap()
        dbg_x1 = nc.dram_tensor("dbg_x1", [QO, D], F32, kind="ExternalOutput").ap()
        dbg_wf = nc.dram_tensor("dbg_wf", [128, 5632], F32, kind="ExternalOutput").ap()
        dbg_wo = nc.dram_tensor("dbg_wo", [8, 128, D], BF16, kind="ExternalOutput").ap()
        dbg_q = nc.dram_tensor("dbg_q", [128, 8 * QO], BF16, kind="ExternalOutput").ap()
        dbg_k = nc.dram_tensor("dbg_k", [128, 8 * 4096], BF16, kind="ExternalOutput").ap()
        dbg_v = nc.dram_tensor("dbg_v", [128, NKT * VTILE], BF16, kind="ExternalOutput").ap()
        dbg_av = nc.dram_tensor("dbg_av", [128, 8 * QO], BF16, kind="ExternalOutput").ap()

    dscr = nc.dram_tensor("dscr", [1, 1024], F32, kind="Internal").ap()
    mscr = nc.dram_tensor("mscr", [16], F32, kind="Internal").ap()
    sscr = nc.dram_tensor("sscr", [8], F32, kind="Internal").ap()
    rscr = nc.dram_tensor("rscr", [128], F32, kind="Internal").ap()
    wo_bf = nc.dram_tensor("wo_bf", [8, 128, D], BF16, kind="Internal").ap()
    x1scr = nc.dram_tensor("x1scr", [QO, D], F32, kind="Internal").ap()

    with ExitStack() as ctx:
        e = ctx.enter_context
        karena = e(nc.sbuf_tensor("karena", [128, 8 * 4096], BF16))
        varena = e(nc.sbuf_tensor("varena", [128, NKT * VTILE], BF16))
        qarena = e(nc.sbuf_tensor("qarena", [128, 8 * QO], BF16))
        wkA = e(nc.sbuf_tensor("wkA", [128, 4096], BF16))
        wkB = e(nc.sbuf_tensor("wkB", [128, 5632], BF16))
        wf = e(nc.sbuf_tensor("wf", [128, 5632], F32))
        wsm = e(nc.sbuf_tensor("wsm", [128, 192], F32))
        wri = e(nc.sbuf_tensor("wri", [128, 512], I32))
        ident = e(nc.sbuf_tensor("ident", [128, 128], BF16))
        wsb = e(nc.sbuf_tensor("wsb", [128, 32], BF16))
        b2row = e(nc.sbuf_tensor("b2row", [1, 1024], BF16))
        s2invrow = e(nc.sbuf_tensor("s2invrow", [1, 128], BF16))

        psBIG = e(nc.psum_tensor("psBIG", [128, 2048], F32))
        psC = e(nc.psum_tensor("psC", [128, 1024], F32))
        psTa = e(nc.psum_tensor("psTa", [128, 512], BF16))
        psTb = e(nc.psum_tensor("psTb", [128, 512], BF16))

        qLD = Ctr(e(nc.semaphore("qLD")), 16)
        qLG = Ctr(e(nc.semaphore("qLG")), 16)
        qOUT = Ctr(e(nc.semaphore("qOUT")), 16)
        cPE = Ctr(e(nc.semaphore("cPE")))
        cP0 = Ctr(e(nc.semaphore("cP0")))
        cTN = Ctr(e(nc.semaphore("cTN")))
        cDV = Ctr(e(nc.semaphore("cDV")))
        cAC = Ctr(e(nc.semaphore("cAC")))

        block = e(nc.Block())

        @block.sync
        def _(sync):
            qLD.inc(sync.dma_start(out=ident[:], in_=ident_in[:]))
            for fc in range(8):
                qLD.inc(sync.dma_start(
                    out=wsm[:, BQ + fc:BQ + fc + 1],
                    in_=bqkv[fc * 128:(fc + 1) * 128].rearrange("(p a) -> p a", a=1)))
                qLD.inc(sync.dma_start(
                    out=wsm[:, BK + fc:BK + fc + 1],
                    in_=bqkv[1024 + fc * 128:1024 + (fc + 1) * 128].rearrange("(p a) -> p a", a=1)))
            inst = sync.dma_start(out=wf[:, 4096:5120],
                                  in_=bqkv[2048:3072].rearrange("(a n) -> a n", a=1)
                                  .partition_broadcast(128))
            qLD.inc(inst)
        LD_CONST = qLD.n

        @block.vector
        def _(vector):
            for inst in [vector.memset(wsm[:, NEGM:NEGM + 1], -16.0),
                         vector.memset(wsb[:, :], 0.0),
                         vector.memset(wsb[0:64, 0:1], 1.0),
                         vector.memset(wsb[64:128, 1:2], 1.0),
                         vector.memset(wsb[:, 2:3], 1.0),
                         vector.memset(wsm[:, 96:97], 1.0),
                         vector.memset(varena[:, :], 1.0)]:
                cDV.inc(inst)
                vector.wait_ge(cDV.sem, cDV.n)
        DV_CONST = cDV.n

        NW = 128
        lgP0 = qLG.n

        @block.gpsimd
        def _(g):
            for i in range(NW):
                if i >= 1:
                    g.wait_ge(cP0.sem, i * 2)
                if i < 64:
                    c, p = divmod(i, 8)
                    srcap = w1t[c * 128:(c + 1) * 128, p * 512:(p + 1) * 512]
                else:
                    c, p = divmod(i - 64, 2)
                    srcap = w2t[c * 128:(c + 1) * 128, p * 512:(p + 1) * 512]
                inst = g.dma_start(out=wf[:, 5120:5632], in_=srcap)
                qLG.inc(inst)

        p0b_state = {"i": 0}

        def p0b_emit_some(vector, n):
            for _ in range(n):
                i = p0b_state["i"]
                if i >= NW:
                    return
                p0b_state["i"] = i + 1
                vector.wait_ge(qLG.sem, lgP0 + (i + 1) * 16)
                inst = vector.reduce_sum(wsm[:, ST + 4:ST + 5], wf[:, 5120:5632],
                                         AX.X, apply_absolute_value=True)
                cP0.inc(inst)
                vector.wait_ge(cP0.sem, cP0.n)
                acc = wsm[:, ST + 5:ST + 6] if i < 64 else wsm[:, ST + 6:ST + 7]
                if i == 0 or i == 64:
                    inst = vector.tensor_copy(acc, wsm[:, ST + 4:ST + 5])
                else:
                    inst = vector.tensor_add(acc, acc, wsm[:, ST + 4:ST + 5])
                cP0.inc(inst)
                vector.wait_ge(cP0.sem, cP0.n)

        ld0p = qLD.n
        stage_owner = [None, None]
        nJob = [0]
        Psync, Pac, Pdv, Ppe = [], [], [], []
        quarter_pred = {0: None, 1: None, 2: None, 3: None}
        nAC = [cAC.n]
        nDV = [cDV.n]
        nPE = [cPE.n]
        wr_g = wqkv.rearrange("(c p) n -> p c n", p=128)
        xr_kv = xT.rearrange("(c p) n -> p c n", p=128)
        xr_q = xTq.rearrange("(c p) n -> p c n", p=128)
        bufree = [None, None]
        wkA_free = [None]
        wkB_free = [None]

        def add_dma(src):
            j = nJob[0]
            nJob[0] += 1
            sidx = j % 2
            prev = stage_owner[sidx]
            Psync.append((prev, src,
                          wf[:, sidx * 2048:sidx * 2048 + 2048]
                          .rearrange("p (c n) -> p c n", n=512)))
            return sidx, ld0p + (j + 1) * 16

        def add_wjob(wcol0, fh, wh, dst_t, free_pe):
            src = wr_g[:, wh * 4:(wh + 1) * 4,
                       wcol0 + fh * 512:wcol0 + fh * 512 + 512]
            sidx, dcnt = add_dma(src)
            nDV[0] += 1
            Pdv.append(("wcast", dcnt, free_pe,
                        wf[:, sidx * 2048:sidx * 2048 + 2048],
                        dst_t[:, wh * 2048:(wh + 1) * 2048]))
            stage_owner[sidx] = ("dv", nDV[0])
            return nDV[0]

        def add_xjob(xr, t, h, dst_ap, free_pe):
            src = xr[:, h * 4:(h + 1) * 4, t * 512:t * 512 + 512]
            sidx, dcnt = add_dma(src)
            nAC[0] += 1
            Pac.append((dcnt, free_pe,
                        wf[:, sidx * 2048:sidx * 2048 + 2048], dst_ap))
            stage_owner[sidx] = ("ac", nAC[0])
            return nAC[0]

        def add_group(wac, wdv, g, mms):
            nPE[0] += 1
            Ppe.append((wac, wdv, quarter_pred[g], mms))
            return nPE[0]

        def build_kv(sub, wcol0):
            xc, wc, grp = {}, {}, {}
            pre = [("W", (0, 0)), ("X", (0, 0)), ("W", (0, 1)), ("X", (0, 1)),
                   ("W", (1, 0)), ("X", (1, 0)), ("W", (1, 1)), ("X", (1, 1))]
            for kind, arg in pre:
                if kind == "W":
                    fh, wh = arg
                    wc[(fh, wh)] = add_wjob(
                        wcol0, fh, wh, wkA if fh == 0 else wkB,
                        (wkA_free if fh == 0 else wkB_free)[0])
                else:
                    t, h = arg
                    dst = qarena[:, (t % 2) * 4096 + h * 2048:
                                 (t % 2) * 4096 + h * 2048 + 2048]
                    xc[(t, h)] = add_xjob(xr_kv, t, h, dst, bufree[t % 2])
            for t in range(8):
                xbuf = qarena[:, (t % 2) * 4096:(t % 2) * 4096 + 4096]
                for q in range(8):
                    fh, g = q // 4, q % 4
                    wk = wkA if fh == 0 else wkB
                    ps = psBIG[:, g * 512:g * 512 + 512]
                    mms = []
                    for dc in range(8):
                        if sub == "K":
                            mms.append((wk[:, dc * 512 + g * 128:
                                           dc * 512 + g * 128 + 128],
                                        xbuf[:, dc * 512:dc * 512 + 512], ps))
                        else:
                            mms.append((xbuf[:, dc * 512 + g * 128:
                                             dc * 512 + g * 128 + 128],
                                        wk[:, dc * 512:dc * 512 + 512], ps))
                    gc = add_group(xc[(t, 1)], wc[(fh, 1)], g, mms)
                    grp[(t, q)] = gc
                    nDV[0] += 1
                    fc = fh * 4 + g
                    if sub == "K":
                        Pdv.append(("evacKQ", gc, ps,
                                    karena[:, fc * 4096 + t * 512:
                                           fc * 4096 + t * 512 + 512],
                                    BK + fc))
                    else:
                        kt = t * 4 + g
                        h0 = fh * 8
                        dst = varena[:, kt * VTILE:(kt + 1) * VTILE].rearrange(
                            "p (h v) -> p h v", v=VSTRIDE)[:, h0:h0 + 8, 0:64]
                        Pdv.append(("evacV", gc,
                                    ps.rearrange("p (h v) -> p h v", v=64),
                                    dst,
                                    wf[:, 4096 + fh * 512:4096 + fh * 512 + 512]
                                    .rearrange("p (h v) -> p h v", v=64)))
                    quarter_pred[g] = nDV[0]
                Pdv.append(("p0b", 5))
                bufree[t % 2] = grp[(t, 7)]
                if t + 2 <= 7:
                    for h in (0, 1):
                        dst = qarena[:, ((t + 2) % 2) * 4096 + h * 2048:
                                     ((t + 2) % 2) * 4096 + h * 2048 + 2048]
                        xc[(t + 2, h)] = add_xjob(xr_kv, t + 2, h, dst,
                                                  bufree[t % 2])
            wkA_free[0] = grp[(7, 3)]
            wkB_free[0] = grp[(7, 7)]

        def build_q():
            for fh in range(2):
                xc, wc, grp = {}, {}, {}
                wc[0] = add_wjob(0, fh, 0, wkA, wkA_free[0])
                xc[(0, 0)] = add_xjob(xr_q, 0, 0, wkB[:, 0:2048], wkB_free[0])
                wc[1] = add_wjob(0, fh, 1, wkA, wkA_free[0])
                xc[(0, 1)] = add_xjob(xr_q, 0, 1, wkB[:, 2048:4096],
                                      wkB_free[0])
                for t in range(4):
                    for g in range(4):
                        ps = psBIG[:, g * 512:g * 512 + 512]
                        mms = [(wkA[:, dc * 512 + g * 128:
                                    dc * 512 + g * 128 + 128],
                                wkB[:, dc * 512:dc * 512 + 512], ps)
                               for dc in range(8)]
                        gc = add_group(xc[(t, 1)], wc[1], g, mms)
                        grp[(t, g)] = gc
                        nDV[0] += 1
                        fc = fh * 4 + g
                        Pdv.append(("evacKQ", gc, ps,
                                    qarena[:, fc * 2048 + t * 512:
                                           fc * 2048 + t * 512 + 512],
                                    BQ + fc))
                        quarter_pred[g] = nDV[0]
                    Pdv.append(("p0b", 6))
                    if t + 1 <= 3:
                        for h in (0, 1):
                            xc[(t + 1, h)] = add_xjob(
                                xr_q, t + 1, h,
                                wkB[:, h * 2048:h * 2048 + 2048], grp[(t, 3)])
                wkA_free[0] = grp[(3, 3)]
                wkB_free[0] = grp[(3, 3)]

        build_kv("K", 1024)
        build_kv("V", 2048)
        build_q()

        @block.sync
        def _(sync):
            for prev, src, dstap in Psync:
                if prev is not None:
                    eng, v = prev
                    sync.wait_ge(cAC.sem if eng == "ac" else cDV.sem, v)
                qLD.inc(sync.dma_start(out=dstap, in_=src))
        assert qLD.n == ld0p + 60 * 16

        @block.scalar
        def _(scalar):
            for dcnt, free_pe, srcap, dstap in Pac:
                scalar.wait_ge(qLD.sem, dcnt)
                if free_pe is not None:
                    scalar.wait_ge(cPE.sem, free_pe)
                cAC.inc(scalar.activation(dstap, srcap, AF.Copy))
        assert cAC.n == 48

        @block.vector
        def _(vector):
            for it in Pdv:
                if it[0] == "wcast":
                    _, dcnt, free_pe, srcap, dstap = it
                    vector.wait_ge(qLD.sem, dcnt)
                    if free_pe is not None:
                        vector.wait_ge(cPE.sem, free_pe)
                    cDV.inc(vector.tensor_copy(dstap, srcap))
                elif it[0] == "evacKQ":
                    _, gc, ps, dstap, bcol = it
                    vector.wait_ge(cPE.sem, gc)
                    cDV.inc(vector.tensor_scalar(dstap, ps,
                                                 wsm[:, bcol:bcol + 1],
                                                 None, OP.add))
                elif it[0] == "evacV":
                    _, gc, psv, dstap, bv = it
                    vector.wait_ge(cPE.sem, gc)
                    cDV.inc(vector.tensor_tensor(dstap, psv, bv, OP.add))
                else:
                    p0b_emit_some(vector, it[1])
        assert cDV.n == nDV[0]

        @block.tensor
        def _(t):
            for wac, wdv, qp, mms in Ppe:
                t.wait_ge(cAC.sem, wac)
                t.wait_ge(cDV.sem, wdv)
                if qp is not None:
                    t.wait_ge(cDV.sem, qp)
                for i, (lh, rh_, ps) in enumerate(mms):
                    inst = t.matmul(ps, lh, rh_, start=(i == 0), stop=(i == 7))
                cPE.inc(inst)
        assert cPE.n == 160

        @block.vector
        def _(vector):
            p0b_emit_some(vector, NW)
        assert p0b_state["i"] == NW
        assert cP0.n == NW * 2
        PE_PROJ = cPE.n
        DV_PROJ = cDV.n

        @block.tensor
        def _(t):
            t.wait_ge(cP0.sem, NW * 2)
            t.matmul(psC[0:1, 0:1], wsm[:, 96:97], wsm[:, ST + 5:ST + 6],
                     start=True, stop=True)
            inst = t.matmul(psC[0:1, 512:513], wsm[:, 96:97], wsm[:, ST + 6:ST + 7],
                            start=True, stop=True)
            cPE.inc(inst)
        PE_W0 = cPE.n

        @block.vector
        def _(vector):
            vector.wait_ge(cPE.sem, PE_W0)
            for inst in [
                vector.tensor_copy(wsm[0:1, ST:ST + 1], psC[0:1, 0:1]),
                vector.tensor_copy(wsm[0:1, ST + 1:ST + 2], psC[0:1, 512:513]),
                vector.tensor_scalar(wsm[0:1, ST:ST + 1], wsm[0:1, ST:ST + 1],
                                     1.0 / (D * H), EPS_Q, OP.mult, OP.max),
                vector.tensor_scalar(wsm[0:1, ST + 1:ST + 2], wsm[0:1, ST + 1:ST + 2],
                                     1.0 / (D * H), EPS_Q, OP.mult, OP.max),
                vector.tensor_scalar_mul(wsm[0:1, ST + 2:ST + 3], wsm[0:1, ST:ST + 1], 0.5),
                vector.tensor_scalar_mul(wsm[0:1, ST + 3:ST + 4], wsm[0:1, ST:ST + 1], -0.5),
                vector.tensor_scalar_mul(wsm[0:1, ST + 4:ST + 5], wsm[0:1, ST + 1:ST + 2], 0.5),
                vector.tensor_scalar_mul(wsm[0:1, ST + 5:ST + 6], wsm[0:1, ST + 1:ST + 2], -0.5),
                vector.tensor_scalar_mul(wsm[0:1, ST + 6:ST + 7], wsm[0:1, ST:ST + 1], 1.0 / 127.0),
                vector.tensor_scalar_mul(wsm[0:1, ST + 7:ST + 8], wsm[0:1, ST + 1:ST + 2],
                                         1.0 / 127.0),
            ]:
                cDV.inc(inst)
                vector.wait_ge(cDV.sem, cDV.n)
        DV_WS = cDV.n

        @block.gpsimd
        def _(g):
            g.wait_ge(cDV.sem, DV_WS)
            inst = g.dma_start(out=sscr.rearrange("(a n) -> a n", a=1), in_=wsm[0:1, ST:ST + 8])
            qLG.inc(inst)
            g.wait_ge(qLG.sem, qLG.n)
            inst = g.dma_start(out=wsm[:, WS:WS + 8],
                               in_=sscr.rearrange("(a n) -> a n", a=1).partition_broadcast(128))
            qLG.inc(inst)
        LG_WS = qLG.n

        pe0 = cPE.n
        ac0 = cAC.n
        dv0 = cDV.n
        lg0 = qLG.n
        ldA0 = qLD.n
        steps = [(h, qc) for h in range(NH) for qc in range(2)]
        NSTEP = len(steps)
        GKT = NSTEP * NKT

        def st_inc(gkt):
            return pe0 + (gkt + 1 if gkt < 2 else 2 * gkt)

        def pv_inc(gkt):
            return pe0 + 2 * gkt + 3 if gkt < GKT - 1 else pe0 + 2 * gkt + 2

        def exp_inc(gkt):
            return ac0 + (gkt // NKT) * 33 + (gkt % NKT) + 1

        def ac_copy(m):
            return ac0 + (m + 1) * 33

        def r_inc(m):
            return dv0 + m * 2 + 1

        def tt_inc(m):
            return dv0 + m * 2 + 2

        def lg_back(m):
            return lg0 + (m * 3 + 2) * 16

        def lg_move(m):
            return lg0 + (m * 3 + 3) * 16

        side_dma = []
        side_dve = []

        def w1_dma(i):
            c, p = divmod(i, 4)
            def f(sync):
                sync.wait_ge(cPE.sem, st_inc((4 * c + 4) * NKT - 1))
                if i >= 1:
                    sync.wait_ge(cTN.sem, 2 * i)
                inst = sync.dma_start(out=wf[:, 4096:5120],
                                      in_=w1t[c * 128:(c + 1) * 128,
                                              p * 1024:(p + 1) * 1024])
                qLD.inc(inst)
            return f

        def w1_dve(i):
            c, p = divmod(i, 4)
            def f(vector):
                if i == 0:
                    vector.wait_ge(qLG.sem, LG_WS)
                vector.wait_ge(qLD.sem, ldA0 + (i + 1) * 16)
                inst = vector.tensor_scalar(wf[:, 1024:2048], wf[:, 4096:5120],
                                            wsm[:, C_NT1:C_NT1 + 1], None, OP.is_lt)
                cTN.inc(inst)
                inst = vector.scalar_tensor_tensor(
                    karena[:, c * 4096 + p * 1024:c * 4096 + p * 1024 + 1024],
                    wf[:, 4096:5120], wsm[:, C_T1:C_T1 + 1], wf[:, 1024:2048],
                    OP.is_gt, OP.subtract)
                cTN.inc(inst)
            return f

        for i in range(32):
            side_dma.append(w1_dma(i))
            side_dve.append(w1_dve(i))

        def wo_in(c):
            def f(sync):
                sync.wait_ge(cTN.sem, 64 + c)
                inst = sync.dma_start(out=wf[:, 4096:5120], in_=wo[c * 128:(c + 1) * 128, :])
                qLD.inc(inst)
            return f

        def wo_out(c):
            def f(sync):
                sync.wait_ge(cTN.sem, 64 + c + 1)
                inst = sync.dma_start(out=wo_bf[c], in_=wkB[:, 3072:4096])
                qLD.inc(inst)
            return f

        def wo_cast(c):
            def f(vector):
                vector.wait_ge(qLD.sem, ldA0 + (32 + 2 * c + 1) * 16)
                if c >= 1:
                    vector.wait_ge(qLD.sem, ldA0 + (32 + 2 * c) * 16)
                inst = vector.tensor_copy(wkB[:, 3072:4096], wf[:, 4096:5120])
                cTN.inc(inst)
            return f

        for c in range(8):
            side_dma.append(wo_in(c))
            side_dma.append(wo_out(c))
            side_dve.append(wo_cast(c))

        def b1_dma(j):
            def f(sync):
                sync.wait_ge(cTN.sem, 72 + j)
                inst = sync.dma_start(out=wf[:, 4096:4608],
                                      in_=b1[j * 512:(j + 1) * 512]
                                      .rearrange("(a n) -> a n", a=1).partition_broadcast(128))
                qLD.inc(inst)
            return f

        def b1_cast(j):
            def f(vector):
                vector.wait_ge(qLD.sem, ldA0 + (48 + j + 1) * 16)
                inst = vector.tensor_copy(wkA[:, j * 512:(j + 1) * 512], wf[:, 4096:4608])
                cTN.inc(inst)
            return f

        for j in range(8):
            side_dma.append(b1_dma(j))
            side_dve.append(b1_cast(j))

        def b2_dma(sync):
            sync.wait_ge(cTN.sem, 80)
            inst = sync.dma_start(out=wf[0:1, 4096:5120],
                                  in_=b2.rearrange("(a n) -> a n", a=1))
            qLD.inc(inst)

        def b2_cast(vector):
            vector.wait_ge(qLD.sem, ldA0 + 57 * 16)
            inst = vector.tensor_copy(b2row[0:1, :], wf[0:1, 4096:5120])
            cTN.inc(inst)

        def bo_dma(sync):
            sync.wait_ge(cTN.sem, 64)
            inst = sync.dma_start(out=wf[:, 1024:2048],
                                  in_=bo.rearrange("(a n) -> a n", a=1).partition_broadcast(128))
            qLD.inc(inst)

        side_dma.append(b2_dma)
        side_dve.append(b2_cast)
        side_dma.append(bo_dma)
        N_SIDE_DMA = len(side_dma)
        N_SIDE_DVE = len(side_dve)

        def emit_S(t, gkt):
            m, kt = divmod(gkt, NKT)
            h, qc = steps[m]
            ch, rh = h // 2, (h % 2) * 64
            stb = (gkt % 2) * 1024
            lhsK = karena[rh:rh + 64, ch * 4096 + kt * 128:ch * 4096 + kt * 128 + 128]
            for half in range(2):
                inst = t.matmul(psBIG[:, stb + half * 512:stb + half * 512 + 512],
                                lhsK,
                                qarena[rh:rh + 64,
                                       ch * 2048 + qc * 1024 + half * 512:
                                       ch * 2048 + qc * 1024 + half * 512 + 512],
                                start=True, stop=True)
            cPE.inc(inst)
            assert cPE.n == st_inc(gkt)

        def emit_PV(t, gkt):
            m, kt = divmod(gkt, NKT)
            h, qc = steps[m]
            t.wait_ge(cAC.sem, exp_inc(gkt))
            if kt == 0 and m >= 1:
                t.wait_ge(cAC.sem, ac_copy(m - 1))
            lhsV = varena[:, kt * VTILE + h * VSTRIDE:kt * VTILE + h * VSTRIDE + VSTRIDE]
            pt = wkB[:, (gkt % 2) * 1024:(gkt % 2) * 1024 + 1024]
            for half in range(2):
                inst = t.matmul(psC[0:65, half * 512:half * 512 + 512],
                                lhsV, pt[:, half * 512:half * 512 + 512],
                                start=(kt == 0), stop=(kt == NKT - 1))
            cPE.inc(inst)
            assert cPE.n == pv_inc(gkt)

        @block.tensor
        def _(t):
            t.wait_ge(cDV.sem, DV_WS)
            t.wait_ge(cDV.sem, DV_PROJ)
            emit_S(t, 0)
            emit_S(t, 1)
            for gkt in range(GKT - 2):
                emit_PV(t, gkt)
                emit_S(t, gkt + 2)
            emit_PV(t, GKT - 2)
            emit_PV(t, GKT - 1)

        @block.sync
        def _(sync):
            for f in side_dma:
                f(sync)
        LD_SIDE = qLD.n
        assert LD_SIDE == ldA0 + N_SIDE_DMA * 16

        @block.scalar
        def _(scalar):
            for m, (h, qc) in enumerate(steps):
                for kt in range(NKT):
                    gkt = m * NKT + kt
                    stb = (gkt % 2) * 1024
                    scalar.wait_ge(cPE.sem, st_inc(gkt))
                    if gkt >= 2:
                        scalar.wait_ge(cPE.sem, pv_inc(gkt - 2))
                    inst = scalar.activation(wkB[:, (gkt % 2) * 1024:(gkt % 2) * 1024 + 1024],
                                             psBIG[:, stb:stb + 1024], AF.Exp,
                                             bias=wsm[:, NEGM:NEGM + 1], scale=0.125)
                    cAC.inc(inst)
                    assert cAC.n == exp_inc(gkt)
                scalar.wait_ge(cPE.sem, pv_inc(m * NKT + NKT - 1))
                if m >= 1:
                    scalar.wait_ge(cDV.sem, tt_inc(m - 1))
                inst = scalar.activation(wf[:, 0:1024], psC[:, 0:1024], AF.Copy)
                cAC.inc(inst)
                assert cAC.n == ac_copy(m)

        @block.vector
        def _(vector):
            popped = 0
            for m, (h, qc) in enumerate(steps):
                vector.wait_ge(cAC.sem, ac_copy(m))
                inst = vector.reciprocal(wf[64:65, 3072:4096], wf[64:65, 0:1024])
                cDV.inc(inst)
                assert cDV.n == r_inc(m)
                vector.wait_ge(qLG.sem, lg_back(m))
                if m >= 1:
                    vector.wait_ge(qLG.sem, lg_move(m - 1))
                inst = vector.tensor_tensor(wkB[0:64, 2048:3072], wf[0:64, 0:1024],
                                            wf[0:64, 3072:4096], OP.mult)
                cDV.inc(inst)
                assert cDV.n == tt_inc(m)
                if m >= 4 and popped < 28:
                    side_dve[popped](vector)
                    popped += 1
            for f in side_dve[popped:]:
                f(vector)
        assert cTN.n == 81

        @block.gpsimd
        def _(g):
            for m, (h, qc) in enumerate(steps):
                ch, rh = h // 2, (h % 2) * 64
                g.wait_ge(cDV.sem, r_inc(m))
                inst = g.dma_start(out=dscr[:], in_=wf[64:65, 3072:4096])
                qLG.inc(inst)
                g.wait_ge(qLG.sem, qLG.n)
                if m >= 1:
                    g.wait_ge(cDV.sem, tt_inc(m - 1))
                inst = g.dma_start(out=wf[0:64, 3072:4096], in_=dscr.partition_broadcast(64))
                qLG.inc(inst)
                g.wait_ge(cDV.sem, tt_inc(m))
                inst = g.dma_start(
                    out=qarena[rh:rh + 64, ch * 2048 + qc * 1024:ch * 2048 + qc * 1024 + 1024],
                    in_=wkB[0:64, 2048:3072])
                qLG.inc(inst)

        assert cPE.n == pe0 + NSTEP * NKT * 2
        assert cAC.n == ac0 + NSTEP * 33
        assert cDV.n == dv0 + NSTEP * 2
        assert qLG.n == lg0 + NSTEP * 3 * 16
        PE_ATT = cPE.n
        LG_ATT = qLG.n

        if debug:
            @block.sync
            def _(sync):
                sync.wait_ge(cPE.sem, PE_ATT)
                sync.wait_ge(qLG.sem, LG_ATT)
                qLD.inc(sync.dma_start(out=dbg_k[:], in_=karena[:, :]))
                qLD.inc(sync.dma_start(out=dbg_v[:], in_=varena[:, :]))
                qLD.inc(sync.dma_start(out=dbg_av[:], in_=qarena[:, :]))
        DBG_DUMP = qLD.n

        lg0 = qLG.n
        dv0 = cDV.n

        @block.gpsimd
        def _(g):
            g.wait_ge(cPE.sem, PE_ATT)
            g.wait_ge(cTN.sem, 81)
            if debug:
                g.wait_ge(qLD.sem, DBG_DUMP)
            for i in range(32):
                if i >= 1:
                    g.wait_ge(cDV.sem, dv0 + i * 2)
                inst = g.dma_start(out=wf[:, 4096:5120],
                                   in_=w2t[i * 128:(i + 1) * 128, :])
                qLG.inc(inst)

        @block.vector
        def _(vector):
            for i in range(32):
                vector.wait_ge(qLG.sem, lg0 + (i + 1) * 16)
                inst = vector.tensor_scalar(wf[:, 1024:2048] if False else wf[:, 2048:3072],
                                            wf[:, 4096:5120],
                                            wsm[:, C_NT2:C_NT2 + 1], None, OP.is_lt)
                cDV.inc(inst)
                inst = vector.scalar_tensor_tensor(
                    varena[:, i * 1024:(i + 1) * 1024],
                    wf[:, 4096:5120], wsm[:, C_T2:C_T2 + 1], wf[:, 2048:3072],
                    OP.is_gt, OP.subtract)
                cDV.inc(inst)
        assert cDV.n == dv0 + 64
        qLG.n = lg0 + 32 * 16
        DV_W = cDV.n
        LG_W = qLG.n

        ld0 = qLD.n
        lg0 = qLG.n
        pe0 = cPE.n
        dv0 = cDV.n
        ac0 = cAC.n

        def pA_xdma(t):
            return ld0 + (t * 3 + 1) * 16

        def pA_wo(t, dh):
            return ld0 + (t * 3 + 2 + dh) * 16

        def pA_pe(t, k):
            return pe0 + t * 3 + k

        def pA_dv(t, k):
            return dv0 + t * 40 + k

        def pA_ac(t, k):
            return ac0 + t * 2 + k

        def pA_x1dma(t):
            return lg0 + (t + 1) * 16

        @block.sync
        def _(sync):
            sync.wait_ge(qLG.sem, LG_ATT)
            for t in range(NTT):
                if t >= 1:
                    sync.wait_ge(cDV.sem, pA_dv(t - 1, 1))
                inst = sync.dma_start(out=wf[:, 0:1024],
                                      in_=x_own[t * 128:(t + 1) * 128, :])
                qLD.inc(inst)
                for dh in range(2):
                    if t * 2 + dh >= 1:
                        pt_, pdh = divmod(t * 2 + dh - 1, 2)
                        sync.wait_ge(cPE.sem, pA_pe(pt_, pdh + 1))
                    inst = sync.dma_start(
                        out=wkB[:, 0:4096].rearrange("p (c x) -> p c x", x=512),
                        in_=wo_bf[:, :, dh * 512:dh * 512 + 512].rearrange("c p x -> p c x"))
                    qLD.inc(inst)

        @block.tensor
        def _(t_):
            t_.wait_ge(qLG.sem, LG_ATT)
            if debug:
                t_.wait_ge(qLD.sem, DBG_DUMP)
            for t in range(NTT):
                ps = psBIG[:, (t % 2) * 1024:(t % 2) * 1024 + 1024]
                if t >= 2:
                    t_.wait_ge(cDV.sem, pA_dv(t - 2, 2))
                for dh in range(2):
                    t_.wait_ge(qLD.sem, pA_wo(t, dh))
                    for c in range(8):
                        inst = t_.matmul(ps[:, dh * 512:dh * 512 + 512],
                                         qarena[:, c * 2048 + t * 128:c * 2048 + t * 128 + 128],
                                         wkB[:, c * 512:c * 512 + 512],
                                         start=(c == 0), stop=(c == 7))
                    cPE.inc(inst)
                    assert cPE.n == pA_pe(t, dh + 1)
                t_.wait_ge(cDV.sem, pA_dv(t, 38))
                if t >= 1:
                    t_.wait_ge(cDV.sem, pA_dv(t - 1, 40))
                for dc in range(8):
                    dst = psTa if dc < 4 else psTb
                    inst = t_.transpose(dst[:, (dc % 4) * 128:(dc % 4) * 128 + 128],
                                        wkB[:, 4096 + dc * 128:4096 + dc * 128 + 128],
                                        ident[:])
                cPE.inc(inst)
                assert cPE.n == pA_pe(t, 3)

        @block.vector
        def _(vector):
            vector.wait_ge(qLD.sem, LD_SIDE)
            for t in range(NTT):
                k = 0
                vector.wait_ge(qLD.sem, pA_xdma(t))
                inst = vector.tensor_add(wf[:, 2048:3072], wf[:, 0:1024], wf[:, 1024:2048])
                k += 1; cDV.inc(inst)
                vector.wait_ge(cPE.sem, pA_pe(t, 2))
                ps = psBIG[:, (t % 2) * 1024:(t % 2) * 1024 + 1024]
                inst = vector.tensor_add(wf[:, 2048:3072], ps, wf[:, 2048:3072])
                k += 1; cDV.inc(inst)
                inst = vector.reduce_sum(wsm[:, ST:ST + 1], wf[:, 2048:3072], AX.X)
                k += 1; cDV.inc(inst); vector.wait_ge(cDV.sem, cDV.n)
                inst = vector.tensor_scalar_mul(wsm[:, ST + 1:ST + 2], wsm[:, ST:ST + 1], -1.0 / D)
                k += 1; cDV.inc(inst); vector.wait_ge(cDV.sem, cDV.n)
                if t >= 1:
                    vector.wait_ge(qLG.sem, pA_x1dma(t - 1))
                inst = vector.tensor_scalar(wf[:, 3072:4096], wf[:, 2048:3072],
                                            wsm[:, ST + 1:ST + 2], None, OP.add)
                k += 1; cDV.inc(inst)
                vector.wait_ge(cAC.sem, pA_ac(t, 1))
                inst = vector.reduce_sum(wsm[:, ST + 2:ST + 3], wf[:, 4096:5120], AX.X)
                k += 1; cDV.inc(inst); vector.wait_ge(cDV.sem, cDV.n)
                inst = vector.tensor_scalar(wsm[:, ST + 3:ST + 4], wsm[:, ST + 2:ST + 3],
                                            1.0 / D, EPS_LN, OP.mult, OP.add)
                k += 1; cDV.inc(inst)
                vector.wait_ge(cAC.sem, pA_ac(t, 2))
                inst = vector.reciprocal(wsm[:, ST + 5:ST + 6], wsm[:, ST + 4:ST + 5])
                k += 1; cDV.inc(inst); vector.wait_ge(cDV.sem, cDV.n)
                inst = vector.tensor_scalar(wf[:, 3072:4096], wf[:, 3072:4096],
                                            wsm[:, ST + 5:ST + 6], None, OP.mult)
                k += 1; cDV.inc(inst)
                inst = vector.reduce_max(wsm[:, ST + 6:ST + 7], wf[:, 3072:4096], AX.X,
                                         apply_absolute_value=True)
                k += 1; cDV.inc(inst); vector.wait_ge(cDV.sem, cDV.n)
                inst = vector.tensor_scalar_max(wsm[:, ST + 6:ST + 7], wsm[:, ST + 6:ST + 7], EPS_Q)
                k += 1; cDV.inc(inst); vector.wait_ge(cDV.sem, cDV.n)
                inst = vector.reciprocal(wsm[:, ST + 7:ST + 8], wsm[:, ST + 6:ST + 7])
                k += 1; cDV.inc(inst); vector.wait_ge(cDV.sem, cDV.n)
                inst = vector.tensor_scalar_mul(wsm[:, ST + 7:ST + 8], wsm[:, ST + 7:ST + 8], 127.0)
                k += 1; cDV.inc(inst); vector.wait_ge(cDV.sem, cDV.n)
                inst = vector.tensor_tensor(wsm[:, S1C + t:S1C + t + 1],
                                            wsm[:, ST + 6:ST + 7],
                                            wsm[:, C_WQ1:C_WQ1 + 1], OP.mult)
                k += 1; cDV.inc(inst)
                for qch in range(2):
                    k += emit_quant_chunk(vector, cDV, wf, wri,
                                          wsm[:, ST + 7:ST + 8],
                                          wf[:, 3072 + qch * 512:3072 + qch * 512 + 512],
                                          wkB[:, 4096 + qch * 512:4096 + qch * 512 + 512])
                assert k == 38
                vector.wait_ge(cPE.sem, pA_pe(t, 3))
                inst = vector.tensor_copy(
                    qarena.ap().rearrange("p (c n) -> p c n", n=2048)[:, 0:4, t * 128:t * 128 + 128],
                    psTa.ap().rearrange("p (c n) -> p c n", n=128))
                k += 1; cDV.inc(inst)
                inst = vector.tensor_copy(
                    qarena.ap().rearrange("p (c n) -> p c n", n=2048)[:, 4:8, t * 128:t * 128 + 128],
                    psTb.ap().rearrange("p (c n) -> p c n", n=128))
                k += 1; cDV.inc(inst)
                assert cDV.n == pA_dv(t, 40)

        @block.scalar
        def _(scalar):
            for t in range(NTT):
                scalar.wait_ge(cDV.sem, pA_dv(t, 5))
                inst = scalar.activation(wf[:, 4096:5120], wf[:, 3072:4096], AF.Square)
                cAC.inc(inst)
                scalar.wait_ge(cDV.sem, pA_dv(t, 7))
                inst = scalar.activation(wsm[:, ST + 4:ST + 5], wsm[:, ST + 3:ST + 4], AF.Sqrt)
                cAC.inc(inst)

        @block.gpsimd
        def _(g):
            for t in range(NTT):
                g.wait_ge(cDV.sem, pA_dv(t, 38))
                inst = g.dma_start(out=x1scr[t * 128:(t + 1) * 128, :], in_=wf[:, 3072:4096])
                qLG.inc(inst)

        qLD.n = ld0 + NTT * 3 * 16
        assert cPE.n == pe0 + NTT * 3
        assert cDV.n == dv0 + NTT * 40
        assert cAC.n == ac0 + NTT * 2
        X1DMA = [pA_x1dma(t) for t in range(NTT)]
        qLG.n = lg0 + NTT * 16
        DV_A = cDV.n

        ld0 = qLD.n
        pe0 = cPE.n
        dv0 = cDV.n
        ac0 = cAC.n

        lgB0 = qLG.n

        def pB_pe(t, k):
            return pe0 + t * 25 + k

        def pB_lg(t, k):
            return lgB0 + (t * 2 + k) * 16

        def pB_dv(t, k):
            return dv0 + t * 127 + k

        def pB_ac(t, k):
            return ac0 + t * 10 + k

        def pB_x1(t):
            return ld0 + (t + 1) * 16

        def pB_y(t):
            return (t + 1) * 16

        @block.sync
        def _(sync):
            for t in range(NTT):
                sync.wait_ge(cDV.sem, pB_dv(t, 110))
                sync.wait_ge(qLG.sem, X1DMA[t])
                inst = sync.dma_start(out=wf[:, 4096:5120], in_=x1scr[t * 128:(t + 1) * 128, :])
                qLD.inc(inst)

        @block.tensor
        def _(t_):
            t_.wait_ge(cDV.sem, DV_A)
            for t in range(NTT):
                for jh in range(2):
                    if jh == 1:
                        t_.wait_ge(cDV.sem, pB_dv(t, 4))
                    elif t >= 1:
                        t_.wait_ge(cDV.sem, pB_dv(t - 1, 8))
                    for dc in range(8):
                        lhs = qarena[:, dc * 2048 + t * 128:dc * 2048 + t * 128 + 128]
                        for sl in range(4):
                            inst = t_.matmul(
                                psBIG[:, sl * 512:sl * 512 + 512], lhs,
                                karena[:, dc * 4096 + jh * 2048 + sl * 512:
                                       dc * 4096 + jh * 2048 + sl * 512 + 512],
                                start=(dc == 0), stop=(dc == 7))
                            if dc == 7:
                                cPE.inc(inst)
                    assert cPE.n == pB_pe(t, jh * 4 + 4)
                t_.wait_ge(cDV.sem, pB_dv(t, 110))
                if t >= 1:
                    t_.wait_ge(cDV.sem, pB_dv(t - 1, 120))

                def _tr(grp):
                    if grp >= 2:
                        t_.wait_ge(cDV.sem, pB_dv(t, 112 + grp - 2))
                    elif t >= 1:
                        t_.wait_ge(cDV.sem, pB_dv(t - 1, 118 + grp))
                    dstp = psTa if grp % 2 == 0 else psTb
                    for i in range(4):
                        hc = grp * 4 + i
                        inst2 = t_.transpose(
                            dstp[:, i * 128:i * 128 + 128],
                            wkB[:, hc * 128:hc * 128 + 128], ident[:])
                    cPE.inc(inst2)

                def _mm(grp):
                    t_.wait_ge(cDV.sem, pB_dv(t, 112 + grp))
                    for i in range(4):
                        hc = grp * 4 + i
                        lhs = wkB[:, 4096 + (grp % 2) * 512 + i * 128:
                                  4096 + (grp % 2) * 512 + i * 128 + 128]
                        for half in range(2):
                            inst2 = t_.matmul(psC[:, half * 512:half * 512 + 512], lhs,
                                              varena[:, hc * 1024 + half * 512:
                                                     hc * 1024 + half * 512 + 512],
                                              start=(hc == 0), stop=False)
                    cPE.inc(inst2)

                _tr(0)
                _tr(1)
                for grp in range(2, 8):
                    _mm(grp - 2)
                    _tr(grp)
                _mm(6)
                _mm(7)
                t_.wait_ge(cDV.sem, pB_dv(t, 111))
                for half in range(2):
                    inst = t_.matmul(psC[:, half * 512:half * 512 + 512],
                                     s2invrow[0:1, :],
                                     b2row[0:1, half * 512:half * 512 + 512],
                                     start=False, stop=True)
                cPE.inc(inst)
                assert cPE.n == pB_pe(t, 25)

        @block.vector
        def _(vector):
            for t in range(NTT):
                k = 0
                for jh in range(2):
                    for sl in range(4):
                        gidx = jh * 4 + sl
                        vector.wait_ge(cPE.sem, pB_pe(t, gidx + 1))
                        if gidx >= 2:
                            vector.wait_ge(cAC.sem, pB_ac(t, gidx - 1))
                        elif t >= 1:
                            vector.wait_ge(cAC.sem, pB_ac(t - 1, 7 + gidx))
                        inst = vector.scalar_tensor_tensor(
                            wf[:, 4096 + (gidx % 2) * 512:4096 + (gidx % 2) * 512 + 512],
                            psBIG[:, sl * 512:sl * 512 + 512],
                            wsm[:, S1C + t:S1C + t + 1],
                            wkA[:, jh * 2048 + sl * 512:jh * 2048 + sl * 512 + 512],
                            OP.mult, OP.add)
                        k += 1; cDV.inc(inst)
                vector.wait_ge(cAC.sem, pB_ac(t, 8))
                inst = vector.reduce_max(wsm[:, ST:ST + 1], wf[:, 0:4096], AX.X,
                                         apply_absolute_value=True)
                k += 1; cDV.inc(inst); vector.wait_ge(cDV.sem, cDV.n)
                inst = vector.tensor_scalar_max(wsm[:, ST:ST + 1], wsm[:, ST:ST + 1], EPS_Q)
                k += 1; cDV.inc(inst); vector.wait_ge(cDV.sem, cDV.n)
                inst = vector.reciprocal(wsm[:, ST + 1:ST + 2], wsm[:, ST:ST + 1])
                k += 1; cDV.inc(inst); vector.wait_ge(cDV.sem, cDV.n)
                inst = vector.tensor_scalar_mul(wsm[:, ST + 1:ST + 2], wsm[:, ST + 1:ST + 2],
                                                127.0)
                k += 1; cDV.inc(inst); vector.wait_ge(cDV.sem, cDV.n)
                inst = vector.tensor_tensor(wsm[:, ST + 2:ST + 3], wsm[:, ST:ST + 1],
                                            wsm[:, C_WQ2:C_WQ2 + 1], OP.mult)
                k += 1; cDV.inc(inst); vector.wait_ge(cDV.sem, cDV.n)
                inst = vector.reciprocal(wsm[:, ST + 3:ST + 4], wsm[:, ST + 2:ST + 3])
                k += 1; cDV.inc(inst)
                assert k == 14
                for qch in range(8):
                    k += emit_quant_chunk(vector, cDV, wf, wri,
                                          wsm[:, ST + 1:ST + 2],
                                          wf[:, qch * 512:qch * 512 + 512],
                                          wkB[:, qch * 512:qch * 512 + 512])
                assert k == 110
                vector.wait_ge(qLG.sem, pB_lg(t, 2))
                inst = vector.tensor_copy(s2invrow[0:1, :], wsm[0:1, 64:192])
                k += 1; cDV.inc(inst)
                for grp in range(8):
                    tri = 9 + grp if grp < 2 else 8 + 2 * grp
                    vector.wait_ge(cPE.sem, pB_pe(t, tri))
                    if grp >= 2:
                        mmi = 11 + 2 * (grp - 2) if grp - 2 <= 5 else 17 + (grp - 2)
                        vector.wait_ge(cPE.sem, pB_pe(t, mmi))
                    elif t >= 1:
                        pmi = 17 + (grp + 6)
                        vector.wait_ge(cPE.sem, pB_pe(t - 1, pmi))
                    srcp = psTa if grp % 2 == 0 else psTb
                    inst = vector.tensor_copy(
                        wkB[:, 4096 + (grp % 2) * 512:4096 + (grp % 2) * 512 + 512],
                        srcp[:, 0:512])
                    k += 1; cDV.inc(inst)
                vector.wait_ge(qLD.sem, pB_x1(t))
                vector.wait_ge(cPE.sem, pB_pe(t, 25))
                if t >= 1:
                    vector.wait_ge(qOUT.sem, pB_y(t - 1))
                inst = vector.scalar_tensor_tensor(wf[:, 0:1024], psC[:, 0:1024],
                                                   wsm[:, ST + 2:ST + 3],
                                                   wf[:, 4096:5120], OP.mult, OP.add)
                k += 1; cDV.inc(inst)
                inst = vector.reduce_sum(wsm[:, ST + 4:ST + 5], wf[:, 0:1024], AX.X)
                k += 1; cDV.inc(inst); vector.wait_ge(cDV.sem, cDV.n)
                inst = vector.tensor_scalar_mul(wsm[:, ST + 5:ST + 6], wsm[:, ST + 4:ST + 5],
                                                -1.0 / D)
                k += 1; cDV.inc(inst); vector.wait_ge(cDV.sem, cDV.n)
                inst = vector.tensor_scalar(wf[:, 0:1024], wf[:, 0:1024],
                                            wsm[:, ST + 5:ST + 6], None, OP.add)
                k += 1; cDV.inc(inst)
                vector.wait_ge(cAC.sem, pB_ac(t, 9))
                inst = vector.reduce_sum(wsm[:, ST + 7:ST + 8], wf[:, 1024:2048], AX.X)
                k += 1; cDV.inc(inst); vector.wait_ge(cDV.sem, cDV.n)
                inst = vector.tensor_scalar(wsm[:, ST + 6:ST + 7], wsm[:, ST + 7:ST + 8],
                                            1.0 / D, EPS_LN, OP.mult, OP.add)
                k += 1; cDV.inc(inst)
                vector.wait_ge(cAC.sem, pB_ac(t, 10))
                inst = vector.reciprocal(wsm[:, ST + 5:ST + 6], wsm[:, ST + 7:ST + 8])
                k += 1; cDV.inc(inst); vector.wait_ge(cDV.sem, cDV.n)
                inst = vector.tensor_scalar(wf[:, 0:1024], wf[:, 0:1024],
                                            wsm[:, ST + 5:ST + 6], None, OP.mult)
                k += 1; cDV.inc(inst)
                assert k == 127
                assert cDV.n == pB_dv(t, 127)

        @block.scalar
        def _(scalar):
            for t in range(NTT):
                for jh in range(2):
                    for sl in range(4):
                        gidx = jh * 4 + sl
                        scalar.wait_ge(cDV.sem, pB_dv(t, gidx + 1))
                        if gidx < 2 and t >= 1:
                            scalar.wait_ge(qOUT.sem, pB_y(t - 1))
                        inst = scalar.activation(
                            wf[:, gidx * 512:gidx * 512 + 512],
                            wf[:, 4096 + (gidx % 2) * 512:4096 + (gidx % 2) * 512 + 512],
                            AF.Gelu)
                        cAC.inc(inst)
                scalar.wait_ge(cDV.sem, pB_dv(t, 123))
                inst = scalar.activation(wf[:, 1024:2048], wf[:, 0:1024], AF.Square)
                cAC.inc(inst)
                scalar.wait_ge(cDV.sem, pB_dv(t, 125))
                inst = scalar.activation(wsm[:, ST + 7:ST + 8], wsm[:, ST + 6:ST + 7], AF.Sqrt)
                cAC.inc(inst)
                assert cAC.n == pB_ac(t, 10)

        if debug:
            @block.vector
            def _(vector):
                vector.wait_ge(qOUT.sem, 0)
        @block.gpsimd
        def _(g):
            for t in range(NTT):
                g.wait_ge(cDV.sem, pB_dv(t, 14))
                if t >= 1:
                    g.wait_ge(cDV.sem, pB_dv(t - 1, 111))
                inst = g.dma_start(out=rscr.rearrange("(p a) -> p a", a=1),
                                   in_=wsm[:, ST + 3:ST + 4])
                qLG.inc(inst)
                g.wait_ge(qLG.sem, pB_lg(t, 1))
                inst = g.dma_start(out=wsm[0:1, 64:192],
                                   in_=rscr.rearrange("(a n) -> a n", a=1))
                qLG.inc(inst)
                g.wait_ge(cDV.sem, pB_dv(t, 127))
                inst = g.dma_start(out=y[t * 128:(t + 1) * 128, :], in_=wf[:, 0:1024])
                qOUT.inc(inst)
            if debug:
                qOUT.inc(g.dma_start(out=dbg_wsm[:], in_=wsm[:, :]))
                qOUT.inc(g.dma_start(out=dbg_x1[:], in_=x1scr[:]))
                qOUT.inc(g.dma_start(out=dbg_wf[:], in_=wf[:, :]))
                qOUT.inc(g.dma_start(out=dbg_wo[:], in_=wo_bf[:]))
                qOUT.inc(g.dma_start(out=dbg_q[:], in_=qarena[:, :]))
            g.wait_ge(qOUT.sem, qOUT.n)

        qLD.n = ld0 + NTT * 16
        assert cPE.n == pe0 + NTT * 25
        assert cDV.n == dv0 + NTT * 127
        assert cAC.n == ac0 + NTT * 10
        assert qLG.n == lgB0 + NTT * 2 * 16

    return nc


_CACHE = {}


def kernel(L0_attn_x, L0_attn_W_qkv, L0_attn_b_qkv, L0_attn_W_o, L0_attn_b_o,
           L0_ffn_W1, L0_ffn_b1, L0_ffn_W2, L0_ffn_b2):
    import ml_dtypes
    x = np.ascontiguousarray(np.asarray(L0_attn_x, dtype=np.float32))
    wqkv_ = np.ascontiguousarray(np.asarray(L0_attn_W_qkv, dtype=np.float32))
    bqkv_ = np.ascontiguousarray(np.asarray(L0_attn_b_qkv, dtype=np.float32))
    wo_ = np.ascontiguousarray(np.asarray(L0_attn_W_o, dtype=np.float32))
    bo_ = np.ascontiguousarray(np.asarray(L0_attn_b_o, dtype=np.float32))
    w1t_ = np.ascontiguousarray(np.asarray(L0_ffn_W1, dtype=np.float32).T)
    b1_ = np.ascontiguousarray(np.asarray(L0_ffn_b1, dtype=np.float32))
    w2t_ = np.ascontiguousarray(np.asarray(L0_ffn_W2, dtype=np.float32).T)
    b2_ = np.ascontiguousarray(np.asarray(L0_ffn_b2, dtype=np.float32))
    ident_ = np.eye(128, dtype=ml_dtypes.bfloat16)

    if "nc" not in _CACHE:
        _CACHE["nc"] = build_nc()
    nc = _CACHE["nc"]

    in_maps = []
    for c in range(8):
        b, half = c // 2, c % 2
        xT_ = np.ascontiguousarray(x[b].T)
        in_maps.append({
            "xT": xT_,
            "xTq": np.ascontiguousarray(xT_[:, half * QO:(half + 1) * QO]),
            "x_own": np.ascontiguousarray(x[b, half * QO:(half + 1) * QO]),
            "wqkv": wqkv_, "bqkv": bqkv_, "wo": wo_, "bo": bo_,
            "w1t": w1t_, "b1": b1_, "w2t": w2t_, "b2": b2_,
            "ident_in": ident_,
        })
    res = run_bass_kernel_spmd(nc, in_maps, core_ids=list(range(8)))
    out = np.empty((B, S, D), dtype=np.float32)
    for c in range(8):
        b, half = c // 2, c % 2
        out[b, half * QO:(half + 1) * QO] = res.results[c]["y"]
    return out


# version 17
# speedup vs baseline: 1.6877x; 1.0134x over previous
import numpy as np
from contextlib import ExitStack

import concourse.bass as bass
import concourse.mybir as mybir
from concourse.bass_utils import run_bass_kernel_spmd
from concourse.alu_op_type import AluOpType as OP

F32 = mybir.dt.float32
BF16 = mybir.dt.bfloat16
I32 = mybir.dt.int32
AF = mybir.ActivationFunctionType
AX = mybir.AxisListType

B, S, D = 4, 4096, 1024
NH, HS = 16, 64
H = 4096
QO = 2048
NKT = S // 128
NTT = QO // 128
EPS_LN = 1e-5
EPS_Q = 1e-5
VSTRIDE = HS + 1
VTILE = NH * VSTRIDE

NEGM = 0
BQ = 16
BK = 24
S1C = 32
WS = 48
ST = 56
MQ2 = 64
C_T1, C_NT1, C_T2, C_NT2 = WS + 2, WS + 3, WS + 4, WS + 5
C_WQ1, C_WQ2 = WS + 6, WS + 7


QCH_OPS = 12


def emit_quant_chunk(vector, cDV, wf, wri, shm_ap, src_ap, dst_ap):
    from concourse.alu_op_type import AluOpType as OP
    qa = wf[:, 4096:4608]
    qb = wf[:, 4608:5120]
    qc = wf[:, 5120:5632]
    n = 0
    inst = vector.tensor_scalar(qa, src_ap, shm_ap, 256.0, OP.mult, OP.add)
    n += 1; cDV.inc(inst)
    inst = vector.tensor_copy(wri[:, :], qa)
    n += 1; cDV.inc(inst)
    inst = vector.tensor_copy(qb, wri[:, :])
    n += 1; cDV.inc(inst)
    inst = vector.tensor_sub(qa, qa, qb)
    n += 1; cDV.inc(inst)
    inst = vector.tensor_scalar(qc, qa, -0.5, None, OP.is_lt)
    n += 1; cDV.inc(inst)
    inst = vector.scalar_tensor_tensor(qc, qa, -1.5, qc, OP.is_lt, OP.add)
    n += 1; cDV.inc(inst)
    inst = vector.scalar_tensor_tensor(qc, qa, -2.5, qc, OP.is_lt, OP.add)
    n += 1; cDV.inc(inst)
    inst = vector.scalar_tensor_tensor(qb, qc, -1.0, qb, OP.mult, OP.add)
    n += 1; cDV.inc(inst)
    inst = vector.tensor_scalar(qc, qa, 0.5, None, OP.is_ge)
    n += 1; cDV.inc(inst)
    inst = vector.scalar_tensor_tensor(qc, qa, 1.5, qc, OP.is_ge, OP.add)
    n += 1; cDV.inc(inst)
    inst = vector.scalar_tensor_tensor(qc, qa, 2.5, qc, OP.is_ge, OP.add)
    n += 1; cDV.inc(inst)
    inst = vector.scalar_tensor_tensor(dst_ap, qb, -256.0, qc, OP.add, OP.add)
    n += 1; cDV.inc(inst)
    assert n == QCH_OPS
    return n


class Ctr:
    def __init__(self, sem, step=1):
        self.sem, self.step, self.n = sem, step, 0

    def inc(self, inst):
        inst.then_inc(self.sem, self.step)
        self.n += self.step
        return self.n


def build_nc(debug=False):
    nc = bass.Bass("TRN2", target_bir_lowering=False, debug=False,
                   detect_race_conditions=False)

    xT = nc.dram_tensor("xT", [D, S], F32, kind="ExternalInput").ap()
    xTq = nc.dram_tensor("xTq", [D, QO], F32, kind="ExternalInput").ap()
    x_own = nc.dram_tensor("x_own", [QO, D], F32, kind="ExternalInput").ap()
    wqkv = nc.dram_tensor("wqkv", [D, 3 * D], F32, kind="ExternalInput").ap()
    bqkv = nc.dram_tensor("bqkv", [3 * D], F32, kind="ExternalInput").ap()
    wo = nc.dram_tensor("wo", [D, D], F32, kind="ExternalInput").ap()
    bo = nc.dram_tensor("bo", [D], F32, kind="ExternalInput").ap()
    w1t = nc.dram_tensor("w1t", [D, H], F32, kind="ExternalInput").ap()
    b1 = nc.dram_tensor("b1", [H], F32, kind="ExternalInput").ap()
    w2t = nc.dram_tensor("w2t", [H, D], F32, kind="ExternalInput").ap()
    b2 = nc.dram_tensor("b2", [D], F32, kind="ExternalInput").ap()
    ident_in = nc.dram_tensor("ident_in", [128, 128], BF16, kind="ExternalInput").ap()
    y = nc.dram_tensor("y", [QO, D], F32, kind="ExternalOutput").ap()
    if debug:
        dbg_wsm = nc.dram_tensor("dbg_wsm", [128, 192], F32, kind="ExternalOutput").ap()
        dbg_x1 = nc.dram_tensor("dbg_x1", [QO, D], F32, kind="ExternalOutput").ap()
        dbg_wf = nc.dram_tensor("dbg_wf", [128, 5632], F32, kind="ExternalOutput").ap()
        dbg_wo = nc.dram_tensor("dbg_wo", [8, 128, D], BF16, kind="ExternalOutput").ap()
        dbg_q = nc.dram_tensor("dbg_q", [128, 8 * QO], BF16, kind="ExternalOutput").ap()
        dbg_k = nc.dram_tensor("dbg_k", [128, 8 * 4096], BF16, kind="ExternalOutput").ap()
        dbg_v = nc.dram_tensor("dbg_v", [128, NKT * VTILE], BF16, kind="ExternalOutput").ap()
        dbg_av = nc.dram_tensor("dbg_av", [128, 8 * QO], BF16, kind="ExternalOutput").ap()

    dscr = nc.dram_tensor("dscr", [1, 1024], F32, kind="Internal").ap()
    mscr = nc.dram_tensor("mscr", [16], F32, kind="Internal").ap()
    sscr = nc.dram_tensor("sscr", [8], F32, kind="Internal").ap()
    rscr = nc.dram_tensor("rscr", [128], F32, kind="Internal").ap()
    wo_bf = nc.dram_tensor("wo_bf", [8, 128, D], BF16, kind="Internal").ap()
    x1scr = nc.dram_tensor("x1scr", [QO, D], F32, kind="Internal").ap()

    with ExitStack() as ctx:
        e = ctx.enter_context
        karena = e(nc.sbuf_tensor("karena", [128, 8 * 4096], BF16))
        varena = e(nc.sbuf_tensor("varena", [128, NKT * VTILE], BF16))
        qarena = e(nc.sbuf_tensor("qarena", [128, 8 * QO], BF16))
        wkA = e(nc.sbuf_tensor("wkA", [128, 4096], BF16))
        wkB = e(nc.sbuf_tensor("wkB", [128, 5632], BF16))
        wf = e(nc.sbuf_tensor("wf", [128, 5632], F32))
        wsm = e(nc.sbuf_tensor("wsm", [128, 192], F32))
        wri = e(nc.sbuf_tensor("wri", [128, 512], I32))
        ident = e(nc.sbuf_tensor("ident", [128, 128], BF16))
        wsb = e(nc.sbuf_tensor("wsb", [128, 32], BF16))
        b2row = e(nc.sbuf_tensor("b2row", [1, 1024], BF16))
        s2invrow = e(nc.sbuf_tensor("s2invrow", [1, 128], BF16))

        psBIG = e(nc.psum_tensor("psBIG", [128, 2048], F32))
        psC = e(nc.psum_tensor("psC", [128, 1024], F32))
        psTa = e(nc.psum_tensor("psTa", [128, 512], BF16))
        psTb = e(nc.psum_tensor("psTb", [128, 512], BF16))

        qLD = Ctr(e(nc.semaphore("qLD")), 16)
        qLG = Ctr(e(nc.semaphore("qLG")), 16)
        qOUT = Ctr(e(nc.semaphore("qOUT")), 16)
        cPE = Ctr(e(nc.semaphore("cPE")))
        cP0 = Ctr(e(nc.semaphore("cP0")))
        cTN = Ctr(e(nc.semaphore("cTN")))
        cDV = Ctr(e(nc.semaphore("cDV")))
        cAC = Ctr(e(nc.semaphore("cAC")))

        block = e(nc.Block())

        @block.sync
        def _(sync):
            qLD.inc(sync.dma_start(out=ident[:], in_=ident_in[:]))
            for fc in range(8):
                qLD.inc(sync.dma_start(
                    out=wsm[:, BQ + fc:BQ + fc + 1],
                    in_=bqkv[fc * 128:(fc + 1) * 128].rearrange("(p a) -> p a", a=1)))
                qLD.inc(sync.dma_start(
                    out=wsm[:, BK + fc:BK + fc + 1],
                    in_=bqkv[1024 + fc * 128:1024 + (fc + 1) * 128].rearrange("(p a) -> p a", a=1)))
            inst = sync.dma_start(out=wf[:, 4096:5120],
                                  in_=bqkv[2048:3072].rearrange("(a n) -> a n", a=1)
                                  .partition_broadcast(128))
            qLD.inc(inst)
        LD_CONST = qLD.n

        @block.vector
        def _(vector):
            for inst in [vector.memset(wsm[:, NEGM:NEGM + 1], -16.0),
                         vector.memset(wsb[:, :], 0.0),
                         vector.memset(wsb[0:64, 0:1], 1.0),
                         vector.memset(wsb[64:128, 1:2], 1.0),
                         vector.memset(wsb[:, 2:3], 1.0),
                         vector.memset(wsm[:, 96:97], 1.0),
                         vector.memset(varena[:, :], 1.0)]:
                cDV.inc(inst)
                vector.wait_ge(cDV.sem, cDV.n)
        DV_CONST = cDV.n

        NW = 128
        lgP0 = qLG.n

        @block.gpsimd
        def _(g):
            for i in range(NW):
                if i >= 1:
                    g.wait_ge(cP0.sem, i * 2)
                if i < 64:
                    c, p = divmod(i, 8)
                    srcap = w1t[c * 128:(c + 1) * 128, p * 512:(p + 1) * 512]
                else:
                    c, p = divmod(i - 64, 2)
                    srcap = w2t[c * 128:(c + 1) * 128, p * 512:(p + 1) * 512]
                inst = g.dma_start(out=wf[:, 5120:5632], in_=srcap)
                qLG.inc(inst)

        p0b_state = {"i": 0}

        def p0b_emit_some(vector, n):
            for _ in range(n):
                i = p0b_state["i"]
                if i >= NW:
                    return
                p0b_state["i"] = i + 1
                vector.wait_ge(qLG.sem, lgP0 + (i + 1) * 16)
                inst = vector.reduce_sum(wsm[:, ST + 4:ST + 5], wf[:, 5120:5632],
                                         AX.X, apply_absolute_value=True)
                cP0.inc(inst)
                vector.wait_ge(cP0.sem, cP0.n)
                acc = wsm[:, ST + 5:ST + 6] if i < 64 else wsm[:, ST + 6:ST + 7]
                if i == 0 or i == 64:
                    inst = vector.tensor_copy(acc, wsm[:, ST + 4:ST + 5])
                else:
                    inst = vector.tensor_add(acc, acc, wsm[:, ST + 4:ST + 5])
                cP0.inc(inst)
                vector.wait_ge(cP0.sem, cP0.n)

        ld0p = qLD.n
        stage_owner = [None, None]
        nJob = [0]
        Psync, Pac, Pdv, Ppe = [], [], [], []
        quarter_pred = {0: None, 1: None, 2: None, 3: None}
        nAC = [cAC.n]
        nDV = [cDV.n]
        nPE = [cPE.n]
        wr_g = wqkv.rearrange("(c p) n -> p c n", p=128)
        xr_kv = xT.rearrange("(c p) n -> p c n", p=128)
        xr_q = xTq.rearrange("(c p) n -> p c n", p=128)
        bufree = [None, None]
        wkA_free = [None]
        wkB_free = [None]

        def add_dma(src):
            j = nJob[0]
            nJob[0] += 1
            sidx = j % 2
            prev = stage_owner[sidx]
            Psync.append((prev, src,
                          wf[:, sidx * 2048:sidx * 2048 + 2048]
                          .rearrange("p (c n) -> p c n", n=512)))
            return sidx, ld0p + (j + 1) * 16

        def add_wjob(wcol0, fh, wh, dst_t, free_pe):
            src = wr_g[:, wh * 4:(wh + 1) * 4,
                       wcol0 + fh * 512:wcol0 + fh * 512 + 512]
            sidx, dcnt = add_dma(src)
            nDV[0] += 1
            Pdv.append(("wcast", dcnt, free_pe,
                        wf[:, sidx * 2048:sidx * 2048 + 2048],
                        dst_t[:, wh * 2048:(wh + 1) * 2048]))
            stage_owner[sidx] = ("dv", nDV[0])
            return nDV[0]

        def add_xjob(xr, t, h, dst_ap, free_pe):
            src = xr[:, h * 4:(h + 1) * 4, t * 512:t * 512 + 512]
            sidx, dcnt = add_dma(src)
            nAC[0] += 1
            Pac.append((dcnt, free_pe,
                        wf[:, sidx * 2048:sidx * 2048 + 2048], dst_ap))
            stage_owner[sidx] = ("ac", nAC[0])
            return nAC[0]

        def add_group(wac, wdv, g, mms):
            nPE[0] += 1
            Ppe.append((wac, wdv, quarter_pred[g], mms))
            return nPE[0]

        def build_kv(sub, wcol0):
            xc, wc, grp = {}, {}, {}
            pre = [("W", (0, 0)), ("X", (0, 0)), ("W", (0, 1)), ("X", (0, 1)),
                   ("W", (1, 0)), ("X", (1, 0)), ("W", (1, 1)), ("X", (1, 1))]
            for kind, arg in pre:
                if kind == "W":
                    fh, wh = arg
                    wc[(fh, wh)] = add_wjob(
                        wcol0, fh, wh, wkA if fh == 0 else wkB,
                        (wkA_free if fh == 0 else wkB_free)[0])
                else:
                    t, h = arg
                    dst = qarena[:, (t % 2) * 4096 + h * 2048:
                                 (t % 2) * 4096 + h * 2048 + 2048]
                    xc[(t, h)] = add_xjob(xr_kv, t, h, dst, bufree[t % 2])
            for t in range(8):
                xbuf = qarena[:, (t % 2) * 4096:(t % 2) * 4096 + 4096]
                for q in range(8):
                    fh, g = q // 4, q % 4
                    wk = wkA if fh == 0 else wkB
                    ps = psBIG[:, g * 512:g * 512 + 512]
                    mms = []
                    for dc in range(8):
                        if sub == "K":
                            mms.append((wk[:, dc * 512 + g * 128:
                                           dc * 512 + g * 128 + 128],
                                        xbuf[:, dc * 512:dc * 512 + 512], ps))
                        else:
                            mms.append((xbuf[:, dc * 512 + g * 128:
                                             dc * 512 + g * 128 + 128],
                                        wk[:, dc * 512:dc * 512 + 512], ps))
                    gc = add_group(xc[(t, 1)], wc[(fh, 1)], g, mms)
                    grp[(t, q)] = gc
                    nDV[0] += 1
                    fc = fh * 4 + g
                    if sub == "K":
                        Pdv.append(("evacKQ", gc, ps,
                                    karena[:, fc * 4096 + t * 512:
                                           fc * 4096 + t * 512 + 512],
                                    BK + fc))
                    else:
                        kt = t * 4 + g
                        h0 = fh * 8
                        dst = varena[:, kt * VTILE:(kt + 1) * VTILE].rearrange(
                            "p (h v) -> p h v", v=VSTRIDE)[:, h0:h0 + 8, 0:64]
                        Pdv.append(("evacV", gc,
                                    ps.rearrange("p (h v) -> p h v", v=64),
                                    dst,
                                    wf[:, 4096 + fh * 512:4096 + fh * 512 + 512]
                                    .rearrange("p (h v) -> p h v", v=64)))
                    quarter_pred[g] = nDV[0]
                Pdv.append(("p0b", 5))
                bufree[t % 2] = grp[(t, 7)]
                if t + 2 <= 7:
                    for h in (0, 1):
                        dst = qarena[:, ((t + 2) % 2) * 4096 + h * 2048:
                                     ((t + 2) % 2) * 4096 + h * 2048 + 2048]
                        xc[(t + 2, h)] = add_xjob(xr_kv, t + 2, h, dst,
                                                  bufree[t % 2])
            wkA_free[0] = grp[(7, 3)]
            wkB_free[0] = grp[(7, 7)]

        def build_q():
            for fh in range(2):
                xc, wc, grp = {}, {}, {}
                wc[0] = add_wjob(0, fh, 0, wkA, wkA_free[0])
                xc[(0, 0)] = add_xjob(xr_q, 0, 0, wkB[:, 0:2048], wkB_free[0])
                wc[1] = add_wjob(0, fh, 1, wkA, wkA_free[0])
                xc[(0, 1)] = add_xjob(xr_q, 0, 1, wkB[:, 2048:4096],
                                      wkB_free[0])
                for t in range(4):
                    for g in range(4):
                        ps = psBIG[:, g * 512:g * 512 + 512]
                        mms = [(wkA[:, dc * 512 + g * 128:
                                    dc * 512 + g * 128 + 128],
                                wkB[:, dc * 512:dc * 512 + 512], ps)
                               for dc in range(8)]
                        gc = add_group(xc[(t, 1)], wc[1], g, mms)
                        grp[(t, g)] = gc
                        nDV[0] += 1
                        fc = fh * 4 + g
                        Pdv.append(("evacKQ", gc, ps,
                                    qarena[:, fc * 2048 + t * 512:
                                           fc * 2048 + t * 512 + 512],
                                    BQ + fc))
                        quarter_pred[g] = nDV[0]
                    Pdv.append(("p0b", 6))
                    if t + 1 <= 3:
                        for h in (0, 1):
                            xc[(t + 1, h)] = add_xjob(
                                xr_q, t + 1, h,
                                wkB[:, h * 2048:h * 2048 + 2048], grp[(t, 3)])
                wkA_free[0] = grp[(3, 3)]
                wkB_free[0] = grp[(3, 3)]

        build_kv("K", 1024)
        build_kv("V", 2048)
        build_q()

        @block.sync
        def _(sync):
            for prev, src, dstap in Psync:
                if prev is not None:
                    eng, v = prev
                    sync.wait_ge(cAC.sem if eng == "ac" else cDV.sem, v)
                qLD.inc(sync.dma_start(out=dstap, in_=src))
        assert qLD.n == ld0p + 60 * 16

        @block.scalar
        def _(scalar):
            for dcnt, free_pe, srcap, dstap in Pac:
                scalar.wait_ge(qLD.sem, dcnt)
                if free_pe is not None:
                    scalar.wait_ge(cPE.sem, free_pe)
                cAC.inc(scalar.activation(dstap, srcap, AF.Copy))
        assert cAC.n == 48

        @block.vector
        def _(vector):
            for it in Pdv:
                if it[0] == "wcast":
                    _, dcnt, free_pe, srcap, dstap = it
                    vector.wait_ge(qLD.sem, dcnt)
                    if free_pe is not None:
                        vector.wait_ge(cPE.sem, free_pe)
                    cDV.inc(vector.tensor_copy(dstap, srcap))
                elif it[0] == "evacKQ":
                    _, gc, ps, dstap, bcol = it
                    vector.wait_ge(cPE.sem, gc)
                    cDV.inc(vector.tensor_scalar(dstap, ps,
                                                 wsm[:, bcol:bcol + 1],
                                                 None, OP.add))
                elif it[0] == "evacV":
                    _, gc, psv, dstap, bv = it
                    vector.wait_ge(cPE.sem, gc)
                    cDV.inc(vector.tensor_tensor(dstap, psv, bv, OP.add))
                else:
                    p0b_emit_some(vector, it[1])
        assert cDV.n == nDV[0]

        @block.tensor
        def _(t):
            for wac, wdv, qp, mms in Ppe:
                t.wait_ge(cAC.sem, wac)
                t.wait_ge(cDV.sem, wdv)
                if qp is not None:
                    t.wait_ge(cDV.sem, qp)
                for i, (lh, rh_, ps) in enumerate(mms):
                    inst = t.matmul(ps, lh, rh_, start=(i == 0), stop=(i == 7))
                cPE.inc(inst)
        assert cPE.n == 160

        @block.vector
        def _(vector):
            p0b_emit_some(vector, NW)
        assert p0b_state["i"] == NW
        assert cP0.n == NW * 2
        PE_PROJ = cPE.n
        DV_PROJ = cDV.n

        @block.tensor
        def _(t):
            t.wait_ge(cP0.sem, NW * 2)
            t.matmul(psC[0:1, 0:1], wsm[:, 96:97], wsm[:, ST + 5:ST + 6],
                     start=True, stop=True)
            inst = t.matmul(psC[0:1, 512:513], wsm[:, 96:97], wsm[:, ST + 6:ST + 7],
                            start=True, stop=True)
            cPE.inc(inst)
        PE_W0 = cPE.n

        @block.vector
        def _(vector):
            vector.wait_ge(cPE.sem, PE_W0)
            for inst in [
                vector.tensor_copy(wsm[0:1, ST:ST + 1], psC[0:1, 0:1]),
                vector.tensor_copy(wsm[0:1, ST + 1:ST + 2], psC[0:1, 512:513]),
                vector.tensor_scalar(wsm[0:1, ST:ST + 1], wsm[0:1, ST:ST + 1],
                                     1.0 / (D * H), EPS_Q, OP.mult, OP.max),
                vector.tensor_scalar(wsm[0:1, ST + 1:ST + 2], wsm[0:1, ST + 1:ST + 2],
                                     1.0 / (D * H), EPS_Q, OP.mult, OP.max),
                vector.tensor_scalar_mul(wsm[0:1, ST + 2:ST + 3], wsm[0:1, ST:ST + 1], 0.5),
                vector.tensor_scalar_mul(wsm[0:1, ST + 3:ST + 4], wsm[0:1, ST:ST + 1], -0.5),
                vector.tensor_scalar_mul(wsm[0:1, ST + 4:ST + 5], wsm[0:1, ST + 1:ST + 2], 0.5),
                vector.tensor_scalar_mul(wsm[0:1, ST + 5:ST + 6], wsm[0:1, ST + 1:ST + 2], -0.5),
                vector.tensor_scalar_mul(wsm[0:1, ST + 6:ST + 7], wsm[0:1, ST:ST + 1], 1.0 / 127.0),
                vector.tensor_scalar_mul(wsm[0:1, ST + 7:ST + 8], wsm[0:1, ST + 1:ST + 2],
                                         1.0 / 127.0),
            ]:
                cDV.inc(inst)
                vector.wait_ge(cDV.sem, cDV.n)
        DV_WS = cDV.n

        @block.gpsimd
        def _(g):
            g.wait_ge(cDV.sem, DV_WS)
            inst = g.dma_start(out=sscr.rearrange("(a n) -> a n", a=1), in_=wsm[0:1, ST:ST + 8])
            qLG.inc(inst)
            g.wait_ge(qLG.sem, qLG.n)
            inst = g.dma_start(out=wsm[:, WS:WS + 8],
                               in_=sscr.rearrange("(a n) -> a n", a=1).partition_broadcast(128))
            qLG.inc(inst)
        LG_WS = qLG.n

        pe0 = cPE.n
        ac0 = cAC.n
        dv0 = cDV.n
        lg0 = qLG.n
        ldA0 = qLD.n
        steps = [(h, qc) for h in range(NH) for qc in range(2)]
        NSTEP = len(steps)
        GKT = NSTEP * NKT

        def st_inc(gkt):
            return pe0 + (gkt + 1 if gkt < 2 else 2 * gkt)

        def pv_inc(gkt):
            return pe0 + 2 * gkt + 3 if gkt < GKT - 1 else pe0 + 2 * gkt + 2

        def exp_inc(gkt):
            return ac0 + (gkt // NKT) * 33 + (gkt % NKT) + 1

        def ac_copy(m):
            return ac0 + (m + 1) * 33

        def r_inc(m):
            return dv0 + m * 2 + 1

        def tt_inc(m):
            return dv0 + m * 2 + 2

        def lg_back(m):
            return lg0 + (m * 3 + 2) * 16

        def lg_move(m):
            return lg0 + (m * 3 + 3) * 16

        side_dma = []
        side_dve = []

        def w1_dma(i):
            c, p = divmod(i, 4)
            def f(sync):
                sync.wait_ge(cPE.sem, st_inc((4 * c + 4) * NKT - 1))
                if i >= 1:
                    sync.wait_ge(cTN.sem, 2 * i)
                inst = sync.dma_start(out=wf[:, 4096:5120],
                                      in_=w1t[c * 128:(c + 1) * 128,
                                              p * 1024:(p + 1) * 1024])
                qLD.inc(inst)
            return f

        def w1_dve(i):
            c, p = divmod(i, 4)
            def f(vector):
                if i == 0:
                    vector.wait_ge(qLG.sem, LG_WS)
                vector.wait_ge(qLD.sem, ldA0 + (i + 1) * 16)
                inst = vector.tensor_scalar(wf[:, 1024:2048], wf[:, 4096:5120],
                                            wsm[:, C_NT1:C_NT1 + 1], None, OP.is_lt)
                cTN.inc(inst)
                inst = vector.scalar_tensor_tensor(
                    karena[:, c * 4096 + p * 1024:c * 4096 + p * 1024 + 1024],
                    wf[:, 4096:5120], wsm[:, C_T1:C_T1 + 1], wf[:, 1024:2048],
                    OP.is_gt, OP.subtract)
                cTN.inc(inst)
            return f

        for i in range(32):
            side_dma.append(w1_dma(i))
            side_dve.append(w1_dve(i))

        def wo_in(c):
            def f(sync):
                sync.wait_ge(cTN.sem, 64 + c)
                inst = sync.dma_start(out=wf[:, 4096:5120], in_=wo[c * 128:(c + 1) * 128, :])
                qLD.inc(inst)
            return f

        def wo_out(c):
            def f(sync):
                sync.wait_ge(cTN.sem, 64 + c + 1)
                inst = sync.dma_start(out=wo_bf[c], in_=wkB[:, 3072:4096])
                qLD.inc(inst)
            return f

        def wo_cast(c):
            def f(vector):
                vector.wait_ge(qLD.sem, ldA0 + (32 + 2 * c + 1) * 16)
                if c >= 1:
                    vector.wait_ge(qLD.sem, ldA0 + (32 + 2 * c) * 16)
                inst = vector.tensor_copy(wkB[:, 3072:4096], wf[:, 4096:5120])
                cTN.inc(inst)
            return f

        for c in range(8):
            side_dma.append(wo_in(c))
            side_dma.append(wo_out(c))
            side_dve.append(wo_cast(c))

        def b1_dma(j):
            def f(sync):
                sync.wait_ge(cTN.sem, 72 + j)
                inst = sync.dma_start(out=wf[:, 4096:4608],
                                      in_=b1[j * 512:(j + 1) * 512]
                                      .rearrange("(a n) -> a n", a=1).partition_broadcast(128))
                qLD.inc(inst)
            return f

        def b1_cast(j):
            def f(vector):
                vector.wait_ge(qLD.sem, ldA0 + (48 + j + 1) * 16)
                inst = vector.tensor_copy(wkA[:, j * 512:(j + 1) * 512], wf[:, 4096:4608])
                cTN.inc(inst)
            return f

        for j in range(8):
            side_dma.append(b1_dma(j))
            side_dve.append(b1_cast(j))

        def b2_dma(sync):
            sync.wait_ge(cTN.sem, 80)
            inst = sync.dma_start(out=wf[0:1, 4096:5120],
                                  in_=b2.rearrange("(a n) -> a n", a=1))
            qLD.inc(inst)

        def b2_cast(vector):
            vector.wait_ge(qLD.sem, ldA0 + 57 * 16)
            inst = vector.tensor_copy(b2row[0:1, :], wf[0:1, 4096:5120])
            cTN.inc(inst)

        def bo_dma(sync):
            sync.wait_ge(cTN.sem, 64)
            inst = sync.dma_start(out=wf[:, 1024:2048],
                                  in_=bo.rearrange("(a n) -> a n", a=1).partition_broadcast(128))
            qLD.inc(inst)

        side_dma.append(b2_dma)
        side_dve.append(b2_cast)
        side_dma.append(bo_dma)
        N_SIDE_DMA = len(side_dma)
        N_SIDE_DVE = len(side_dve)

        def emit_S(t, gkt):
            m, kt = divmod(gkt, NKT)
            h, qc = steps[m]
            ch, rh = h // 2, (h % 2) * 64
            stb = (gkt % 2) * 1024
            lhsK = karena[rh:rh + 64, ch * 4096 + kt * 128:ch * 4096 + kt * 128 + 128]
            for half in range(2):
                inst = t.matmul(psBIG[:, stb + half * 512:stb + half * 512 + 512],
                                lhsK,
                                qarena[rh:rh + 64,
                                       ch * 2048 + qc * 1024 + half * 512:
                                       ch * 2048 + qc * 1024 + half * 512 + 512],
                                start=True, stop=True)
            cPE.inc(inst)
            assert cPE.n == st_inc(gkt)

        def emit_PV(t, gkt):
            m, kt = divmod(gkt, NKT)
            h, qc = steps[m]
            t.wait_ge(cAC.sem, exp_inc(gkt))
            if kt == 0 and m >= 1:
                t.wait_ge(cAC.sem, ac_copy(m - 1))
            lhsV = varena[:, kt * VTILE + h * VSTRIDE:kt * VTILE + h * VSTRIDE + VSTRIDE]
            pt = wkB[:, (gkt % 2) * 1024:(gkt % 2) * 1024 + 1024]
            for half in range(2):
                inst = t.matmul(psC[0:65, half * 512:half * 512 + 512],
                                lhsV, pt[:, half * 512:half * 512 + 512],
                                start=(kt == 0), stop=(kt == NKT - 1))
            cPE.inc(inst)
            assert cPE.n == pv_inc(gkt)

        @block.tensor
        def _(t):
            t.wait_ge(cDV.sem, DV_WS)
            t.wait_ge(cDV.sem, DV_PROJ)
            emit_S(t, 0)
            emit_S(t, 1)
            for gkt in range(GKT - 2):
                emit_PV(t, gkt)
                emit_S(t, gkt + 2)
            emit_PV(t, GKT - 2)
            emit_PV(t, GKT - 1)

        @block.sync
        def _(sync):
            for f in side_dma:
                f(sync)
        LD_SIDE = qLD.n
        assert LD_SIDE == ldA0 + N_SIDE_DMA * 16

        @block.scalar
        def _(scalar):
            for m, (h, qc) in enumerate(steps):
                for kt in range(NKT):
                    gkt = m * NKT + kt
                    stb = (gkt % 2) * 1024
                    scalar.wait_ge(cPE.sem, st_inc(gkt))
                    if gkt >= 2:
                        scalar.wait_ge(cPE.sem, pv_inc(gkt - 2))
                    inst = scalar.activation(wkB[:, (gkt % 2) * 1024:(gkt % 2) * 1024 + 1024],
                                             psBIG[:, stb:stb + 1024], AF.Exp,
                                             bias=wsm[:, NEGM:NEGM + 1], scale=0.125)
                    cAC.inc(inst)
                    assert cAC.n == exp_inc(gkt)
                scalar.wait_ge(cPE.sem, pv_inc(m * NKT + NKT - 1))
                if m >= 1:
                    scalar.wait_ge(cDV.sem, tt_inc(m - 1))
                inst = scalar.activation(wf[:, 0:1024], psC[:, 0:1024], AF.Copy)
                cAC.inc(inst)
                assert cAC.n == ac_copy(m)

        @block.vector
        def _(vector):
            popped = 0
            for m, (h, qc) in enumerate(steps):
                vector.wait_ge(cAC.sem, ac_copy(m))
                inst = vector.reciprocal(wf[64:65, 3072:4096], wf[64:65, 0:1024])
                cDV.inc(inst)
                assert cDV.n == r_inc(m)
                vector.wait_ge(qLG.sem, lg_back(m))
                if m >= 1:
                    vector.wait_ge(qLG.sem, lg_move(m - 1))
                inst = vector.tensor_tensor(wkB[0:64, 2048:3072], wf[0:64, 0:1024],
                                            wf[0:64, 3072:4096], OP.mult)
                cDV.inc(inst)
                assert cDV.n == tt_inc(m)
                if m >= 4 and popped < 28:
                    side_dve[popped](vector)
                    popped += 1
            for f in side_dve[popped:]:
                f(vector)
        assert cTN.n == 81

        @block.gpsimd
        def _(g):
            for m, (h, qc) in enumerate(steps):
                ch, rh = h // 2, (h % 2) * 64
                g.wait_ge(cDV.sem, r_inc(m))
                inst = g.dma_start(out=dscr[:], in_=wf[64:65, 3072:4096])
                qLG.inc(inst)
                g.wait_ge(qLG.sem, qLG.n)
                if m >= 1:
                    g.wait_ge(cDV.sem, tt_inc(m - 1))
                inst = g.dma_start(out=wf[0:64, 3072:4096], in_=dscr.partition_broadcast(64))
                qLG.inc(inst)
                g.wait_ge(cDV.sem, tt_inc(m))
                inst = g.dma_start(
                    out=qarena[rh:rh + 64, ch * 2048 + qc * 1024:ch * 2048 + qc * 1024 + 1024],
                    in_=wkB[0:64, 2048:3072])
                qLG.inc(inst)

        assert cPE.n == pe0 + NSTEP * NKT * 2
        assert cAC.n == ac0 + NSTEP * 33
        assert cDV.n == dv0 + NSTEP * 2
        assert qLG.n == lg0 + NSTEP * 3 * 16
        PE_ATT = cPE.n
        LG_ATT = qLG.n

        if debug:
            @block.sync
            def _(sync):
                sync.wait_ge(cPE.sem, PE_ATT)
                sync.wait_ge(qLG.sem, LG_ATT)
                qLD.inc(sync.dma_start(out=dbg_k[:], in_=karena[:, :]))
                qLD.inc(sync.dma_start(out=dbg_v[:], in_=varena[:, :]))
                qLD.inc(sync.dma_start(out=dbg_av[:], in_=qarena[:, :]))
        DBG_DUMP = qLD.n

        lg0 = qLG.n
        dv0 = cDV.n

        @block.gpsimd
        def _(g):
            g.wait_ge(cPE.sem, PE_ATT)
            g.wait_ge(cTN.sem, 81)
            if debug:
                g.wait_ge(qLD.sem, DBG_DUMP)
            for i in range(32):
                if i == 1:
                    g.wait_ge(cDV.sem, dv0)
                elif i >= 2:
                    g.wait_ge(cDV.sem, dv0 + (i - 2) * 2 + 2)
                stg = wf[:, 4096:5120] if i % 2 == 0 else wf[:, 0:1024]
                inst = g.dma_start(out=stg, in_=w2t[i * 128:(i + 1) * 128, :])
                qLG.inc(inst)

        @block.vector
        def _(vector):
            for i in range(32):
                stg = wf[:, 4096:5120] if i % 2 == 0 else wf[:, 0:1024]
                vector.wait_ge(qLG.sem, lg0 + (i + 1) * 16)
                inst = vector.tensor_scalar(wf[:, 2048:3072], stg,
                                            wsm[:, C_NT2:C_NT2 + 1], None, OP.is_lt)
                cDV.inc(inst)
                inst = vector.scalar_tensor_tensor(
                    varena[:, i * 1024:(i + 1) * 1024],
                    stg, wsm[:, C_T2:C_T2 + 1], wf[:, 2048:3072],
                    OP.is_gt, OP.subtract)
                cDV.inc(inst)
        assert cDV.n == dv0 + 64
        qLG.n = lg0 + 32 * 16
        DV_W = cDV.n
        LG_W = qLG.n

        ld0 = qLD.n
        lg0 = qLG.n
        pe0 = cPE.n
        dv0 = cDV.n
        ac0 = cAC.n

        def pA_xdma(t):
            return ld0 + (t * 3 + 1) * 16

        def pA_wo(t, dh):
            return ld0 + (t * 3 + 2 + dh) * 16

        def pA_pe(t, k):
            return pe0 + t * 3 + k

        def pA_dv(t, k):
            return dv0 + t * 40 + k

        def pA_ac(t, k):
            return ac0 + t * 2 + k

        def pA_x1dma(t):
            return lg0 + (t + 1) * 16

        @block.sync
        def _(sync):
            sync.wait_ge(qLG.sem, LG_ATT)
            sync.wait_ge(cDV.sem, DV_W)
            for t in range(NTT):
                if t >= 1:
                    sync.wait_ge(cDV.sem, pA_dv(t - 1, 1))
                inst = sync.dma_start(out=wf[:, 0:1024],
                                      in_=x_own[t * 128:(t + 1) * 128, :])
                qLD.inc(inst)
                for dh in range(2):
                    if t * 2 + dh >= 1:
                        pt_, pdh = divmod(t * 2 + dh - 1, 2)
                        sync.wait_ge(cPE.sem, pA_pe(pt_, pdh + 1))
                    inst = sync.dma_start(
                        out=wkB[:, 0:4096].rearrange("p (c x) -> p c x", x=512),
                        in_=wo_bf[:, :, dh * 512:dh * 512 + 512].rearrange("c p x -> p c x"))
                    qLD.inc(inst)

        @block.tensor
        def _(t_):
            t_.wait_ge(qLG.sem, LG_ATT)
            if debug:
                t_.wait_ge(qLD.sem, DBG_DUMP)
            for t in range(NTT):
                ps = psBIG[:, (t % 2) * 1024:(t % 2) * 1024 + 1024]
                if t >= 2:
                    t_.wait_ge(cDV.sem, pA_dv(t - 2, 2))
                for dh in range(2):
                    t_.wait_ge(qLD.sem, pA_wo(t, dh))
                    for c in range(8):
                        inst = t_.matmul(ps[:, dh * 512:dh * 512 + 512],
                                         qarena[:, c * 2048 + t * 128:c * 2048 + t * 128 + 128],
                                         wkB[:, c * 512:c * 512 + 512],
                                         start=(c == 0), stop=(c == 7))
                    cPE.inc(inst)
                    assert cPE.n == pA_pe(t, dh + 1)
                t_.wait_ge(cDV.sem, pA_dv(t, 38))
                if t >= 1:
                    t_.wait_ge(cDV.sem, pA_dv(t - 1, 40))
                for dc in range(8):
                    dst = psTa if dc < 4 else psTb
                    inst = t_.transpose(dst[:, (dc % 4) * 128:(dc % 4) * 128 + 128],
                                        wkB[:, 4096 + dc * 128:4096 + dc * 128 + 128],
                                        ident[:])
                cPE.inc(inst)
                assert cPE.n == pA_pe(t, 3)

        @block.vector
        def _(vector):
            vector.wait_ge(qLD.sem, LD_SIDE)
            for t in range(NTT):
                k = 0
                vector.wait_ge(qLD.sem, pA_xdma(t))
                inst = vector.tensor_add(wf[:, 2048:3072], wf[:, 0:1024], wf[:, 1024:2048])
                k += 1; cDV.inc(inst)
                vector.wait_ge(cPE.sem, pA_pe(t, 2))
                ps = psBIG[:, (t % 2) * 1024:(t % 2) * 1024 + 1024]
                inst = vector.tensor_add(wf[:, 2048:3072], ps, wf[:, 2048:3072])
                k += 1; cDV.inc(inst)
                inst = vector.reduce_sum(wsm[:, ST:ST + 1], wf[:, 2048:3072], AX.X)
                k += 1; cDV.inc(inst); vector.wait_ge(cDV.sem, cDV.n)
                inst = vector.tensor_scalar_mul(wsm[:, ST + 1:ST + 2], wsm[:, ST:ST + 1], -1.0 / D)
                k += 1; cDV.inc(inst); vector.wait_ge(cDV.sem, cDV.n)
                if t >= 1:
                    vector.wait_ge(qLG.sem, pA_x1dma(t - 1))
                inst = vector.tensor_scalar(wf[:, 3072:4096], wf[:, 2048:3072],
                                            wsm[:, ST + 1:ST + 2], None, OP.add)
                k += 1; cDV.inc(inst)
                vector.wait_ge(cAC.sem, pA_ac(t, 1))
                inst = vector.reduce_sum(wsm[:, ST + 2:ST + 3], wf[:, 4096:5120], AX.X)
                k += 1; cDV.inc(inst); vector.wait_ge(cDV.sem, cDV.n)
                inst = vector.tensor_scalar(wsm[:, ST + 3:ST + 4], wsm[:, ST + 2:ST + 3],
                                            1.0 / D, EPS_LN, OP.mult, OP.add)
                k += 1; cDV.inc(inst)
                vector.wait_ge(cAC.sem, pA_ac(t, 2))
                inst = vector.reciprocal(wsm[:, ST + 5:ST + 6], wsm[:, ST + 4:ST + 5])
                k += 1; cDV.inc(inst); vector.wait_ge(cDV.sem, cDV.n)
                inst = vector.tensor_scalar(wf[:, 3072:4096], wf[:, 3072:4096],
                                            wsm[:, ST + 5:ST + 6], None, OP.mult)
                k += 1; cDV.inc(inst)
                inst = vector.reduce_max(wsm[:, ST + 6:ST + 7], wf[:, 3072:4096], AX.X,
                                         apply_absolute_value=True)
                k += 1; cDV.inc(inst); vector.wait_ge(cDV.sem, cDV.n)
                inst = vector.tensor_scalar_max(wsm[:, ST + 6:ST + 7], wsm[:, ST + 6:ST + 7], EPS_Q)
                k += 1; cDV.inc(inst); vector.wait_ge(cDV.sem, cDV.n)
                inst = vector.reciprocal(wsm[:, ST + 7:ST + 8], wsm[:, ST + 6:ST + 7])
                k += 1; cDV.inc(inst); vector.wait_ge(cDV.sem, cDV.n)
                inst = vector.tensor_scalar_mul(wsm[:, ST + 7:ST + 8], wsm[:, ST + 7:ST + 8], 127.0)
                k += 1; cDV.inc(inst); vector.wait_ge(cDV.sem, cDV.n)
                inst = vector.tensor_tensor(wsm[:, S1C + t:S1C + t + 1],
                                            wsm[:, ST + 6:ST + 7],
                                            wsm[:, C_WQ1:C_WQ1 + 1], OP.mult)
                k += 1; cDV.inc(inst)
                for qch in range(2):
                    k += emit_quant_chunk(vector, cDV, wf, wri,
                                          wsm[:, ST + 7:ST + 8],
                                          wf[:, 3072 + qch * 512:3072 + qch * 512 + 512],
                                          wkB[:, 4096 + qch * 512:4096 + qch * 512 + 512])
                assert k == 38
                vector.wait_ge(cPE.sem, pA_pe(t, 3))
                inst = vector.tensor_copy(
                    qarena.ap().rearrange("p (c n) -> p c n", n=2048)[:, 0:4, t * 128:t * 128 + 128],
                    psTa.ap().rearrange("p (c n) -> p c n", n=128))
                k += 1; cDV.inc(inst)
                inst = vector.tensor_copy(
                    qarena.ap().rearrange("p (c n) -> p c n", n=2048)[:, 4:8, t * 128:t * 128 + 128],
                    psTb.ap().rearrange("p (c n) -> p c n", n=128))
                k += 1; cDV.inc(inst)
                assert cDV.n == pA_dv(t, 40)

        @block.scalar
        def _(scalar):
            for t in range(NTT):
                scalar.wait_ge(cDV.sem, pA_dv(t, 5))
                inst = scalar.activation(wf[:, 4096:5120], wf[:, 3072:4096], AF.Square)
                cAC.inc(inst)
                scalar.wait_ge(cDV.sem, pA_dv(t, 7))
                inst = scalar.activation(wsm[:, ST + 4:ST + 5], wsm[:, ST + 3:ST + 4], AF.Sqrt)
                cAC.inc(inst)

        @block.gpsimd
        def _(g):
            for t in range(NTT):
                g.wait_ge(cDV.sem, pA_dv(t, 38))
                inst = g.dma_start(out=x1scr[t * 128:(t + 1) * 128, :], in_=wf[:, 3072:4096])
                qLG.inc(inst)

        qLD.n = ld0 + NTT * 3 * 16
        assert cPE.n == pe0 + NTT * 3
        assert cDV.n == dv0 + NTT * 40
        assert cAC.n == ac0 + NTT * 2
        X1DMA = [pA_x1dma(t) for t in range(NTT)]
        qLG.n = lg0 + NTT * 16
        DV_A = cDV.n

        ld0 = qLD.n
        pe0 = cPE.n
        dv0 = cDV.n
        ac0 = cAC.n

        lgB0 = qLG.n

        def pB_pe(t, k):
            return pe0 + t * 25 + k

        def pB_lg(t, k):
            return lgB0 + (t * 2 + k) * 16

        def pB_dv(t, k):
            return dv0 + t * 127 + k

        def pB_ac(t, k):
            return ac0 + t * 10 + k

        def pB_x1(t):
            return ld0 + (t + 1) * 16

        def pB_y(t):
            return (t + 1) * 16

        @block.sync
        def _(sync):
            for t in range(NTT):
                sync.wait_ge(cDV.sem, pB_dv(t, 110))
                sync.wait_ge(qLG.sem, X1DMA[t])
                inst = sync.dma_start(out=wf[:, 4096:5120], in_=x1scr[t * 128:(t + 1) * 128, :])
                qLD.inc(inst)

        @block.tensor
        def _(t_):
            t_.wait_ge(cDV.sem, DV_A)
            for t in range(NTT):
                for jh in range(2):
                    if jh == 1:
                        t_.wait_ge(cDV.sem, pB_dv(t, 4))
                    elif t >= 1:
                        t_.wait_ge(cDV.sem, pB_dv(t - 1, 8))
                    for dc in range(8):
                        lhs = qarena[:, dc * 2048 + t * 128:dc * 2048 + t * 128 + 128]
                        for sl in range(4):
                            inst = t_.matmul(
                                psBIG[:, sl * 512:sl * 512 + 512], lhs,
                                karena[:, dc * 4096 + jh * 2048 + sl * 512:
                                       dc * 4096 + jh * 2048 + sl * 512 + 512],
                                start=(dc == 0), stop=(dc == 7))
                            if dc == 7:
                                cPE.inc(inst)
                    assert cPE.n == pB_pe(t, jh * 4 + 4)
                t_.wait_ge(cDV.sem, pB_dv(t, 110))
                if t >= 1:
                    t_.wait_ge(cDV.sem, pB_dv(t - 1, 120))

                def _tr(grp):
                    if grp >= 2:
                        t_.wait_ge(cDV.sem, pB_dv(t, 112 + grp - 2))
                    elif t >= 1:
                        t_.wait_ge(cDV.sem, pB_dv(t - 1, 118 + grp))
                    dstp = psTa if grp % 2 == 0 else psTb
                    for i in range(4):
                        hc = grp * 4 + i
                        inst2 = t_.transpose(
                            dstp[:, i * 128:i * 128 + 128],
                            wkB[:, hc * 128:hc * 128 + 128], ident[:])
                    cPE.inc(inst2)

                def _mm(grp):
                    t_.wait_ge(cDV.sem, pB_dv(t, 112 + grp))
                    for i in range(4):
                        hc = grp * 4 + i
                        lhs = wkB[:, 4096 + (grp % 2) * 512 + i * 128:
                                  4096 + (grp % 2) * 512 + i * 128 + 128]
                        for half in range(2):
                            inst2 = t_.matmul(psC[:, half * 512:half * 512 + 512], lhs,
                                              varena[:, hc * 1024 + half * 512:
                                                     hc * 1024 + half * 512 + 512],
                                              start=(hc == 0), stop=False)
                    cPE.inc(inst2)

                _tr(0)
                _tr(1)
                for grp in range(2, 8):
                    _mm(grp - 2)
                    _tr(grp)
                _mm(6)
                _mm(7)
                t_.wait_ge(cDV.sem, pB_dv(t, 111))
                for half in range(2):
                    inst = t_.matmul(psC[:, half * 512:half * 512 + 512],
                                     s2invrow[0:1, :],
                                     b2row[0:1, half * 512:half * 512 + 512],
                                     start=False, stop=True)
                cPE.inc(inst)
                assert cPE.n == pB_pe(t, 25)

        @block.vector
        def _(vector):
            for t in range(NTT):
                k = 0
                for jh in range(2):
                    for sl in range(4):
                        gidx = jh * 4 + sl
                        vector.wait_ge(cPE.sem, pB_pe(t, gidx + 1))
                        if gidx >= 2:
                            vector.wait_ge(cAC.sem, pB_ac(t, gidx - 1))
                        elif t >= 1:
                            vector.wait_ge(cAC.sem, pB_ac(t - 1, 7 + gidx))
                        inst = vector.scalar_tensor_tensor(
                            wf[:, 4096 + (gidx % 2) * 512:4096 + (gidx % 2) * 512 + 512],
                            psBIG[:, sl * 512:sl * 512 + 512],
                            wsm[:, S1C + t:S1C + t + 1],
                            wkA[:, jh * 2048 + sl * 512:jh * 2048 + sl * 512 + 512],
                            OP.mult, OP.add)
                        k += 1; cDV.inc(inst)
                vector.wait_ge(cAC.sem, pB_ac(t, 8))
                inst = vector.reduce_max(wsm[:, ST:ST + 1], wf[:, 0:4096], AX.X,
                                         apply_absolute_value=True)
                k += 1; cDV.inc(inst); vector.wait_ge(cDV.sem, cDV.n)
                inst = vector.tensor_scalar_max(wsm[:, ST:ST + 1], wsm[:, ST:ST + 1], EPS_Q)
                k += 1; cDV.inc(inst); vector.wait_ge(cDV.sem, cDV.n)
                inst = vector.reciprocal(wsm[:, ST + 1:ST + 2], wsm[:, ST:ST + 1])
                k += 1; cDV.inc(inst); vector.wait_ge(cDV.sem, cDV.n)
                inst = vector.tensor_scalar_mul(wsm[:, ST + 1:ST + 2], wsm[:, ST + 1:ST + 2],
                                                127.0)
                k += 1; cDV.inc(inst); vector.wait_ge(cDV.sem, cDV.n)
                inst = vector.tensor_tensor(wsm[:, ST + 2:ST + 3], wsm[:, ST:ST + 1],
                                            wsm[:, C_WQ2:C_WQ2 + 1], OP.mult)
                k += 1; cDV.inc(inst); vector.wait_ge(cDV.sem, cDV.n)
                inst = vector.reciprocal(wsm[:, ST + 3:ST + 4], wsm[:, ST + 2:ST + 3])
                k += 1; cDV.inc(inst)
                assert k == 14
                for qch in range(8):
                    k += emit_quant_chunk(vector, cDV, wf, wri,
                                          wsm[:, ST + 1:ST + 2],
                                          wf[:, qch * 512:qch * 512 + 512],
                                          wkB[:, qch * 512:qch * 512 + 512])
                assert k == 110
                vector.wait_ge(qLG.sem, pB_lg(t, 2))
                inst = vector.tensor_copy(s2invrow[0:1, :], wsm[0:1, 64:192])
                k += 1; cDV.inc(inst)
                for grp in range(8):
                    tri = 9 + grp if grp < 2 else 8 + 2 * grp
                    vector.wait_ge(cPE.sem, pB_pe(t, tri))
                    if grp >= 2:
                        mmi = 11 + 2 * (grp - 2) if grp - 2 <= 5 else 17 + (grp - 2)
                        vector.wait_ge(cPE.sem, pB_pe(t, mmi))
                    elif t >= 1:
                        pmi = 17 + (grp + 6)
                        vector.wait_ge(cPE.sem, pB_pe(t - 1, pmi))
                    srcp = psTa if grp % 2 == 0 else psTb
                    inst = vector.tensor_copy(
                        wkB[:, 4096 + (grp % 2) * 512:4096 + (grp % 2) * 512 + 512],
                        srcp[:, 0:512])
                    k += 1; cDV.inc(inst)
                vector.wait_ge(qLD.sem, pB_x1(t))
                vector.wait_ge(cPE.sem, pB_pe(t, 25))
                if t >= 1:
                    vector.wait_ge(qOUT.sem, pB_y(t - 1))
                inst = vector.scalar_tensor_tensor(wf[:, 0:1024], psC[:, 0:1024],
                                                   wsm[:, ST + 2:ST + 3],
                                                   wf[:, 4096:5120], OP.mult, OP.add)
                k += 1; cDV.inc(inst)
                inst = vector.reduce_sum(wsm[:, ST + 4:ST + 5], wf[:, 0:1024], AX.X)
                k += 1; cDV.inc(inst); vector.wait_ge(cDV.sem, cDV.n)
                inst = vector.tensor_scalar_mul(wsm[:, ST + 5:ST + 6], wsm[:, ST + 4:ST + 5],
                                                -1.0 / D)
                k += 1; cDV.inc(inst); vector.wait_ge(cDV.sem, cDV.n)
                inst = vector.tensor_scalar(wf[:, 0:1024], wf[:, 0:1024],
                                            wsm[:, ST + 5:ST + 6], None, OP.add)
                k += 1; cDV.inc(inst)
                vector.wait_ge(cAC.sem, pB_ac(t, 9))
                inst = vector.reduce_sum(wsm[:, ST + 7:ST + 8], wf[:, 1024:2048], AX.X)
                k += 1; cDV.inc(inst); vector.wait_ge(cDV.sem, cDV.n)
                inst = vector.tensor_scalar(wsm[:, ST + 6:ST + 7], wsm[:, ST + 7:ST + 8],
                                            1.0 / D, EPS_LN, OP.mult, OP.add)
                k += 1; cDV.inc(inst)
                vector.wait_ge(cAC.sem, pB_ac(t, 10))
                inst = vector.reciprocal(wsm[:, ST + 5:ST + 6], wsm[:, ST + 7:ST + 8])
                k += 1; cDV.inc(inst); vector.wait_ge(cDV.sem, cDV.n)
                inst = vector.tensor_scalar(wf[:, 0:1024], wf[:, 0:1024],
                                            wsm[:, ST + 5:ST + 6], None, OP.mult)
                k += 1; cDV.inc(inst)
                assert k == 127
                assert cDV.n == pB_dv(t, 127)

        @block.scalar
        def _(scalar):
            for t in range(NTT):
                for jh in range(2):
                    for sl in range(4):
                        gidx = jh * 4 + sl
                        scalar.wait_ge(cDV.sem, pB_dv(t, gidx + 1))
                        if gidx < 2 and t >= 1:
                            scalar.wait_ge(qOUT.sem, pB_y(t - 1))
                        inst = scalar.activation(
                            wf[:, gidx * 512:gidx * 512 + 512],
                            wf[:, 4096 + (gidx % 2) * 512:4096 + (gidx % 2) * 512 + 512],
                            AF.Gelu)
                        cAC.inc(inst)
                scalar.wait_ge(cDV.sem, pB_dv(t, 123))
                inst = scalar.activation(wf[:, 1024:2048], wf[:, 0:1024], AF.Square)
                cAC.inc(inst)
                scalar.wait_ge(cDV.sem, pB_dv(t, 125))
                inst = scalar.activation(wsm[:, ST + 7:ST + 8], wsm[:, ST + 6:ST + 7], AF.Sqrt)
                cAC.inc(inst)
                assert cAC.n == pB_ac(t, 10)

        if debug:
            @block.vector
            def _(vector):
                vector.wait_ge(qOUT.sem, 0)
        @block.gpsimd
        def _(g):
            for t in range(NTT):
                g.wait_ge(cDV.sem, pB_dv(t, 14))
                if t >= 1:
                    g.wait_ge(cDV.sem, pB_dv(t - 1, 111))
                inst = g.dma_start(out=rscr.rearrange("(p a) -> p a", a=1),
                                   in_=wsm[:, ST + 3:ST + 4])
                qLG.inc(inst)
                g.wait_ge(qLG.sem, pB_lg(t, 1))
                inst = g.dma_start(out=wsm[0:1, 64:192],
                                   in_=rscr.rearrange("(a n) -> a n", a=1))
                qLG.inc(inst)
                g.wait_ge(cDV.sem, pB_dv(t, 127))
                inst = g.dma_start(out=y[t * 128:(t + 1) * 128, :], in_=wf[:, 0:1024])
                qOUT.inc(inst)
            if debug:
                qOUT.inc(g.dma_start(out=dbg_wsm[:], in_=wsm[:, :]))
                qOUT.inc(g.dma_start(out=dbg_x1[:], in_=x1scr[:]))
                qOUT.inc(g.dma_start(out=dbg_wf[:], in_=wf[:, :]))
                qOUT.inc(g.dma_start(out=dbg_wo[:], in_=wo_bf[:]))
                qOUT.inc(g.dma_start(out=dbg_q[:], in_=qarena[:, :]))
            g.wait_ge(qOUT.sem, qOUT.n)

        qLD.n = ld0 + NTT * 16
        assert cPE.n == pe0 + NTT * 25
        assert cDV.n == dv0 + NTT * 127
        assert cAC.n == ac0 + NTT * 10
        assert qLG.n == lgB0 + NTT * 2 * 16

    return nc


_CACHE = {}


def kernel(L0_attn_x, L0_attn_W_qkv, L0_attn_b_qkv, L0_attn_W_o, L0_attn_b_o,
           L0_ffn_W1, L0_ffn_b1, L0_ffn_W2, L0_ffn_b2):
    import ml_dtypes
    x = np.ascontiguousarray(np.asarray(L0_attn_x, dtype=np.float32))
    wqkv_ = np.ascontiguousarray(np.asarray(L0_attn_W_qkv, dtype=np.float32))
    bqkv_ = np.ascontiguousarray(np.asarray(L0_attn_b_qkv, dtype=np.float32))
    wo_ = np.ascontiguousarray(np.asarray(L0_attn_W_o, dtype=np.float32))
    bo_ = np.ascontiguousarray(np.asarray(L0_attn_b_o, dtype=np.float32))
    w1t_ = np.ascontiguousarray(np.asarray(L0_ffn_W1, dtype=np.float32).T)
    b1_ = np.ascontiguousarray(np.asarray(L0_ffn_b1, dtype=np.float32))
    w2t_ = np.ascontiguousarray(np.asarray(L0_ffn_W2, dtype=np.float32).T)
    b2_ = np.ascontiguousarray(np.asarray(L0_ffn_b2, dtype=np.float32))
    ident_ = np.eye(128, dtype=ml_dtypes.bfloat16)

    if "nc" not in _CACHE:
        _CACHE["nc"] = build_nc()
    nc = _CACHE["nc"]

    in_maps = []
    for c in range(8):
        b, half = c // 2, c % 2
        xT_ = np.ascontiguousarray(x[b].T)
        in_maps.append({
            "xT": xT_,
            "xTq": np.ascontiguousarray(xT_[:, half * QO:(half + 1) * QO]),
            "x_own": np.ascontiguousarray(x[b, half * QO:(half + 1) * QO]),
            "wqkv": wqkv_, "bqkv": bqkv_, "wo": wo_, "bo": bo_,
            "w1t": w1t_, "b1": b1_, "w2t": w2t_, "b2": b2_,
            "ident_in": ident_,
        })
    res = run_bass_kernel_spmd(nc, in_maps, core_ids=list(range(8)))
    out = np.empty((B, S, D), dtype=np.float32)
    for c in range(8):
        b, half = c // 2, c % 2
        out[b, half * QO:(half + 1) * QO] = res.results[c]["y"]
    return out


# version 22 (final)
# speedup vs baseline: 1.8603x; 1.1023x over previous
import numpy as np
from contextlib import ExitStack

import concourse.bass as bass
import concourse.mybir as mybir
from concourse.bass_utils import run_bass_kernel_spmd
from concourse.alu_op_type import AluOpType as OP

F32 = mybir.dt.float32
BF16 = mybir.dt.bfloat16
I32 = mybir.dt.int32
AF = mybir.ActivationFunctionType
AX = mybir.AxisListType

B, S, D = 4, 4096, 1024
NH, HS = 16, 64
H = 4096
QO = 2048
NKT = S // 128
NTT = QO // 128
EPS_LN = 1e-5
EPS_Q = 1e-5
VSTRIDE = HS + 1
VTILE = NH * VSTRIDE

NEGM = 0
BQ = 16
BK = 24
S1C = 32
WS = 48
ST = 56
MQ2 = 64
C_T1, C_NT1, C_T2, C_NT2 = WS + 2, WS + 3, WS + 4, WS + 5
C_WQ1, C_WQ2 = WS + 6, WS + 7


QCH_OPS = 12


def emit_quant_chunk(vector, cDV, wf, wri, shm_ap, src_ap, dst_ap):
    from concourse.alu_op_type import AluOpType as OP
    qa = wf[:, 4096:4608]
    qb = wf[:, 4608:5120]
    qc = wf[:, 5120:5632]
    n = 0
    inst = vector.tensor_scalar(qa, src_ap, shm_ap, 256.0, OP.mult, OP.add)
    n += 1; cDV.inc(inst)
    inst = vector.tensor_copy(wri[:, :], qa)
    n += 1; cDV.inc(inst)
    inst = vector.tensor_copy(qb, wri[:, :])
    n += 1; cDV.inc(inst)
    inst = vector.tensor_sub(qa, qa, qb)
    n += 1; cDV.inc(inst)
    inst = vector.tensor_scalar(qc, qa, -0.5, None, OP.is_lt)
    n += 1; cDV.inc(inst)
    inst = vector.scalar_tensor_tensor(qc, qa, -1.5, qc, OP.is_lt, OP.add)
    n += 1; cDV.inc(inst)
    inst = vector.scalar_tensor_tensor(qc, qa, -2.5, qc, OP.is_lt, OP.add)
    n += 1; cDV.inc(inst)
    inst = vector.scalar_tensor_tensor(qb, qc, -1.0, qb, OP.mult, OP.add)
    n += 1; cDV.inc(inst)
    inst = vector.tensor_scalar(qc, qa, 0.5, None, OP.is_ge)
    n += 1; cDV.inc(inst)
    inst = vector.scalar_tensor_tensor(qc, qa, 1.5, qc, OP.is_ge, OP.add)
    n += 1; cDV.inc(inst)
    inst = vector.scalar_tensor_tensor(qc, qa, 2.5, qc, OP.is_ge, OP.add)
    n += 1; cDV.inc(inst)
    inst = vector.scalar_tensor_tensor(dst_ap, qb, -256.0, qc, OP.add, OP.add)
    n += 1; cDV.inc(inst)
    assert n == QCH_OPS
    return n


class Ctr:
    def __init__(self, sem, step=1):
        self.sem, self.step, self.n = sem, step, 0

    def inc(self, inst):
        inst.then_inc(self.sem, self.step)
        self.n += self.step
        return self.n


def build_nc(debug=False):
    nc = bass.Bass("TRN2", target_bir_lowering=False, debug=False,
                   detect_race_conditions=False)

    xT = nc.dram_tensor("xT", [D, S], F32, kind="ExternalInput").ap()
    xTq = nc.dram_tensor("xTq", [D, QO], F32, kind="ExternalInput").ap()
    x_own = nc.dram_tensor("x_own", [QO, D], F32, kind="ExternalInput").ap()
    wqkv = nc.dram_tensor("wqkv", [D, 3 * D], F32, kind="ExternalInput").ap()
    bqkv = nc.dram_tensor("bqkv", [3 * D], F32, kind="ExternalInput").ap()
    wo = nc.dram_tensor("wo", [D, D], F32, kind="ExternalInput").ap()
    bo = nc.dram_tensor("bo", [D], F32, kind="ExternalInput").ap()
    w1t = nc.dram_tensor("w1t", [D, H], F32, kind="ExternalInput").ap()
    b1 = nc.dram_tensor("b1", [H], F32, kind="ExternalInput").ap()
    w2t = nc.dram_tensor("w2t", [H, D], F32, kind="ExternalInput").ap()
    b2 = nc.dram_tensor("b2", [D], F32, kind="ExternalInput").ap()
    ident_in = nc.dram_tensor("ident_in", [128, 128], BF16, kind="ExternalInput").ap()
    y = nc.dram_tensor("y", [QO, D], F32, kind="ExternalOutput").ap()
    if debug:
        dbg_wsm = nc.dram_tensor("dbg_wsm", [128, 192], F32, kind="ExternalOutput").ap()
        dbg_x1 = nc.dram_tensor("dbg_x1", [QO, D], F32, kind="ExternalOutput").ap()
        dbg_wf = nc.dram_tensor("dbg_wf", [128, 5632], F32, kind="ExternalOutput").ap()
        dbg_wo = nc.dram_tensor("dbg_wo", [8, 128, D], BF16, kind="ExternalOutput").ap()
        dbg_q = nc.dram_tensor("dbg_q", [128, 8 * QO], BF16, kind="ExternalOutput").ap()
        dbg_k = nc.dram_tensor("dbg_k", [128, 8 * 4096], BF16, kind="ExternalOutput").ap()
        dbg_v = nc.dram_tensor("dbg_v", [128, NKT * VTILE], BF16, kind="ExternalOutput").ap()
        dbg_av = nc.dram_tensor("dbg_av", [128, 8 * QO], BF16, kind="ExternalOutput").ap()

    dscr = nc.dram_tensor("dscr", [1, 1024], F32, kind="Internal").ap()
    mscr = nc.dram_tensor("mscr", [16], F32, kind="Internal").ap()
    sscr = nc.dram_tensor("sscr", [8], F32, kind="Internal").ap()
    rscr = nc.dram_tensor("rscr", [128], F32, kind="Internal").ap()
    wo_bf = nc.dram_tensor("wo_bf", [8, 128, D], BF16, kind="Internal").ap()
    x1scr = nc.dram_tensor("x1scr", [QO, D], F32, kind="Internal").ap()

    with ExitStack() as ctx:
        e = ctx.enter_context
        karena = e(nc.sbuf_tensor("karena", [128, 8 * 4096], BF16))
        varena = e(nc.sbuf_tensor("varena", [128, NKT * VTILE], BF16))
        qarena = e(nc.sbuf_tensor("qarena", [128, 8 * QO], BF16))
        wkA = e(nc.sbuf_tensor("wkA", [128, 4096], BF16))
        wkB = e(nc.sbuf_tensor("wkB", [128, 5632], BF16))
        wf = e(nc.sbuf_tensor("wf", [128, 5632], F32))
        wsm = e(nc.sbuf_tensor("wsm", [128, 192], F32))
        wri = e(nc.sbuf_tensor("wri", [128, 512], I32))
        ident = e(nc.sbuf_tensor("ident", [128, 128], BF16))
        wsb = e(nc.sbuf_tensor("wsb", [128, 32], BF16))
        b2row = e(nc.sbuf_tensor("b2row", [1, 1024], BF16))
        s2invrow = e(nc.sbuf_tensor("s2invrow", [1, 128], BF16))

        psBIG = e(nc.psum_tensor("psBIG", [128, 2048], F32))
        psC = e(nc.psum_tensor("psC", [128, 1024], F32))
        psTa = e(nc.psum_tensor("psTa", [128, 512], BF16))
        psTb = e(nc.psum_tensor("psTb", [128, 512], BF16))

        qLD = Ctr(e(nc.semaphore("qLD")), 16)
        qLG = Ctr(e(nc.semaphore("qLG")), 16)
        qOUT = Ctr(e(nc.semaphore("qOUT")), 16)
        cPE = Ctr(e(nc.semaphore("cPE")))
        cP0 = Ctr(e(nc.semaphore("cP0")))
        cTN = Ctr(e(nc.semaphore("cTN")))
        cDV = Ctr(e(nc.semaphore("cDV")))
        cAC = Ctr(e(nc.semaphore("cAC")))

        block = e(nc.Block())

        @block.sync
        def _(sync):
            qLD.inc(sync.dma_start(out=ident[:], in_=ident_in[:]))
            for fc in range(8):
                qLD.inc(sync.dma_start(
                    out=wsm[:, BQ + fc:BQ + fc + 1],
                    in_=bqkv[fc * 128:(fc + 1) * 128].rearrange("(p a) -> p a", a=1)))
                qLD.inc(sync.dma_start(
                    out=wsm[:, BK + fc:BK + fc + 1],
                    in_=bqkv[1024 + fc * 128:1024 + (fc + 1) * 128].rearrange("(p a) -> p a", a=1)))
            inst = sync.dma_start(out=wf[:, 4096:5120],
                                  in_=bqkv[2048:3072].rearrange("(a n) -> a n", a=1)
                                  .partition_broadcast(128))
            qLD.inc(inst)
        LD_CONST = qLD.n

        @block.vector
        def _(vector):
            for inst in [vector.memset(wsm[:, NEGM:NEGM + 1], -16.0),
                         vector.memset(wsb[:, :], 0.0),
                         vector.memset(wsb[0:64, 0:1], 1.0),
                         vector.memset(wsb[64:128, 1:2], 1.0),
                         vector.memset(wsb[:, 2:3], 1.0),
                         vector.memset(wsm[:, 96:97], 1.0),
                         vector.memset(varena[:, :], 1.0)]:
                cDV.inc(inst)
                vector.wait_ge(cDV.sem, cDV.n)
        DV_CONST = cDV.n

        ld0p = qLD.n
        stage_owner = [None, None]
        nJob = [0]
        Psync, Pac, Pdv, Ppe = [], [], [], []
        quarter_pred = {0: None, 1: None, 2: None, 3: None}
        nAC = [cAC.n]
        nDV = [cDV.n]
        nPE = [cPE.n]
        wr_g = wqkv.rearrange("(c p) n -> p c n", p=128)
        xr_kv = xT.rearrange("(c p) n -> p c n", p=128)
        xr_q = xTq.rearrange("(c p) n -> p c n", p=128)
        bufree = [None, None]
        wkA_free = [None]
        wkB_free = [None]

        def add_dma(src):
            j = nJob[0]
            nJob[0] += 1
            sidx = j % 2
            prev = stage_owner[sidx]
            Psync.append((prev, src,
                          wf[:, sidx * 2048:sidx * 2048 + 2048]
                          .rearrange("p (c n) -> p c n", n=512)))
            return sidx, ld0p + (j + 1) * 16

        def add_wjob(wcol0, fh, wh, dst_t, free_pe):
            src = wr_g[:, wh * 4:(wh + 1) * 4,
                       wcol0 + fh * 512:wcol0 + fh * 512 + 512]
            sidx, dcnt = add_dma(src)
            nDV[0] += 1
            Pdv.append(("wcast", dcnt, free_pe,
                        wf[:, sidx * 2048:sidx * 2048 + 2048],
                        dst_t[:, wh * 2048:(wh + 1) * 2048]))
            stage_owner[sidx] = ("dv", nDV[0])
            return nDV[0]

        def add_xjob(xr, t, h, dst_ap, free_pe):
            src = xr[:, h * 4:(h + 1) * 4, t * 512:t * 512 + 512]
            sidx, dcnt = add_dma(src)
            nAC[0] += 1
            Pac.append((dcnt, free_pe,
                        wf[:, sidx * 2048:sidx * 2048 + 2048], dst_ap))
            stage_owner[sidx] = ("ac", nAC[0])
            return nAC[0]

        def add_group(wac, wdv, g, mms):
            nPE[0] += 1
            Ppe.append((wac, wdv, quarter_pred[g], mms))
            return nPE[0]

        def build_kv(sub, wcol0):
            xc, wc, grp = {}, {}, {}
            pre = [("W", (0, 0)), ("X", (0, 0)), ("W", (0, 1)), ("X", (0, 1)),
                   ("W", (1, 0)), ("X", (1, 0)), ("W", (1, 1)), ("X", (1, 1))]
            for kind, arg in pre:
                if kind == "W":
                    fh, wh = arg
                    wc[(fh, wh)] = add_wjob(
                        wcol0, fh, wh, wkA if fh == 0 else wkB,
                        (wkA_free if fh == 0 else wkB_free)[0])
                else:
                    t, h = arg
                    dst = qarena[:, (t % 2) * 4096 + h * 2048:
                                 (t % 2) * 4096 + h * 2048 + 2048]
                    xc[(t, h)] = add_xjob(xr_kv, t, h, dst, bufree[t % 2])
            for t in range(8):
                xbuf = qarena[:, (t % 2) * 4096:(t % 2) * 4096 + 4096]
                for q in range(8):
                    fh, g = q // 4, q % 4
                    wk = wkA if fh == 0 else wkB
                    ps = psBIG[:, g * 512:g * 512 + 512]
                    mms = []
                    for dc in range(8):
                        if sub == "K":
                            mms.append((wk[:, dc * 512 + g * 128:
                                           dc * 512 + g * 128 + 128],
                                        xbuf[:, dc * 512:dc * 512 + 512], ps))
                        else:
                            mms.append((xbuf[:, dc * 512 + g * 128:
                                             dc * 512 + g * 128 + 128],
                                        wk[:, dc * 512:dc * 512 + 512], ps))
                    gc = add_group(xc[(t, 1)], wc[(fh, 1)], g, mms)
                    grp[(t, q)] = gc
                    nDV[0] += 1
                    fc = fh * 4 + g
                    if sub == "K":
                        Pdv.append(("evacKQ", gc, ps,
                                    karena[:, fc * 4096 + t * 512:
                                           fc * 4096 + t * 512 + 512],
                                    BK + fc))
                    else:
                        kt = t * 4 + g
                        h0 = fh * 8
                        dst = varena[:, kt * VTILE:(kt + 1) * VTILE].rearrange(
                            "p (h v) -> p h v", v=VSTRIDE)[:, h0:h0 + 8, 0:64]
                        Pdv.append(("evacV", gc,
                                    ps.rearrange("p (h v) -> p h v", v=64),
                                    dst,
                                    wf[:, 4096 + fh * 512:4096 + fh * 512 + 512]
                                    .rearrange("p (h v) -> p h v", v=64)))
                    quarter_pred[g] = nDV[0]
                bufree[t % 2] = grp[(t, 7)]
                if t + 2 <= 7:
                    for h in (0, 1):
                        dst = qarena[:, ((t + 2) % 2) * 4096 + h * 2048:
                                     ((t + 2) % 2) * 4096 + h * 2048 + 2048]
                        xc[(t + 2, h)] = add_xjob(xr_kv, t + 2, h, dst,
                                                  bufree[t % 2])
            wkA_free[0] = grp[(7, 3)]
            wkB_free[0] = grp[(7, 7)]

        def build_q():
            for fh in range(2):
                xc, wc, grp = {}, {}, {}
                wc[0] = add_wjob(0, fh, 0, wkA, wkA_free[0])
                xc[(0, 0)] = add_xjob(xr_q, 0, 0, wkB[:, 0:2048], wkB_free[0])
                wc[1] = add_wjob(0, fh, 1, wkA, wkA_free[0])
                xc[(0, 1)] = add_xjob(xr_q, 0, 1, wkB[:, 2048:4096],
                                      wkB_free[0])
                for t in range(4):
                    for g in range(4):
                        ps = psBIG[:, g * 512:g * 512 + 512]
                        mms = [(wkA[:, dc * 512 + g * 128:
                                    dc * 512 + g * 128 + 128],
                                wkB[:, dc * 512:dc * 512 + 512], ps)
                               for dc in range(8)]
                        gc = add_group(xc[(t, 1)], wc[1], g, mms)
                        grp[(t, g)] = gc
                        nDV[0] += 1
                        fc = fh * 4 + g
                        Pdv.append(("evacKQ", gc, ps,
                                    qarena[:, fc * 2048 + t * 512:
                                           fc * 2048 + t * 512 + 512],
                                    BQ + fc))
                        quarter_pred[g] = nDV[0]
                    if t + 1 <= 3:
                        for h in (0, 1):
                            xc[(t + 1, h)] = add_xjob(
                                xr_q, t + 1, h,
                                wkB[:, h * 2048:h * 2048 + 2048], grp[(t, 3)])
                wkA_free[0] = grp[(3, 3)]
                wkB_free[0] = grp[(3, 3)]

        build_kv("K", 1024)
        build_kv("V", 2048)
        build_q()

        @block.sync
        def _(sync):
            for prev, src, dstap in Psync:
                if prev is not None:
                    eng, v = prev
                    sync.wait_ge(cAC.sem if eng == "ac" else cDV.sem, v)
                qLD.inc(sync.dma_start(out=dstap, in_=src))
        assert qLD.n == ld0p + 60 * 16

        @block.scalar
        def _(scalar):
            for dcnt, free_pe, srcap, dstap in Pac:
                scalar.wait_ge(qLD.sem, dcnt)
                if free_pe is not None:
                    scalar.wait_ge(cPE.sem, free_pe)
                cAC.inc(scalar.activation(dstap, srcap, AF.Copy))
        assert cAC.n == 48

        @block.vector
        def _(vector):
            for it in Pdv:
                if it[0] == "wcast":
                    _, dcnt, free_pe, srcap, dstap = it
                    vector.wait_ge(qLD.sem, dcnt)
                    if free_pe is not None:
                        vector.wait_ge(cPE.sem, free_pe)
                    cDV.inc(vector.tensor_copy(dstap, srcap))
                elif it[0] == "evacKQ":
                    _, gc, ps, dstap, bcol = it
                    vector.wait_ge(cPE.sem, gc)
                    cDV.inc(vector.tensor_scalar(dstap, ps,
                                                 wsm[:, bcol:bcol + 1],
                                                 None, OP.add))
                else:
                    _, gc, psv, dstap, bv = it
                    vector.wait_ge(cPE.sem, gc)
                    cDV.inc(vector.tensor_tensor(dstap, psv, bv, OP.add))
        assert cDV.n == nDV[0]

        @block.tensor
        def _(t):
            for wac, wdv, qp, mms in Ppe:
                t.wait_ge(cAC.sem, wac)
                t.wait_ge(cDV.sem, wdv)
                if qp is not None:
                    t.wait_ge(cDV.sem, qp)
                for i, (lh, rh_, ps) in enumerate(mms):
                    inst = t.matmul(ps, lh, rh_, start=(i == 0), stop=(i == 7))
                cPE.inc(inst)
        assert cPE.n == 160

        PE_PROJ = cPE.n
        DV_PROJ = cDV.n

        pe0 = cPE.n
        ac0 = cAC.n
        dv0 = cDV.n
        lg0 = qLG.n
        ldA0 = qLD.n
        steps = [(h, qc) for h in range(NH) for qc in range(2)]
        NSTEP = len(steps)
        GKT = NSTEP * NKT

        def st_inc(gkt):
            return pe0 + (gkt + 1 if gkt < 2 else 2 * gkt)

        def pv_inc(gkt):
            return pe0 + 2 * gkt + 3 if gkt < GKT - 1 else pe0 + 2 * gkt + 2

        def exp_inc(gkt):
            return ac0 + (gkt // NKT) * 33 + (gkt % NKT) + 1

        def ac_copy(m):
            return ac0 + (m + 1) * 33

        def r_inc(m):
            return dv0 + m * 2 + 1

        def tt_inc(m):
            return dv0 + m * 2 + 2

        def lg_back(m):
            return lg0 + (m * 4 + 3) * 16

        def lg_move(m):
            return lg0 + (m * 4 + 4) * 16

        side_dma = []
        side_dve = []

        NWS = 128

        def ws_dma(i):
            def f(sync):
                if i < 2:
                    sync.wait_ge(cDV.sem, DV_PROJ)
                else:
                    sync.wait_ge(cP0.sem, i - 1)
                if i < 64:
                    r, c = divmod(i, 8)
                    srcap = w1t[r * 128:(r + 1) * 128, c * 512:(c + 1) * 512]
                else:
                    r, c = divmod(i - 64, 2)
                    srcap = w2t[r * 128:(r + 1) * 128, c * 512:(c + 1) * 512]
                inst = sync.dma_start(
                    out=wf[:, 4096 + (i % 2) * 512:4608 + (i % 2) * 512],
                    in_=srcap)
                qLD.inc(inst)
            return f

        for i in range(NWS):
            side_dma.append(ws_dma(i))

        def ws_b1(sync):
            sync.wait_ge(cP0.sem, NWS + 2)
            inst = sync.dma_start(out=rscr.rearrange("(p a) -> p a", a=1),
                                  in_=wsm[:, ST + 4:ST + 5])
            qLD.inc(inst)

        def ws_b2(sync):
            sync.wait_ge(qLD.sem, ldA0 + (NWS + 1) * 16)
            sync.wait_ge(cPE.sem, st_inc(17 * NKT))
            inst = sync.dma_start(out=wsm[0:1, 64:192],
                                  in_=rscr.rearrange("(a n) -> a n", a=1))
            qLD.inc(inst)

        def ws_b3(sync):
            sync.wait_ge(cTN.sem, 1)
            inst = sync.dma_start(out=rscr.rearrange("(p a) -> p a", a=1),
                                  in_=wsm[:, ST + 5:ST + 6])
            qLD.inc(inst)

        def ws_b4(sync):
            sync.wait_ge(qLD.sem, ldA0 + (NWS + 3) * 16)
            sync.wait_ge(cPE.sem, st_inc(19 * NKT))
            inst = sync.dma_start(out=wsm[0:1, 64:192],
                                  in_=rscr.rearrange("(a n) -> a n", a=1))
            qLD.inc(inst)

        def ws_b5(sync):
            sync.wait_ge(cTN.sem, 10)
            inst = sync.dma_start(out=sscr.rearrange("(a n) -> a n", a=1),
                                  in_=wsm[0:1, ST:ST + 8])
            qLD.inc(inst)

        def ws_b6(sync):
            sync.wait_ge(qLD.sem, ldA0 + (NWS + 5) * 16)
            sync.wait_ge(cPE.sem, st_inc(21 * NKT))
            inst = sync.dma_start(out=wsm[:, WS:WS + 8],
                                  in_=sscr.rearrange("(a n) -> a n", a=1)
                                  .partition_broadcast(128))
            qLD.inc(inst)

        for f_ in (ws_b1, ws_b2, ws_b3, ws_b4, ws_b5, ws_b6):
            side_dma.append(f_)
        LDW = NWS + 6

        ws_dve = []

        def ws_rowsum(i):
            def f(vector):
                vector.wait_ge(qLD.sem, ldA0 + (i + 1) * 16)
                inst = vector.reduce_sum(
                    wsm[:, 64 + i:65 + i],
                    wf[:, 4096 + (i % 2) * 512:4608 + (i % 2) * 512],
                    AX.X, apply_absolute_value=True)
                cP0.inc(inst)
            return f

        def ws_acc(k):
            def f(vector):
                vector.wait_ge(cP0.sem, NWS)
                inst = vector.reduce_sum(wsm[:, ST + 4 + k:ST + 5 + k],
                                         wsm[:, 64 + 64 * k:128 + 64 * k],
                                         AX.X)
                cP0.inc(inst)
            return f

        for i in range(NWS):
            ws_dve.append(ws_rowsum(i))
        ws_dve.append(ws_acc(0))
        ws_dve.append(ws_acc(1))

        def wsf_r(k):
            def f(vector):
                vector.wait_ge(qLD.sem, ldA0 + (NWS + 2 + 2 * k) * 16)
                inst = vector.reduce_sum(wsm[0:1, ST + k:ST + k + 1],
                                         wsm[0:1, 64:192], AX.X)
                cTN.inc(inst)
                vector.wait_ge(cTN.sem, cTN.n)
            return f

        def wsf_op(fn):
            def f(vector):
                cTN.inc(fn(vector))
                vector.wait_ge(cTN.sem, cTN.n)
            return f

        side_dve.append(wsf_r(0))
        side_dve.append(wsf_r(1))
        for fn in [
            lambda v: v.tensor_scalar(wsm[0:1, ST:ST + 1], wsm[0:1, ST:ST + 1],
                                      1.0 / (D * H), EPS_Q, OP.mult, OP.max),
            lambda v: v.tensor_scalar(wsm[0:1, ST + 1:ST + 2], wsm[0:1, ST + 1:ST + 2],
                                      1.0 / (D * H), EPS_Q, OP.mult, OP.max),
            lambda v: v.tensor_scalar_mul(wsm[0:1, ST + 2:ST + 3], wsm[0:1, ST:ST + 1], 0.5),
            lambda v: v.tensor_scalar_mul(wsm[0:1, ST + 3:ST + 4], wsm[0:1, ST:ST + 1], -0.5),
            lambda v: v.tensor_scalar_mul(wsm[0:1, ST + 4:ST + 5], wsm[0:1, ST + 1:ST + 2], 0.5),
            lambda v: v.tensor_scalar_mul(wsm[0:1, ST + 5:ST + 6], wsm[0:1, ST + 1:ST + 2], -0.5),
            lambda v: v.tensor_scalar_mul(wsm[0:1, ST + 6:ST + 7], wsm[0:1, ST:ST + 1], 1.0 / 127.0),
            lambda v: v.tensor_scalar_mul(wsm[0:1, ST + 7:ST + 8], wsm[0:1, ST + 1:ST + 2], 1.0 / 127.0),
        ]:
            side_dve.append(wsf_op(fn))

        def w1_dma(i):
            c, p = divmod(i, 4)
            def f(sync):
                sync.wait_ge(cPE.sem, st_inc((4 * c + 4) * NKT - 1))
                sync.wait_ge(cTN.sem, 10 + 2 * i)
                inst = sync.dma_start(out=wf[:, 4096:5120],
                                      in_=w1t[c * 128:(c + 1) * 128,
                                              p * 1024:(p + 1) * 1024])
                qLD.inc(inst)
            return f

        def w1_dve(i):
            c, p = divmod(i, 4)
            def f(vector):
                vector.wait_ge(qLD.sem, ldA0 + (LDW + i + 1) * 16)
                inst = vector.tensor_scalar(wf[:, 1024:2048], wf[:, 4096:5120],
                                            wsm[:, C_NT1:C_NT1 + 1], None, OP.is_lt)
                cTN.inc(inst)
                inst = vector.scalar_tensor_tensor(
                    karena[:, c * 4096 + p * 1024:c * 4096 + p * 1024 + 1024],
                    wf[:, 4096:5120], wsm[:, C_T1:C_T1 + 1], wf[:, 1024:2048],
                    OP.is_gt, OP.subtract)
                cTN.inc(inst)
            return f

        for i in range(32):
            side_dma.append(w1_dma(i))
            side_dve.append(w1_dve(i))

        def wo_in(c):
            def f(sync):
                sync.wait_ge(cTN.sem, 74 + c)
                inst = sync.dma_start(out=wf[:, 4096:5120], in_=wo[c * 128:(c + 1) * 128, :])
                qLD.inc(inst)
            return f

        def wo_out(c):
            def f(sync):
                sync.wait_ge(cTN.sem, 74 + c + 1)
                inst = sync.dma_start(out=wo_bf[c], in_=wkB[:, 3072:4096])
                qLD.inc(inst)
            return f

        def wo_cast(c):
            def f(vector):
                vector.wait_ge(qLD.sem, ldA0 + (LDW + 32 + 2 * c + 1) * 16)
                if c >= 1:
                    vector.wait_ge(qLD.sem, ldA0 + (LDW + 32 + 2 * c) * 16)
                inst = vector.tensor_copy(wkB[:, 3072:4096], wf[:, 4096:5120])
                cTN.inc(inst)
            return f

        for c in range(8):
            side_dma.append(wo_in(c))
            side_dma.append(wo_out(c))
            side_dve.append(wo_cast(c))

        def b1_dma(j):
            def f(sync):
                sync.wait_ge(cTN.sem, 82 + j)
                inst = sync.dma_start(out=wf[:, 4096:4608],
                                      in_=b1[j * 512:(j + 1) * 512]
                                      .rearrange("(a n) -> a n", a=1).partition_broadcast(128))
                qLD.inc(inst)
            return f

        def b1_cast(j):
            def f(vector):
                vector.wait_ge(qLD.sem, ldA0 + (LDW + 48 + j + 1) * 16)
                inst = vector.tensor_copy(wkA[:, j * 512:(j + 1) * 512], wf[:, 4096:4608])
                cTN.inc(inst)
            return f

        for j in range(8):
            side_dma.append(b1_dma(j))
            side_dve.append(b1_cast(j))

        def b2_dma(sync):
            sync.wait_ge(cTN.sem, 90)
            inst = sync.dma_start(out=wf[0:1, 4096:5120],
                                  in_=b2.rearrange("(a n) -> a n", a=1))
            qLD.inc(inst)

        def b2_cast(vector):
            vector.wait_ge(qLD.sem, ldA0 + (LDW + 57) * 16)
            inst = vector.tensor_copy(b2row[0:1, :], wf[0:1, 4096:5120])
            cTN.inc(inst)

        def bo_dma(sync):
            sync.wait_ge(cTN.sem, 74)
            inst = sync.dma_start(out=wf[:, 1024:2048],
                                  in_=bo.rearrange("(a n) -> a n", a=1).partition_broadcast(128))
            qLD.inc(inst)

        side_dma.append(b2_dma)
        side_dve.append(b2_cast)
        side_dma.append(bo_dma)
        N_SIDE_DMA = len(side_dma)
        N_SIDE_DVE = len(side_dve)
        side_earliest = ([17, 19] + [19] * 8
                         + [max(22, 4 * (i // 4) + 5) for i in range(32)]
                         + [99] * 17)
        assert len(side_earliest) == N_SIDE_DVE

        def emit_S(t, gkt):
            m, kt = divmod(gkt, NKT)
            h, qc = steps[m]
            ch, rh = h // 2, (h % 2) * 64
            stb = (gkt % 2) * 1024
            lhsK = karena[rh:rh + 64, ch * 4096 + kt * 128:ch * 4096 + kt * 128 + 128]
            for half in range(2):
                inst = t.matmul(psBIG[:, stb + half * 512:stb + half * 512 + 512],
                                lhsK,
                                qarena[rh:rh + 64,
                                       ch * 2048 + qc * 1024 + half * 512:
                                       ch * 2048 + qc * 1024 + half * 512 + 512],
                                start=True, stop=True)
            cPE.inc(inst)
            assert cPE.n == st_inc(gkt)

        def emit_PV(t, gkt):
            m, kt = divmod(gkt, NKT)
            h, qc = steps[m]
            t.wait_ge(cAC.sem, exp_inc(gkt))
            if kt == 0 and m >= 1:
                t.wait_ge(cAC.sem, ac_copy(m - 1))
            lhsV = varena[:, kt * VTILE + h * VSTRIDE:kt * VTILE + h * VSTRIDE + VSTRIDE]
            pt = wkB[:, (gkt % 2) * 1024:(gkt % 2) * 1024 + 1024]
            for half in range(2):
                inst = t.matmul(psC[0:65, half * 512:half * 512 + 512],
                                lhsV, pt[:, half * 512:half * 512 + 512],
                                start=(kt == 0), stop=(kt == NKT - 1))
            cPE.inc(inst)
            assert cPE.n == pv_inc(gkt)

        @block.tensor
        def _(t):
            t.wait_ge(cDV.sem, DV_PROJ)
            emit_S(t, 0)
            emit_S(t, 1)
            for gkt in range(GKT - 2):
                emit_PV(t, gkt)
                emit_S(t, gkt + 2)
            emit_PV(t, GKT - 2)
            emit_PV(t, GKT - 1)

        @block.sync
        def _(sync):
            for f in side_dma:
                f(sync)
        LD_SIDE = qLD.n
        assert LD_SIDE == ldA0 + N_SIDE_DMA * 16

        @block.scalar
        def _(scalar):
            for m, (h, qc) in enumerate(steps):
                for kt in range(NKT):
                    gkt = m * NKT + kt
                    stb = (gkt % 2) * 1024
                    scalar.wait_ge(cPE.sem, st_inc(gkt))
                    if gkt >= 2:
                        scalar.wait_ge(cPE.sem, pv_inc(gkt - 2))
                    inst = scalar.activation(wkB[:, (gkt % 2) * 1024:(gkt % 2) * 1024 + 1024],
                                             psBIG[:, stb:stb + 1024], AF.Exp,
                                             bias=wsm[:, NEGM:NEGM + 1], scale=0.125)
                    cAC.inc(inst)
                    assert cAC.n == exp_inc(gkt)
                scalar.wait_ge(cPE.sem, pv_inc(m * NKT + NKT - 1))
                if m >= 1:
                    scalar.wait_ge(cDV.sem, tt_inc(m - 1))
                inst = scalar.activation(wf[0:65, 0:1024], psC[0:65, 0:1024], AF.Copy)
                cAC.inc(inst)
                assert cAC.n == ac_copy(m)

        @block.vector
        def _(vector):
            popped = 0
            wpopped = [0]
            for m, (h, qc) in enumerate(steps):
                vector.wait_ge(cAC.sem, ac_copy(m))
                inst = vector.reciprocal(wf[64:65, 3072:4096], wf[64:65, 0:1024])
                cDV.inc(inst)
                assert cDV.n == r_inc(m)
                vector.wait_ge(qLG.sem, lg_back(m))
                if m >= 1:
                    vector.wait_ge(qLG.sem, lg_move(m - 1))
                inst = vector.tensor_tensor(wkB[0:64, 2048:3072], wf[0:64, 0:1024],
                                            wf[0:64, 3072:4096], OP.mult)
                cDV.inc(inst)
                assert cDV.n == tt_inc(m)
                if m < 16:
                    for _ in range(9):
                        if wpopped[0] < len(ws_dve):
                            ws_dve[wpopped[0]](vector)
                            wpopped[0] += 1
                for _ in range(6):
                    if popped < len(side_dve) and side_earliest[popped] <= m:
                        side_dve[popped](vector)
                        popped += 1
                    else:
                        break
            for f in ws_dve[wpopped[0]:]:
                f(vector)
            for f in side_dve[popped:]:
                f(vector)
        assert cTN.n == 91

        @block.gpsimd
        def _(g):
            for m, (h, qc) in enumerate(steps):
                ch, rh = h // 2, (h % 2) * 64
                g.wait_ge(cDV.sem, r_inc(m))
                inst = g.dma_start(out=dscr[:], in_=wf[64:65, 3072:4096])
                qLG.inc(inst)
                g.wait_ge(qLG.sem, qLG.n)
                inst = g.dma_start(out=wsm[64:65, 4:12], in_=dscr[0:1, 0:8])
                qLG.inc(inst)
                g.wait_ge(qLG.sem, qLG.n)
                if m >= 1:
                    g.wait_ge(cDV.sem, tt_inc(m - 1))
                inst = g.dma_start(out=wf[0:64, 3072:4096], in_=dscr.partition_broadcast(64))
                qLG.inc(inst)
                g.wait_ge(cDV.sem, tt_inc(m))
                inst = g.dma_start(
                    out=qarena[rh:rh + 64, ch * 2048 + qc * 1024:ch * 2048 + qc * 1024 + 1024],
                    in_=wkB[0:64, 2048:3072])
                qLG.inc(inst)

        assert cPE.n == pe0 + NSTEP * NKT * 2
        assert cAC.n == ac0 + NSTEP * 33
        assert cDV.n == dv0 + NSTEP * 2
        assert qLG.n == lg0 + NSTEP * 4 * 16
        PE_ATT = cPE.n
        LG_ATT = qLG.n

        if debug:
            @block.sync
            def _(sync):
                sync.wait_ge(cPE.sem, PE_ATT)
                sync.wait_ge(qLG.sem, LG_ATT)
                qLD.inc(sync.dma_start(out=dbg_k[:], in_=karena[:, :]))
                qLD.inc(sync.dma_start(out=dbg_v[:], in_=varena[:, :]))
                qLD.inc(sync.dma_start(out=dbg_av[:], in_=qarena[:, :]))
        DBG_DUMP = qLD.n

        lg0 = qLG.n
        dv0 = cDV.n

        @block.gpsimd
        def _(g):
            g.wait_ge(cPE.sem, PE_ATT)
            g.wait_ge(cTN.sem, 91)
            if debug:
                g.wait_ge(qLD.sem, DBG_DUMP)
            for i in range(32):
                if i == 1:
                    g.wait_ge(cDV.sem, dv0)
                elif i >= 2:
                    g.wait_ge(cDV.sem, dv0 + (i - 2) * 2 + 2)
                stg = wf[:, 4096:5120] if i % 2 == 0 else wf[:, 0:1024]
                inst = g.dma_start(out=stg, in_=w2t[i * 128:(i + 1) * 128, :])
                qLG.inc(inst)

        @block.vector
        def _(vector):
            for i in range(32):
                stg = wf[:, 4096:5120] if i % 2 == 0 else wf[:, 0:1024]
                vector.wait_ge(qLG.sem, lg0 + (i + 1) * 16)
                inst = vector.tensor_scalar(wf[:, 2048:3072], stg,
                                            wsm[:, C_NT2:C_NT2 + 1], None, OP.is_lt)
                cDV.inc(inst)
                inst = vector.scalar_tensor_tensor(
                    varena[:, i * 1024:(i + 1) * 1024],
                    stg, wsm[:, C_T2:C_T2 + 1], wf[:, 2048:3072],
                    OP.is_gt, OP.subtract)
                cDV.inc(inst)
        assert cDV.n == dv0 + 64
        qLG.n = lg0 + 32 * 16
        DV_W = cDV.n
        LG_W = qLG.n

        ld0 = qLD.n
        lg0 = qLG.n
        pe0 = cPE.n
        dv0 = cDV.n
        ac0 = cAC.n

        def pA_xdma(t):
            return ld0 + (t * 3 + 1) * 16

        def pA_wo(t, dh):
            return ld0 + (t * 3 + 2 + dh) * 16

        def pA_pe(t, k):
            return pe0 + t * 3 + k

        def pA_dv(t, k):
            return dv0 + t * 40 + k

        def pA_ac(t, k):
            return ac0 + t * 2 + k

        def pA_x1dma(t):
            return lg0 + (t + 1) * 16

        @block.sync
        def _(sync):
            sync.wait_ge(qLG.sem, LG_ATT)
            sync.wait_ge(cDV.sem, DV_W)
            for t in range(NTT):
                if t >= 1:
                    sync.wait_ge(cDV.sem, pA_dv(t - 1, 1))
                inst = sync.dma_start(out=wf[:, 0:1024],
                                      in_=x_own[t * 128:(t + 1) * 128, :])
                qLD.inc(inst)
                for dh in range(2):
                    if t * 2 + dh >= 1:
                        pt_, pdh = divmod(t * 2 + dh - 1, 2)
                        sync.wait_ge(cPE.sem, pA_pe(pt_, pdh + 1))
                    inst = sync.dma_start(
                        out=wkB[:, 0:4096].rearrange("p (c x) -> p c x", x=512),
                        in_=wo_bf[:, :, dh * 512:dh * 512 + 512].rearrange("c p x -> p c x"))
                    qLD.inc(inst)

        @block.tensor
        def _(t_):
            t_.wait_ge(qLG.sem, LG_ATT)
            if debug:
                t_.wait_ge(qLD.sem, DBG_DUMP)
            for t in range(NTT):
                ps = psBIG[:, (t % 2) * 1024:(t % 2) * 1024 + 1024]
                if t >= 2:
                    t_.wait_ge(cDV.sem, pA_dv(t - 2, 2))
                for dh in range(2):
                    t_.wait_ge(qLD.sem, pA_wo(t, dh))
                    for c in range(8):
                        inst = t_.matmul(ps[:, dh * 512:dh * 512 + 512],
                                         qarena[:, c * 2048 + t * 128:c * 2048 + t * 128 + 128],
                                         wkB[:, c * 512:c * 512 + 512],
                                         start=(c == 0), stop=(c == 7))
                    cPE.inc(inst)
                    assert cPE.n == pA_pe(t, dh + 1)
                t_.wait_ge(cDV.sem, pA_dv(t, 38))
                if t >= 1:
                    t_.wait_ge(cDV.sem, pA_dv(t - 1, 40))
                for dc in range(8):
                    dst = psTa if dc < 4 else psTb
                    inst = t_.transpose(dst[:, (dc % 4) * 128:(dc % 4) * 128 + 128],
                                        wkB[:, 4096 + dc * 128:4096 + dc * 128 + 128],
                                        ident[:])
                cPE.inc(inst)
                assert cPE.n == pA_pe(t, 3)

        @block.vector
        def _(vector):
            vector.wait_ge(qLD.sem, LD_SIDE)
            for t in range(NTT):
                k = 0
                vector.wait_ge(qLD.sem, pA_xdma(t))
                inst = vector.tensor_add(wf[:, 2048:3072], wf[:, 0:1024], wf[:, 1024:2048])
                k += 1; cDV.inc(inst)
                vector.wait_ge(cPE.sem, pA_pe(t, 2))
                ps = psBIG[:, (t % 2) * 1024:(t % 2) * 1024 + 1024]
                inst = vector.tensor_add(wf[:, 2048:3072], ps, wf[:, 2048:3072])
                k += 1; cDV.inc(inst)
                inst = vector.reduce_sum(wsm[:, ST:ST + 1], wf[:, 2048:3072], AX.X)
                k += 1; cDV.inc(inst); vector.wait_ge(cDV.sem, cDV.n)
                inst = vector.tensor_scalar_mul(wsm[:, ST + 1:ST + 2], wsm[:, ST:ST + 1], -1.0 / D)
                k += 1; cDV.inc(inst); vector.wait_ge(cDV.sem, cDV.n)
                if t >= 1:
                    vector.wait_ge(qLG.sem, pA_x1dma(t - 1))
                inst = vector.tensor_scalar(wf[:, 3072:4096], wf[:, 2048:3072],
                                            wsm[:, ST + 1:ST + 2], None, OP.add)
                k += 1; cDV.inc(inst)
                vector.wait_ge(cAC.sem, pA_ac(t, 1))
                inst = vector.reduce_sum(wsm[:, ST + 2:ST + 3], wf[:, 4096:5120], AX.X)
                k += 1; cDV.inc(inst); vector.wait_ge(cDV.sem, cDV.n)
                inst = vector.tensor_scalar(wsm[:, ST + 3:ST + 4], wsm[:, ST + 2:ST + 3],
                                            1.0 / D, EPS_LN, OP.mult, OP.add)
                k += 1; cDV.inc(inst)
                vector.wait_ge(cAC.sem, pA_ac(t, 2))
                inst = vector.reciprocal(wsm[:, ST + 5:ST + 6], wsm[:, ST + 4:ST + 5])
                k += 1; cDV.inc(inst); vector.wait_ge(cDV.sem, cDV.n)
                inst = vector.tensor_scalar(wf[:, 3072:4096], wf[:, 3072:4096],
                                            wsm[:, ST + 5:ST + 6], None, OP.mult)
                k += 1; cDV.inc(inst)
                inst = vector.reduce_max(wsm[:, ST + 6:ST + 7], wf[:, 3072:4096], AX.X,
                                         apply_absolute_value=True)
                k += 1; cDV.inc(inst); vector.wait_ge(cDV.sem, cDV.n)
                inst = vector.tensor_scalar_max(wsm[:, ST + 6:ST + 7], wsm[:, ST + 6:ST + 7], EPS_Q)
                k += 1; cDV.inc(inst); vector.wait_ge(cDV.sem, cDV.n)
                inst = vector.reciprocal(wsm[:, ST + 7:ST + 8], wsm[:, ST + 6:ST + 7])
                k += 1; cDV.inc(inst); vector.wait_ge(cDV.sem, cDV.n)
                inst = vector.tensor_scalar_mul(wsm[:, ST + 7:ST + 8], wsm[:, ST + 7:ST + 8], 127.0)
                k += 1; cDV.inc(inst); vector.wait_ge(cDV.sem, cDV.n)
                inst = vector.tensor_tensor(wsm[:, S1C + t:S1C + t + 1],
                                            wsm[:, ST + 6:ST + 7],
                                            wsm[:, C_WQ1:C_WQ1 + 1], OP.mult)
                k += 1; cDV.inc(inst)
                for qch in range(2):
                    k += emit_quant_chunk(vector, cDV, wf, wri,
                                          wsm[:, ST + 7:ST + 8],
                                          wf[:, 3072 + qch * 512:3072 + qch * 512 + 512],
                                          wkB[:, 4096 + qch * 512:4096 + qch * 512 + 512])
                assert k == 38
                vector.wait_ge(cPE.sem, pA_pe(t, 3))
                inst = vector.tensor_copy(
                    qarena.ap().rearrange("p (c n) -> p c n", n=2048)[:, 0:4, t * 128:t * 128 + 128],
                    psTa.ap().rearrange("p (c n) -> p c n", n=128))
                k += 1; cDV.inc(inst)
                inst = vector.tensor_copy(
                    qarena.ap().rearrange("p (c n) -> p c n", n=2048)[:, 4:8, t * 128:t * 128 + 128],
                    psTb.ap().rearrange("p (c n) -> p c n", n=128))
                k += 1; cDV.inc(inst)
                assert cDV.n == pA_dv(t, 40)

        @block.scalar
        def _(scalar):
            for t in range(NTT):
                scalar.wait_ge(cDV.sem, pA_dv(t, 5))
                inst = scalar.activation(wf[:, 4096:5120], wf[:, 3072:4096], AF.Square)
                cAC.inc(inst)
                scalar.wait_ge(cDV.sem, pA_dv(t, 7))
                inst = scalar.activation(wsm[:, ST + 4:ST + 5], wsm[:, ST + 3:ST + 4], AF.Sqrt)
                cAC.inc(inst)

        @block.gpsimd
        def _(g):
            for t in range(NTT):
                g.wait_ge(cDV.sem, pA_dv(t, 38))
                inst = g.dma_start(out=x1scr[t * 128:(t + 1) * 128, :], in_=wf[:, 3072:4096])
                qLG.inc(inst)

        qLD.n = ld0 + NTT * 3 * 16
        assert cPE.n == pe0 + NTT * 3
        assert cDV.n == dv0 + NTT * 40
        assert cAC.n == ac0 + NTT * 2
        X1DMA = [pA_x1dma(t) for t in range(NTT)]
        qLG.n = lg0 + NTT * 16
        DV_A = cDV.n

        ld0 = qLD.n
        pe0 = cPE.n
        dv0 = cDV.n
        ac0 = cAC.n

        lgB0 = qLG.n

        def pB_pe(t, k):
            return pe0 + t * 25 + k

        def pB_lg(t, k):
            return lgB0 + (t * 3 + k) * 16

        def pB_dv(t, k):
            return dv0 + t * 127 + k

        def pB_ac(t, k):
            return ac0 + t * 10 + k

        def pB_x1(t):
            return ld0 + (t + 1) * 16

        def pB_y(t):
            return (t + 1) * 16

        @block.sync
        def _(sync):
            for t in range(NTT):
                sync.wait_ge(cDV.sem, pB_dv(t, 110))
                sync.wait_ge(qLG.sem, X1DMA[t])
                inst = sync.dma_start(out=wf[:, 4096:5120], in_=x1scr[t * 128:(t + 1) * 128, :])
                qLD.inc(inst)

        @block.tensor
        def _(t_):
            t_.wait_ge(cDV.sem, DV_A)
            for t in range(NTT):
                for jh in range(2):
                    if jh == 1:
                        t_.wait_ge(cDV.sem, pB_dv(t, 4))
                    elif t >= 1:
                        t_.wait_ge(cDV.sem, pB_dv(t - 1, 8))
                    for dc in range(8):
                        lhs = qarena[:, dc * 2048 + t * 128:dc * 2048 + t * 128 + 128]
                        for sl in range(4):
                            inst = t_.matmul(
                                psBIG[:, sl * 512:sl * 512 + 512], lhs,
                                karena[:, dc * 4096 + jh * 2048 + sl * 512:
                                       dc * 4096 + jh * 2048 + sl * 512 + 512],
                                start=(dc == 0), stop=(dc == 7))
                            if dc == 7:
                                cPE.inc(inst)
                    assert cPE.n == pB_pe(t, jh * 4 + 4)
                t_.wait_ge(cDV.sem, pB_dv(t, 110))
                if t >= 1:
                    t_.wait_ge(cDV.sem, pB_dv(t - 1, 120))

                def _tr(grp):
                    if grp >= 2:
                        t_.wait_ge(cDV.sem, pB_dv(t, 112 + grp - 2))
                    elif t >= 1:
                        t_.wait_ge(cDV.sem, pB_dv(t - 1, 118 + grp))
                    dstp = psTa if grp % 2 == 0 else psTb
                    for i in range(4):
                        hc = grp * 4 + i
                        inst2 = t_.transpose(
                            dstp[:, i * 128:i * 128 + 128],
                            wkB[:, hc * 128:hc * 128 + 128], ident[:])
                    cPE.inc(inst2)

                def _mm(grp):
                    t_.wait_ge(cDV.sem, pB_dv(t, 112 + grp))
                    for i in range(4):
                        hc = grp * 4 + i
                        lhs = wkB[:, 4096 + (grp % 2) * 512 + i * 128:
                                  4096 + (grp % 2) * 512 + i * 128 + 128]
                        for half in range(2):
                            inst2 = t_.matmul(psC[:, half * 512:half * 512 + 512], lhs,
                                              varena[:, hc * 1024 + half * 512:
                                                     hc * 1024 + half * 512 + 512],
                                              start=(hc == 0), stop=False)
                    cPE.inc(inst2)

                _tr(0)
                _tr(1)
                for grp in range(2, 8):
                    _mm(grp - 2)
                    _tr(grp)
                _mm(6)
                _mm(7)
                t_.wait_ge(cDV.sem, pB_dv(t, 111))
                for half in range(2):
                    inst = t_.matmul(psC[:, half * 512:half * 512 + 512],
                                     s2invrow[0:1, :],
                                     b2row[0:1, half * 512:half * 512 + 512],
                                     start=False, stop=True)
                cPE.inc(inst)
                assert cPE.n == pB_pe(t, 25)

        @block.vector
        def _(vector):
            for t in range(NTT):
                k = 0
                for jh in range(2):
                    for sl in range(4):
                        gidx = jh * 4 + sl
                        vector.wait_ge(cPE.sem, pB_pe(t, gidx + 1))
                        if gidx >= 2:
                            vector.wait_ge(cAC.sem, pB_ac(t, gidx - 1))
                        elif t >= 1:
                            vector.wait_ge(cAC.sem, pB_ac(t - 1, 7 + gidx))
                        inst = vector.scalar_tensor_tensor(
                            wf[:, 4096 + (gidx % 2) * 512:4096 + (gidx % 2) * 512 + 512],
                            psBIG[:, sl * 512:sl * 512 + 512],
                            wsm[:, S1C + t:S1C + t + 1],
                            wkA[:, jh * 2048 + sl * 512:jh * 2048 + sl * 512 + 512],
                            OP.mult, OP.add)
                        k += 1; cDV.inc(inst)
                vector.wait_ge(cAC.sem, pB_ac(t, 8))
                inst = vector.reduce_max(wsm[:, ST:ST + 1], wf[:, 0:4096], AX.X,
                                         apply_absolute_value=True)
                k += 1; cDV.inc(inst); vector.wait_ge(cDV.sem, cDV.n)
                inst = vector.tensor_scalar_max(wsm[:, ST:ST + 1], wsm[:, ST:ST + 1], EPS_Q)
                k += 1; cDV.inc(inst); vector.wait_ge(cDV.sem, cDV.n)
                inst = vector.reciprocal(wsm[:, ST + 1:ST + 2], wsm[:, ST:ST + 1])
                k += 1; cDV.inc(inst); vector.wait_ge(cDV.sem, cDV.n)
                inst = vector.tensor_scalar_mul(wsm[:, ST + 1:ST + 2], wsm[:, ST + 1:ST + 2],
                                                127.0)
                k += 1; cDV.inc(inst); vector.wait_ge(cDV.sem, cDV.n)
                inst = vector.tensor_tensor(wsm[:, ST + 2:ST + 3], wsm[:, ST:ST + 1],
                                            wsm[:, C_WQ2:C_WQ2 + 1], OP.mult)
                k += 1; cDV.inc(inst); vector.wait_ge(cDV.sem, cDV.n)
                inst = vector.reciprocal(wsm[:, ST + 3:ST + 4], wsm[:, ST + 2:ST + 3])
                k += 1; cDV.inc(inst)
                assert k == 14
                for qch in range(8):
                    k += emit_quant_chunk(vector, cDV, wf, wri,
                                          wsm[:, ST + 1:ST + 2],
                                          wf[:, qch * 512:qch * 512 + 512],
                                          wkB[:, qch * 512:qch * 512 + 512])
                assert k == 110
                vector.wait_ge(qLG.sem, pB_lg(t, 3))
                inst = vector.tensor_copy(s2invrow[0:1, :], wsm[0:1, 64:192])
                k += 1; cDV.inc(inst)
                for grp in range(8):
                    tri = 9 + grp if grp < 2 else 8 + 2 * grp
                    vector.wait_ge(cPE.sem, pB_pe(t, tri))
                    if grp >= 2:
                        mmi = 11 + 2 * (grp - 2) if grp - 2 <= 5 else 17 + (grp - 2)
                        vector.wait_ge(cPE.sem, pB_pe(t, mmi))
                    elif t >= 1:
                        pmi = 17 + (grp + 6)
                        vector.wait_ge(cPE.sem, pB_pe(t - 1, pmi))
                    srcp = psTa if grp % 2 == 0 else psTb
                    inst = vector.tensor_copy(
                        wkB[:, 4096 + (grp % 2) * 512:4096 + (grp % 2) * 512 + 512],
                        srcp[:, 0:512])
                    k += 1; cDV.inc(inst)
                vector.wait_ge(qLD.sem, pB_x1(t))
                vector.wait_ge(cPE.sem, pB_pe(t, 25))
                if t >= 1:
                    vector.wait_ge(qOUT.sem, pB_y(t - 1))
                inst = vector.scalar_tensor_tensor(wf[:, 0:1024], psC[:, 0:1024],
                                                   wsm[:, ST + 2:ST + 3],
                                                   wf[:, 4096:5120], OP.mult, OP.add)
                k += 1; cDV.inc(inst)
                inst = vector.reduce_sum(wsm[:, ST + 4:ST + 5], wf[:, 0:1024], AX.X)
                k += 1; cDV.inc(inst); vector.wait_ge(cDV.sem, cDV.n)
                inst = vector.tensor_scalar_mul(wsm[:, ST + 5:ST + 6], wsm[:, ST + 4:ST + 5],
                                                -1.0 / D)
                k += 1; cDV.inc(inst); vector.wait_ge(cDV.sem, cDV.n)
                inst = vector.tensor_scalar(wf[:, 0:1024], wf[:, 0:1024],
                                            wsm[:, ST + 5:ST + 6], None, OP.add)
                k += 1; cDV.inc(inst)
                vector.wait_ge(cAC.sem, pB_ac(t, 9))
                inst = vector.reduce_sum(wsm[:, ST + 7:ST + 8], wf[:, 1024:2048], AX.X)
                k += 1; cDV.inc(inst); vector.wait_ge(cDV.sem, cDV.n)
                inst = vector.tensor_scalar(wsm[:, ST + 6:ST + 7], wsm[:, ST + 7:ST + 8],
                                            1.0 / D, EPS_LN, OP.mult, OP.add)
                k += 1; cDV.inc(inst)
                vector.wait_ge(cAC.sem, pB_ac(t, 10))
                inst = vector.reciprocal(wsm[:, ST + 5:ST + 6], wsm[:, ST + 7:ST + 8])
                k += 1; cDV.inc(inst); vector.wait_ge(cDV.sem, cDV.n)
                inst = vector.tensor_scalar(wf[:, 0:1024], wf[:, 0:1024],
                                            wsm[:, ST + 5:ST + 6], None, OP.mult)
                k += 1; cDV.inc(inst)
                assert k == 127
                assert cDV.n == pB_dv(t, 127)

        @block.scalar
        def _(scalar):
            for t in range(NTT):
                for jh in range(2):
                    for sl in range(4):
                        gidx = jh * 4 + sl
                        scalar.wait_ge(cDV.sem, pB_dv(t, gidx + 1))
                        if gidx < 2 and t >= 1:
                            scalar.wait_ge(qOUT.sem, pB_y(t - 1))
                        inst = scalar.activation(
                            wf[:, gidx * 512:gidx * 512 + 512],
                            wf[:, 4096 + (gidx % 2) * 512:4096 + (gidx % 2) * 512 + 512],
                            AF.Gelu)
                        cAC.inc(inst)
                scalar.wait_ge(cDV.sem, pB_dv(t, 123))
                inst = scalar.activation(wf[:, 1024:2048], wf[:, 0:1024], AF.Square)
                cAC.inc(inst)
                scalar.wait_ge(cDV.sem, pB_dv(t, 125))
                inst = scalar.activation(wsm[:, ST + 7:ST + 8], wsm[:, ST + 6:ST + 7], AF.Sqrt)
                cAC.inc(inst)
                assert cAC.n == pB_ac(t, 10)

        if debug:
            @block.vector
            def _(vector):
                vector.wait_ge(qOUT.sem, 0)
        @block.gpsimd
        def _(g):
            for t in range(NTT):
                g.wait_ge(cDV.sem, pB_dv(t, 14))
                if t >= 1:
                    g.wait_ge(cDV.sem, pB_dv(t - 1, 111))
                inst = g.dma_start(out=rscr.rearrange("(p a) -> p a", a=1),
                                   in_=wsm[:, ST + 3:ST + 4])
                qLG.inc(inst)
                g.wait_ge(qLG.sem, pB_lg(t, 1))
                inst = g.dma_start(out=wsm[64:65, 4:12], in_=rscr[0:8]
                                   .rearrange("(a n) -> a n", a=1))
                qLG.inc(inst)
                g.wait_ge(qLG.sem, pB_lg(t, 2))
                inst = g.dma_start(out=wsm[0:1, 64:192],
                                   in_=rscr.rearrange("(a n) -> a n", a=1))
                qLG.inc(inst)
                g.wait_ge(cDV.sem, pB_dv(t, 127))
                inst = g.dma_start(out=y[t * 128:(t + 1) * 128, :], in_=wf[:, 0:1024])
                qOUT.inc(inst)
            if debug:
                qOUT.inc(g.dma_start(out=dbg_wsm[:], in_=wsm[:, :]))
                qOUT.inc(g.dma_start(out=dbg_x1[:], in_=x1scr[:]))
                qOUT.inc(g.dma_start(out=dbg_wf[:], in_=wf[:, :]))
                qOUT.inc(g.dma_start(out=dbg_wo[:], in_=wo_bf[:]))
                qOUT.inc(g.dma_start(out=dbg_q[:], in_=qarena[:, :]))
            g.wait_ge(qOUT.sem, qOUT.n)

        qLD.n = ld0 + NTT * 16
        assert cPE.n == pe0 + NTT * 25
        assert cDV.n == dv0 + NTT * 127
        assert cAC.n == ac0 + NTT * 10
        assert qLG.n == lgB0 + NTT * 3 * 16

    return nc


_CACHE = {}


def kernel(L0_attn_x, L0_attn_W_qkv, L0_attn_b_qkv, L0_attn_W_o, L0_attn_b_o,
           L0_ffn_W1, L0_ffn_b1, L0_ffn_W2, L0_ffn_b2):
    import ml_dtypes
    x = np.ascontiguousarray(np.asarray(L0_attn_x, dtype=np.float32))
    wqkv_ = np.ascontiguousarray(np.asarray(L0_attn_W_qkv, dtype=np.float32))
    bqkv_ = np.ascontiguousarray(np.asarray(L0_attn_b_qkv, dtype=np.float32))
    wo_ = np.ascontiguousarray(np.asarray(L0_attn_W_o, dtype=np.float32))
    bo_ = np.ascontiguousarray(np.asarray(L0_attn_b_o, dtype=np.float32))
    w1t_ = np.ascontiguousarray(np.asarray(L0_ffn_W1, dtype=np.float32).T)
    b1_ = np.ascontiguousarray(np.asarray(L0_ffn_b1, dtype=np.float32))
    w2t_ = np.ascontiguousarray(np.asarray(L0_ffn_W2, dtype=np.float32).T)
    b2_ = np.ascontiguousarray(np.asarray(L0_ffn_b2, dtype=np.float32))
    ident_ = np.eye(128, dtype=ml_dtypes.bfloat16)

    if "nc" not in _CACHE:
        _CACHE["nc"] = build_nc()
    nc = _CACHE["nc"]

    in_maps = []
    for c in range(8):
        b, half = c // 2, c % 2
        xT_ = np.ascontiguousarray(x[b].T)
        in_maps.append({
            "xT": xT_,
            "xTq": np.ascontiguousarray(xT_[:, half * QO:(half + 1) * QO]),
            "x_own": np.ascontiguousarray(x[b, half * QO:(half + 1) * QO]),
            "wqkv": wqkv_, "bqkv": bqkv_, "wo": wo_, "bo": bo_,
            "w1t": w1t_, "b1": b1_, "w2t": w2t_, "b2": b2_,
            "ident_in": ident_,
        })
    res = run_bass_kernel_spmd(nc, in_maps, core_ids=list(range(8)))
    out = np.empty((B, S, D), dtype=np.float32)
    for c in range(8):
        b, half = c // 2, c % 2
        out[b, half * QO:(half + 1) * QO] = res.results[c]["y"]
    return out
